# version 03
# baseline (speedup 1.0000x reference)
import numpy as np
import ml_dtypes
import concourse.bass as bass
import concourse.mybir as mybir
from concourse.bass_utils import run_bass_kernel_spmd

F32 = mybir.dt.float32
BF16 = mybir.dt.bfloat16
AF = mybir.ActivationFunctionType
ALU = mybir.AluOpType
AX = mybir.AxisListType
BF = ml_dtypes.bfloat16

B, S, D = 2, 8192, 2048
NCORES = 8
TOK = 2048
NTT = TOK // 128
HPC = 4
DH = 128
NKC = D // 128
NQT = S // 512
FSH, FRT = 2816, 1408
DC, DQC = 512, 1024
FT = 5120
EPS = 1e-5
MOE_EXPERTS = 9
MOE_HALVES = 2
ATTN_CC = True
MOE_DBG = False
MOE_DBG2 = False
RGROUPS = [[0, 1, 2, 3], [4, 5, 6, 7]]
PUB0, PUB1 = D, D + DC + DQC


class Sem:
    __slots__ = ("h", "n", "maxw", "name")

    def __init__(self, h, name):
        self.h = h
        self.n = 0
        self.maxw = 0
        self.name = name

    def inc(self, ins, k):
        ins.then_inc(self.h, k)
        self.n += k
        return self.n

    def wait(self, eng, v):
        if v > self.maxw:
            self.maxw = v
        eng.wait_ge(self.h, v)


def _slices(total, step):
    return [(o, min(step, total - o)) for o in range(0, total, step)]


def _host_consts():
    c = {}
    c["ident16"] = np.eye(128, dtype=BF)
    c["ident32"] = np.eye(128, dtype=np.float32)
    P = np.zeros((128, 128), np.float32)
    for i in range(64):
        P[2 * i, 2 * i + 1] = -1.0
        P[2 * i + 1, 2 * i] = 1.0
    c["prope"] = P.T.astype(BF)
    m = np.zeros((128, 4 * 512), np.float32)
    qq = np.arange(512)
    for j in range(4):
        for kk in range(128):
            m[kk, j * 512:(j + 1) * 512] = (128 * j + kk <= qq)
    c["masks"] = m.astype(BF)
    idx = np.arange(64, dtype=np.float64)
    inv = (10000.0 ** (-2.0 * idx / 128.0))
    pos = np.arange(S, dtype=np.float64)
    theta = pos[:, None] * inv[None, :]
    cos = np.cos(theta).astype(np.float32)
    sin = np.sin(theta).astype(np.float32)
    c["cosq"] = np.ascontiguousarray(np.repeat(cos.T, 2, axis=0))
    c["sinq"] = np.ascontiguousarray(np.repeat(sin.T, 2, axis=0))
    c["cosk"] = np.ascontiguousarray(c["cosq"] / np.float32(16.0))
    c["sink"] = np.ascontiguousarray(c["sinq"] / np.float32(16.0))
    return c


def host_prep(ins, layers=(0, 1)):
    cst = _host_consts()
    shared = {}
    if 0 in layers:
        shared.update(
            sw1=ins["L0_ffn_sw1"], sw2=ins["L0_ffn_sw2"], sw3=ins["L0_ffn_sw3"],
            rw1=ins["L0_ffn_rw1"], rw2=ins["L0_ffn_rw2"], rw3=ins["L0_ffn_rw3"],
            gwf=ins["L0_ffn_gate_w"],
            gwT=np.ascontiguousarray(ins["L0_attn_gate_w"].T),
            gbcol=np.ascontiguousarray(ins["L0_attn_gate_b"].reshape(128, 1)),
            woT=np.ascontiguousarray(ins["L0_attn_o_w"].T),
        )
    if 1 in layers:
        shared.update(
            wdkvT=np.ascontiguousarray(ins["L1_attn_W_DKV_w"].T),
            bdkv=np.ascontiguousarray(ins["L1_attn_W_DKV_b"].reshape(DC, 1)),
            wdqT=np.ascontiguousarray(ins["L1_attn_W_DQ_w"].T),
            bdq=np.ascontiguousarray(ins["L1_attn_W_DQ_b"].reshape(DQC, 1)),
            woT1=np.ascontiguousarray(ins["L1_attn_W_O_w"].T),
            borow=np.ascontiguousarray(ins["L1_attn_W_O_b"].reshape(1, D)),
            wfcT=np.ascontiguousarray(ins["L1_ffn_w_fc"].T),
            wpjT=np.ascontiguousarray(ins["L1_ffn_w_proj"].T),
            cosq=cst["cosq"], sinq=cst["sinq"], cosk=cst["cosk"], sink=cst["sink"],
            prope=cst["prope"],
        )
    shared.update(ident16=cst["ident16"], ident32=cst["ident32"], masks=cst["masks"])

    if 0 in layers:
        qT = np.ascontiguousarray(ins["L0_attn_q_w"].T)
        kT = np.ascontiguousarray(ins["L0_attn_k_w"].T)
        vT = np.ascontiguousarray(ins["L0_attn_v_w"].T)
    if 1 in layers:
        ukT = np.ascontiguousarray(ins["L1_attn_W_UK_w"].T)
        uvT = np.ascontiguousarray(ins["L1_attn_W_UV_w"].T)
        uqT = np.ascontiguousarray(ins["L1_attn_W_UQ_w"].T)
        krT = np.ascontiguousarray(ins["L1_attn_W_KR_w"].T)

    xin_key = "L0_attn_x" if 0 in layers else "x1"
    xfull = np.asarray(ins[xin_key])
    if xfull.ndim == 3:
        xfull = xfull.reshape(B, S, D)

    maps = []
    for c in range(NCORES):
        b, p = c // 4, c % 4
        hs = slice(p * 512, (p + 1) * 512)
        m = dict(shared)
        m["xin"] = np.ascontiguousarray(xfull[b, p * TOK:(p + 1) * TOK, :], dtype=np.float32)
        if 0 in layers:
            m["wqT"] = np.ascontiguousarray(qT[:, hs])
            m["wkT"] = np.ascontiguousarray(kT[:, hs])
            m["wvT"] = np.ascontiguousarray(vT[:, hs])
        if 1 in layers:
            m["wukT"] = np.ascontiguousarray(ukT[:, hs])
            m["buk"] = np.ascontiguousarray(ins["L1_attn_W_UK_b"][hs].reshape(512, 1))
            m["wuvT"] = np.ascontiguousarray(uvT[:, hs])
            m["buv"] = np.ascontiguousarray(ins["L1_attn_W_UV_b"][hs].reshape(512, 1))
            m["wuqT"] = np.ascontiguousarray(uqT[:, hs])
            m["buq"] = np.ascontiguousarray(ins["L1_attn_W_UQ_b"][hs].reshape(512, 1))
            m["wkrT"] = np.ascontiguousarray(krT[:, hs])
            m["bkr"] = np.ascontiguousarray(ins["L1_attn_W_KR_b"][hs].reshape(512, 1))
        maps.append(m)
    return maps


def build(layers=(0, 1), taps=(), upto=None):
    import contextlib
    _pidx = [0]

    def _go():
        _pidx[0] += 1
        return upto is None or _pidx[0] <= upto

    nc = bass.Bass("TRN2", target_bir_lowering=False)
    HASL0, HASL1 = (0 in layers), (1 in layers)

    xin = nc.dram_tensor("xin", [TOK, D], F32, kind="ExternalInput").ap()
    out = nc.dram_tensor("out", [TOK, D], F32, kind="ExternalOutput").ap()
    if HASL0 and HASL1:
        r1 = nc.dram_tensor("r1", [TOK, D], F32).ap()
    else:
        r1 = out if HASL0 else None
    l_in = {0: xin, 1: (r1 if HASL0 else xin)}
    l_out = {0: r1, 1: out}

    W = {}

    def din(name, shape, dt=F32):
        W[name] = nc.dram_tensor(name, list(shape), dt, kind="ExternalInput").ap()
        return W[name]

    din("ident16", (128, 128), BF16)
    din("ident32", (128, 128), F32)
    din("masks", (128, 4 * 512), BF16)
    if HASL0:
        din("sw1", (D, FSH)); din("sw2", (FSH, D)); din("sw3", (D, FSH))
        din("rw1", (8, D, FRT)); din("rw2", (8, FRT, D)); din("rw3", (8, D, FRT))
        din("gwf", (D, 8))
        din("gwT", (128, 128)); din("gbcol", (128, 1))
        din("wqT", (D, 512)); din("wkT", (D, 512)); din("wvT", (D, 512))
        din("woT", (D, D))
    if HASL1:
        din("wdkvT", (D, DC)); din("bdkv", (DC, 1))
        din("wdqT", (D, DQC)); din("bdq", (DQC, 1))
        din("wukT", (DC, 512)); din("buk", (512, 1))
        din("wuvT", (DC, 512)); din("buv", (512, 1))
        din("wuqT", (DQC, 512)); din("buq", (512, 1))
        din("wkrT", (D, 512)); din("bkr", (512, 1))
        din("woT1", (D, D)); din("borow", (1, D))
        din("wfcT", (D, FT)); din("wpjT", (FT, D))
        din("prope", (128, 128), BF16)
        din("cosq", (128, S)); din("sinq", (128, S))
        din("cosk", (128, S)); din("sink", (128, S))

    pub, gat, puba, gata = {}, {}, {}, {}
    for lay in layers:
        pr = PUB0 if lay == 0 else PUB1
        pub[lay] = nc.dram_tensor(f"pub{lay}", [pr, TOK], BF16).ap()
        gat[lay] = nc.dram_tensor(f"gat{lay}", [4 * pr, TOK], BF16).ap()
        puba[lay] = nc.dram_tensor(f"puba{lay}", [HPC * DH, S], BF16).ap()
        gata[lay] = nc.dram_tensor(f"gata{lay}", [4 * HPC * DH, S], BF16).ap()
    if HASL1:
        qfcT = nc.dram_tensor("qfcT", [D, FT], BF16).ap()
        qpjT = nc.dram_tensor("qpjT", [FT, D], BF16).ap()

    tap_out = {}
    for t in taps:
        srcs = {"pub0": (pub.get(0), BF16), "pub1": (pub.get(1), BF16),
                "puba0": (puba.get(0), BF16), "puba1": (puba.get(1), BF16),
                "gata0": (gata.get(0), BF16),
                "r1": (r1 if (HASL0 and HASL1) else None, F32)}
        src, dt = srcs[t]
        tap_out[t] = (nc.dram_tensor("tap_" + t, list(src.tensor.shape), dt,
                                     kind="ExternalOutput").ap(), src)

    ctx = contextlib.ExitStack()
    with ctx:
        sems = {}
        for name in ("qs", "qc", "qg", "cc", "pe", "act", "dve", "pool",
                     "wt0", "wt1", "wt2", "wt3"):
            sems[name] = Sem(ctx.enter_context(nc.semaphore(name)), name)
        qs, qc, qg, cc = sems["qs"], sems["qc"], sems["qg"], sems["cc"]
        pe, act, dve, pool = sems["pe"], sems["act"], sems["dve"], sems["pool"]

        ident16 = ctx.enter_context(nc.sbuf_tensor("ident16s", [128, 128], BF16))
        ident32 = ctx.enter_context(nc.sbuf_tensor("ident32s", [128, 128], F32))
        masks = ctx.enter_context(nc.sbuf_tensor("masks_s", [128, 4 * 512], BF16))
        ones_c = ctx.enter_context(nc.sbuf_tensor("ones_c", [128, 1], F32))
        ones_c16 = ctx.enter_context(nc.sbuf_tensor("ones_c16", [128, 1], BF16))
        ones_r = ctx.enter_context(nc.sbuf_tensor("ones_r", [1, 128], F32))
        pcol = ctx.enter_context(nc.sbuf_tensor("pcol", [128, 40], F32))
        prow = ctx.enter_context(nc.sbuf_tensor("prow", [1, 256], F32))
        bo_bc = ctx.enter_context(nc.sbuf_tensor("bo_bc", [128, D], F32)) if HASL1 else None
        prope_s = ctx.enter_context(nc.sbuf_tensor("prope_s", [128, 128], BF16)) if HASL1 else None
        borow_s = ctx.enter_context(nc.sbuf_tensor("borow_s", [1, D], F32)) if HASL1 else None

        with nc.Block() as blk:
            @blk.sync
            def _(sy):
                sy.dma_start(out=ident16[:, :], in_=W["ident16"]).then_inc(qs.h, 16)
                sy.dma_start(out=ident32[:, :], in_=W["ident32"]).then_inc(qs.h, 16)
                sy.dma_start(out=masks[:, :], in_=W["masks"]).then_inc(qs.h, 16)
                qs.n += 48
                if HASL0:
                    sy.dma_start(out=pcol[:, 5:6], in_=W["gbcol"]).then_inc(qs.h, 16)
                    qs.n += 16
                if HASL1:
                    for i in range(4):
                        sy.dma_start(out=pcol[:, 6 + i:7 + i],
                                     in_=W["bdkv"][i * 128:(i + 1) * 128, :]).then_inc(qs.h, 16)
                    for i in range(8):
                        sy.dma_start(out=pcol[:, 10 + i:11 + i],
                                     in_=W["bdq"][i * 128:(i + 1) * 128, :]).then_inc(qs.h, 16)
                    for nm, base in (("buk", 18), ("buv", 22), ("buq", 26), ("bkr", 30)):
                        for i in range(4):
                            sy.dma_start(out=pcol[:, base + i:base + i + 1],
                                         in_=W[nm][i * 128:(i + 1) * 128, :]).then_inc(qs.h, 16)
                    sy.dma_start(out=prope_s[:, :], in_=W["prope"]).then_inc(qs.h, 16)
                    sy.dma_start(out=borow_s[:, :], in_=W["borow"]).then_inc(qs.h, 16)
                    qs.n += 16 * (4 + 8 + 16 + 2)

            @blk.vector
            def _(v):
                v.memset(ones_c[:, :], 1.0)
                v.memset(ones_c16[:, :], 1.0)
                v.memset(pcol[:, 39:40], EPS)
                v.memset(ones_r[:, :], 1.0).then_inc(dve.h, 1)
                dve.n += 1

            @blk.scalar
            def _(sc):
                qs.wait(sc, qs.n)
                dve.wait(sc, dve.n)
                if HASL1:
                    for i in range(4):
                        sc.mul(pcol[:, 18 + i:19 + i], pcol[:, 18 + i:19 + i], 1.0 / 16.0)

        if HASL1:
            with nc.psum_tensor("pbo", [128, 512], F32) as pbo, nc.Block() as blk:
                @blk.tensor
                def _(te):
                    qs.wait(te, qs.n)
                    dve.wait(te, dve.n)
                    for i in range(4):
                        act.wait(te, act.n + i)
                        te.matmul(pbo[:, :], ones_r[:, 0:128], borow_s[0:1, i * 512:(i + 1) * 512],
                                  start=True, stop=True).then_inc(pe.h, 1)
                        pe.n += 1

                @blk.scalar
                def _(sc):
                    for i in range(4):
                        pe.wait(sc, pe.n - 3 + i)
                        sc.activation(bo_bc[:, i * 512:(i + 1) * 512], pbo[:, :],
                                      AF.Copy).then_inc(act.h, 1)
                        act.n += 1

        if HASL0:
            if _go():
                _phase_ln_pub(nc, 0, l_in[0], pub[0], gat[0], W, sems,
                              ident16, ones_c, ones_r, pcol, bo_bc)
            if _go():
                _phase_moe(nc, l_in[0], l_out[0], pub[0], W, sems, pcol)
            if HASL1 and _go():
                _tern_prepass(nc, W, sems, ones_c, ones_r, pcol, prow)
            if _go():
                _phase_attn(nc, 0, gat[0], puba[0], gata[0], W, sems,
                            ident16, ones_c16, ones_r, pcol, prope_s, sems["cc"].n)
            if _go():
                _phase_oproj(nc, gata[0], l_out[0], l_out[0], W["woT"], sems,
                             sems["cc"].n, tag=0)
            if HASL1 and _go():
                _tern_quant(nc, W, sems, pcol, qfcT, qpjT)
        if HASL1:
            if not HASL0:
                if _go():
                    _tern_prepass(nc, W, sems, ones_c, ones_r, pcol, prow)
                if _go():
                    _tern_quant(nc, W, sems, pcol, qfcT, qpjT)
            if _go():
                _phase_ln_pub(nc, 1, l_in[1], pub[1], gat[1], W, sems,
                              ident16, ones_c, ones_r, pcol, bo_bc)
            if _go():
                _phase_ffn1(nc, l_in[1], l_out[1], pub[1], qfcT, qpjT, W, sems,
                            pcol, bo_bc)
            if _go():
                _phase_attn(nc, 1, gat[1], puba[1], gata[1], W, sems,
                            ident16, ones_c16, ones_r, pcol, prope_s, sems["cc"].n)
            if _go():
                _phase_oproj(nc, gata[1], l_out[1], out, W["woT1"], sems,
                             sems["cc"].n, tag=1)

        with nc.Block() as blk:
            @blk.gpsimd
            def _(g):
                qc.wait(g, qc.n)
                qs.wait(g, qs.n)
                for t, (dst, src) in tap_out.items():
                    g.dma_start(out=dst, in_=src).then_inc(qg.h, 16)
                    qg.n += 16
                qg.wait(g, qg.n)

            @blk.sync
            def _(sy):
                qc.wait(sy, qc.n)

        for s in sems.values():
            assert s.maxw <= s.n, f"sem {s.name}: wait {s.maxw} > total {s.n}"
    return nc


def _phase_ln_pub(nc, lay, src, pubT, gatT, W, sems, ident16, ones_c, ones_r,
                  pcol, bo_bc):
    qs, qc, qg, cc = sems["qs"], sems["qc"], sems["qg"], sems["cc"]
    pe, act, dve, pool = sems["pe"], sems["act"], sems["dve"], sems["pool"]
    L1 = (lay == 1)
    APT = 21 if L1 else 9
    PPT = 16 if L1 else 4
    NPUB = 2 if L1 else 1
    qs0, qc0, qg0, pe0, act0, dve0 = qs.n, qc.n, qg.n, pe.n, act.n, dve.n
    w0_, w1_ = sems["wt0"].n, sems["wt1"].n
    D_ = D

    import contextlib
    with contextlib.ExitStack() as st:
        xbuf = st.enter_context(nc.sbuf_tensor(f"A{lay}_xbuf", [128, 2 * D_], F32))
        xsq = st.enter_context(nc.sbuf_tensor(f"A{lay}_xsq", [128, 2 * D_], F32))
        xnt = st.enter_context(nc.sbuf_tensor(f"A{lay}_xnt", [128, 2 * D_], BF16))
        stage = st.enter_context(nc.sbuf_tensor(f"A{lay}_stage", [128, 2 * D_], BF16))
        stf = st.enter_context(nc.sbuf_tensor(f"A{lay}_stf", [128, 114], F32))
        trpb = [st.enter_context(nc.psum_tensor(f"A{lay}_trp{i}", [128, 512], BF16))
                for i in range(2)]
        if L1:
            ckst = st.enter_context(nc.sbuf_tensor("A1_ckst", [128, 2 * 1536], BF16))
            wdkv_sb = st.enter_context(nc.sbuf_tensor("A1_wdkv", [128, 16 * 512], BF16))
            wdq_sb = st.enter_context(nc.sbuf_tensor("A1_wdq", [128, 16 * 1024], BF16))
            pck = [st.enter_context(nc.psum_tensor(f"A1_pck{i}", [128, 128], F32))
                   for i in range(4)]

        with nc.Block() as blk:
            @blk.sync
            def _(sy):
                qc.wait(sy, qc0)
                for t in range(NTT):
                    if t >= 2:
                        dve.wait(sy, dve0 + 5 * (t - 1))
                    wsem = sems["wt" + str(t % 2)]
                    sy.dma_start(out=xbuf[:, (t % 2) * D_:(t % 2 + 1) * D_],
                                 in_=src[t * 128:(t + 1) * 128, :]).then_inc(wsem.h, 16)
                    wsem.n += 16

            @blk.gpsimd
            def _(g):
                qc.wait(g, qc0)
                if L1:
                    for kc in range(16):
                        g.dma_start(out=wdkv_sb[:, kc * 512:(kc + 1) * 512],
                                    in_=W["wdkvT"][kc * 128:(kc + 1) * 128, :]).then_inc(qg.h, 16)
                        qg.n += 16
                    for kc in range(16):
                        g.dma_start(out=wdq_sb[:, kc * 1024:(kc + 1) * 1024],
                                    in_=W["wdqT"][kc * 128:(kc + 1) * 128, :]).then_inc(qg.h, 16)
                        qg.n += 16
                qc.wait(g, qc0 + 16 * NPUB * NTT)
                g.collective_compute(
                    "AllGather", ALU.bypass, replica_groups=RGROUPS,
                    ins=[pubT.opt()], outs=[gatT.opt()]).then_inc(cc.h, 1)
                cc.n += 1

            @blk.vector
            def _(v):
                qc.wait(v, qc0)
                for t in range(NTT):
                    sl = slice((t % 2) * D_, (t % 2 + 1) * D_)
                    wsem = sems["wt" + str(t % 2)]
                    wsem.wait(v, (w0_ if t % 2 == 0 else w1_) + 16 * (t // 2 + 1))
                    v.reduce_sum(stf[:, t:t + 1], xbuf[:, sl], axis=AX.X).then_inc(dve.h, 1)
                    act.wait(v, act0 + APT * t + 1)
                    v.reduce_sum(stf[:, 16 + t:17 + t], xsq[:, sl], axis=AX.X).then_inc(dve.h, 1)
                    act.wait(v, act0 + APT * t + 4)
                    v.tensor_scalar(stf[:, 64 + t:65 + t], stf[:, 112 + t % 2:113 + t % 2],
                                    stf[:, 48 + t:49 + t], None,
                                    ALU.subtract).then_inc(dve.h, 1)
                    act.wait(v, act0 + APT * t + 5)
                    v.reciprocal(stf[:, 96 + t:97 + t], stf[:, 80 + t:81 + t]).then_inc(dve.h, 1)
                    v.tensor_scalar(xnt[:, sl], xbuf[:, sl], stf[:, 32 + t:33 + t],
                                    stf[:, 96 + t:97 + t], ALU.subtract,
                                    ALU.mult).then_inc(dve.h, 1)
                    dve.n += 5

            @blk.tensor
            def _(te):
                qc.wait(te, qc0)
                act.wait(te, act0)
                if L1:
                    qg.wait(te, qg0 + 16 * 32)
                for t in range(NTT):
                    sl16 = slice((t % 2) * D_, (t % 2 + 1) * D_)
                    dve.wait(te, dve0 + 5 * t + 5)
                    for g in range(4):
                        G = 4 * t + g
                        if G >= 2:
                            t2, g2 = (G - 2) // 4, (G - 2) % 4
                            act.wait(te, act0 + APT * t2 + 6 + g2)
                        for j in range(4):
                            ins = te.transpose(trpb[g % 2][:, j * 128:(j + 1) * 128],
                                               xnt[:, (t % 2) * D_ + (4 * g + j) * 128:
                                                   (t % 2) * D_ + (4 * g + j + 1) * 128],
                                               ident16[:, :])
                            if j == 3:
                                ins.then_inc(pe.h, 1)
                                pe.n += 1
                    if L1:
                        act.wait(te, act0 + APT * t + 9)
                        for r, (wsb, wstep, mcs) in enumerate(
                                [(wdkv_sb, 512, range(4)),
                                 (wdq_sb, 1024, range(4)),
                                 (wdq_sb, 1024, range(4, 8))]):
                            for i, mc in enumerate(mcs):
                                if r > 0:
                                    act.wait(te, act0 + APT * t + 9 + 4 * (r - 1) + i + 1)
                                for kc in range(16):
                                    ins = te.matmul(
                                        pck[i][:, :],
                                        wsb[:, kc * wstep + mc * 128:kc * wstep + (mc + 1) * 128],
                                        stage[:, (t % 2) * D_ + kc * 128:(t % 2) * D_ + (kc + 1) * 128],
                                        start=(kc == 0), stop=(kc == 15))
                                ins.then_inc(pe.h, 1)
                                pe.n += 1

            @blk.scalar
            def _(sc):
                qc.wait(sc, qc0)
                for t in range(NTT):
                    sl = slice((t % 2) * D_, (t % 2 + 1) * D_)
                    wsem = sems["wt" + str(t % 2)]
                    wsem.wait(sc, (w0_ if t % 2 == 0 else w1_) + 16 * (t // 2 + 1))
                    sc.activation(xsq[:, sl], xbuf[:, sl], AF.Square).then_inc(act.h, 1)
                    dve.wait(sc, dve0 + 5 * t + 1)
                    sc.activation(stf[:, 32 + t:33 + t], stf[:, t:t + 1], AF.Copy,
                                  scale=1.0 / D_).then_inc(act.h, 1)
                    sc.activation(stf[:, 48 + t:49 + t], stf[:, 32 + t:33 + t],
                                  AF.Square).then_inc(act.h, 1)
                    dve.wait(sc, dve0 + 5 * t + 2)
                    sc.activation(stf[:, 112 + t % 2:113 + t % 2], stf[:, 16 + t:17 + t],
                                  AF.Copy, scale=1.0 / D_).then_inc(act.h, 1)
                    dve.wait(sc, dve0 + 5 * t + 3)
                    sc.activation(stf[:, 80 + t:81 + t], stf[:, 64 + t:65 + t], AF.Sqrt,
                                  bias=pcol[:, 39:40]).then_inc(act.h, 1)
                    act.n += 5
                    if t >= 1:
                        qc.wait(sc, qc0 + 16 * NPUB * t)
                    for g in range(4):
                        pe.wait(sc, pe0 + PPT * t + g + 1)
                        sc.activation(stage[:, (t % 2) * D_ + g * 512:(t % 2) * D_ + (g + 1) * 512],
                                      trpb[g % 2][:, :], AF.Copy).then_inc(act.h, 1)
                        act.n += 1
                    if L1:
                        for mc in range(4):
                            pe.wait(sc, pe0 + PPT * t + 4 + mc + 1)
                            sc.activation(ckst[:, (t % 2) * 1536 + mc * 128:(t % 2) * 1536 + (mc + 1) * 128],
                                          pck[mc][:, :], AF.Identity,
                                          bias=pcol[:, 6 + mc:7 + mc]).then_inc(act.h, 1)
                            act.n += 1
                        for mc in range(8):
                            pe.wait(sc, pe0 + PPT * t + 8 + mc + 1)
                            sc.activation(ckst[:, (t % 2) * 1536 + 512 + mc * 128:
                                              (t % 2) * 1536 + 512 + (mc + 1) * 128],
                                          pck[mc % 4][:, :], AF.Identity,
                                          bias=pcol[:, 10 + mc:11 + mc]).then_inc(act.h, 1)
                            act.n += 1
                    sc.dma_start(
                        out=pubT[0:D_, t * 128:(t + 1) * 128].rearrange("(c p) t -> p c t", p=128),
                        in_=stage[:, sl].rearrange("p (c t) -> p c t", t=128)).then_inc(qc.h, 16)
                    qc.n += 16
                    if L1:
                        sc.dma_start(
                            out=pubT[D_:D_ + 1536, t * 128:(t + 1) * 128].rearrange("(c p) t -> p c t", p=128),
                            in_=ckst[:, (t % 2) * 1536:(t % 2 + 1) * 1536].rearrange(
                                "p (c t) -> p c t", t=128)).then_inc(qc.h, 16)
                        qc.n += 16



def _phase_moe(nc, src, dst, pubT, W, sems, pcol, bo_bc=None):
    import contextlib
    qs, qc, qg = sems["qs"], sems["qc"], sems["qg"]
    pe, act, dve = sems["pe"], sems["act"], sems["dve"]
    base = {k: s.n for k, s in sems.items()}
    marks = {}
    SH = _slices(FSH, 512)
    RT = _slices(FRT, 512)

    with contextlib.ExitStack() as st:
        xnh = st.enter_context(nc.sbuf_tensor("M_xnh", [128, 16 * 1024], BF16))
        oac = st.enter_context(nc.sbuf_tensor("M_oac", [128, 8 * 2048], F32))
        s_sb = st.enter_context(nc.sbuf_tensor("M_s", [128, 8 * 512], BF16))
        htmp = st.enter_context(nc.sbuf_tensor("M_ht", [128, 2 * 512], F32))
        htA = st.enter_context(nc.sbuf_tensor("M_htA", [128, 2 * 512], F32))
        wt = st.enter_context(nc.sbuf_tensor("M_wt", [128, 4 * 512], BF16))
        h_fm = st.enter_context(nc.sbuf_tensor("M_hfm", [128, 22 * 1024], BF16))
        cwsb = st.enter_context(nc.sbuf_tensor("M_cw", [128, 64], F32))
        rb = st.enter_context(nc.sbuf_tensor("M_rb", [128, 4 * 64], F32))
        rst = st.enter_context(nc.sbuf_tensor("M_rst", [128, 6 * 8], F32))
        lg = st.enter_context(nc.sbuf_tensor("M_lg", [128, 64], F32))
        gwf_sb = st.enter_context(nc.sbuf_tensor("M_gwf", [128, 16 * 8], BF16))
        ident32 = st.enter_context(nc.sbuf_tensor("M_id32", [128, 128], F32))
        ps = [st.enter_context(nc.psum_tensor(f"M_ps{i}", [128, 512], F32))
              for i in range(8)]

        def body(R, eng=None):
            C = dict(base)

            def INC(sname, k, ins=None):
                C[sname] += k
                if ins is not None:
                    ins.then_inc(sems[sname].h, k)
                return C[sname]

            def MK(key, sname):
                if R == "sim":
                    marks[key] = C[sname]

            def WT(sname, key_or_val):
                if R == "sim":
                    return
                v = marks[key_or_val] if isinstance(key_or_val, tuple) else key_or_val
                sems[sname].wait(eng, v)

            if R != "sim":
                sems["qc"].wait(eng, base["qc"])
            if R == "gpsimd":
                eng.dma_start(out=gwf_sb[:, :],
                              in_=W["gwf"].rearrange("(c p) e -> p c e", p=128)
                              ).then_inc(qg.h, 16)
            if R == "sync":
                eng.dma_start(out=ident32[:, :], in_=W["ident32"]).then_inc(qs.h, 16)
            INC("qg", 16)
            INC("qs", 16)
            MK(("m_pre", "qg"), "qg")
            MK(("m_pre", "qs"), "qs")

            for hf in range(MOE_HALVES):
                if hf == 1:
                    WT("qc", ("m_st", 0))
                    WT("pe", ("m_peend", 0))
                elif R == "sync":
                    WT("qc", base["qc"])
                for tt in range(8):
                    if R == "sync":
                        eng.dma_start(out=oac[:, tt * 2048:(tt + 1) * 2048],
                                      in_=src[hf * 1024 + tt * 128:hf * 1024 + (tt + 1) * 128, :]
                                      ).then_inc(qs.h, 16)
                    INC("qs", 16)
                for kc in range(16):
                    if R == "sync":
                        eng.dma_start(out=xnh[:, kc * 1024:(kc + 1) * 1024],
                                      in_=pubT[kc * 128:(kc + 1) * 128,
                                               hf * 1024:(hf + 1) * 1024]).then_inc(qs.h, 16)
                    INC("qs", 16)
                MK(("m_ld", hf), "qs")

                if R == "tensor":
                    WT("qs", ("m_ld", hf))
                    WT("qg", ("m_pre", "qg"))
                    if hf == 0:
                        WT("act", base["act"])
                        WT("dve", base["dve"])
                    if hf == 1:
                        WT("dve", ("m_yev", hf - 1, MOE_EXPERTS - 1, 3))
                    for kc in range(16):
                        for tt in range(8):
                            ins = eng.matmul(ps[tt][:, 0:8],
                                             xnh[:, kc * 1024 + tt * 128:kc * 1024 + (tt + 1) * 128],
                                             gwf_sb[:, kc * 8:(kc + 1) * 8],
                                             start=(kc == 0), stop=(kc == 15))
                            if kc == 15:
                                ins.then_inc(pe.h, 1)
                for tt in range(8):
                    INC("pe", 1)
                    MK(("m_rpe", hf, tt), "pe")
                for tt in range(8):
                    sl8 = slice(tt * 8, (tt + 1) * 8)
                    c = lambda j: rst[:, tt * 6 + j:tt * 6 + j + 1]
                    if R == "scalar":
                        WT("pe", ("m_rpe", hf, tt))
                        eng.activation(lg[:, sl8], ps[tt][:, 0:8], AF.Copy).then_inc(act.h, 1)
                    INC("act", 1)
                    MK(("m_rev", hf, tt), "act")
                    if R == "vector":
                        WT("act", ("m_rev", hf, tt))
                        eng.reduce_max(c(0), lg[:, sl8], axis=AX.X).then_inc(dve.h, 1)
                        sems["dve"].wait(eng, C["dve"] + 1)
                        eng.tensor_scalar(c(1), c(0), -1.0, None, ALU.mult).then_inc(dve.h, 1)
                    INC("dve", 2)
                    MK(("m_rneg", hf, tt), "dve")
                    if R == "scalar":
                        WT("dve", ("m_rneg", hf, tt))
                        eng.activation(rb[:, sl8], lg[:, sl8], AF.Exp,
                                       bias=c(1)).then_inc(act.h, 1)
                    INC("act", 1)
                    MK(("m_rexp", hf, tt), "act")
                    if R == "vector":
                        WT("act", ("m_rexp", hf, tt))
                        eng.reduce_sum(c(2), rb[:, sl8], axis=AX.X).then_inc(dve.h, 1)
                        sems["dve"].wait(eng, C["dve"] + 1)
                        eng.reciprocal(c(3), c(2)).then_inc(dve.h, 1)
                        eng.reduce_max(c(4), rb[:, sl8], axis=AX.X).then_inc(dve.h, 1)
                        sems["dve"].wait(eng, C["dve"] + 3)
                        eng.tensor_scalar(rb[:, 64 + tt * 8:64 + (tt + 1) * 8], rb[:, sl8],
                                          c(4), None, ALU.subtract).then_inc(dve.h, 1)
                    INC("dve", 4)
                    MK(("m_rd", hf, tt), "dve")
                    if R == "scalar":
                        WT("dve", ("m_rd", hf, tt))
                        eng.activation(rb[:, 128 + tt * 8:128 + (tt + 1) * 8],
                                       rb[:, 64 + tt * 8:64 + (tt + 1) * 8],
                                       AF.Sign).then_inc(act.h, 1)
                    INC("act", 1)
                    MK(("m_rsg", hf, tt), "act")
                    if R == "vector":
                        WT("act", ("m_rsg", hf, tt))
                        eng.tensor_scalar(rb[:, 64 + tt * 8:64 + (tt + 1) * 8],
                                          rb[:, 128 + tt * 8:128 + (tt + 1) * 8], -1.0,
                                          None, ALU.mult).then_inc(dve.h, 1)
                        sems["dve"].wait(eng, C["dve"] + 1)
                        eng.tensor_mul(rb[:, 192 + tt * 8:192 + (tt + 1) * 8],
                                       rb[:, 64 + tt * 8:64 + (tt + 1) * 8],
                                       rb[:, sl8]).then_inc(dve.h, 1)
                        sems["dve"].wait(eng, C["dve"] + 2)
                        eng.reduce_max(c(5), rb[:, 192 + tt * 8:192 + (tt + 1) * 8],
                                       axis=AX.X).then_inc(dve.h, 1)
                        sems["dve"].wait(eng, C["dve"] + 3)
                        eng.tensor_scalar(rb[:, 64 + tt * 8:64 + (tt + 1) * 8], rb[:, sl8],
                                          c(5), None, ALU.is_ge).then_inc(dve.h, 1)
                        eng.tensor_scalar(rb[:, 192 + tt * 8:192 + (tt + 1) * 8], rb[:, sl8],
                                          c(3), None, ALU.mult).then_inc(dve.h, 1)
                        sems["dve"].wait(eng, C["dve"] + 5)
                        eng.tensor_mul(cwsb[:, sl8],
                                       rb[:, 192 + tt * 8:192 + (tt + 1) * 8],
                                       rb[:, 64 + tt * 8:64 + (tt + 1) * 8]).then_inc(dve.h, 1)
                    INC("dve", 6)
                    MK(("m_rcw", hf, tt), "dve")

                for e in range(MOE_EXPERTS):
                    shared = (e == 0)
                    sls = SH if shared else RT
                    nch = 22 if shared else 11
                    if shared:
                        w1a, w3a, w2a = W["sw1"], W["sw3"], W["sw2"]
                    else:
                        w1a, w3a, w2a = W["rw1"][e - 1], W["rw3"][e - 1], W["rw2"][e - 1]

                    for si, (f0, fw) in enumerate(sls):
                        nsub = fw // 128
                        for ph, wa in ((0, w1a), (1, w3a)):
                            for kc in range(16):
                                wti = C["wti"] = C.get("wti", 0) + 1
                                slot = (wti - 1) % 4
                                sn = f"wt{slot}"
                                if R == "gpsimd":
                                    if (("m_wfree", wti - 4) in marks) if R != "sim" else (wti > 4):
                                        WT("pe", ("m_wfree", wti - 4))
                                    eng.dma_start(
                                        out=wt[:, slot * 512:slot * 512 + fw],
                                        in_=wa[kc * 128:(kc + 1) * 128, f0:f0 + fw]
                                    ).then_inc(sems[sn].h, 16)
                                INC(sn, 16)
                                MK(("m_wld", wti), sn)
                                if R == "tensor":
                                    WT(sn, ("m_wld", wti))
                                    if ph == 0 and kc == 0:
                                        if e == 0 and si == 0:
                                            WT("act", ("m_rev", hf, 7))
                                            WT("dve", ("m_rcw", hf, 7))
                                        elif si == 0:
                                            WT("dve", ("m_yev", hf, e - 1, 3))
                                        else:
                                            WT("act", ("m_tev", hf, e, si - 1, 7))
                                    if ph == 1 and kc == 0:
                                        WT("act", ("m_aev", hf, e, si, 7))
                                    for tt in range(8):
                                        ins = eng.matmul(
                                            ps[tt][:, 0:fw],
                                            xnh[:, kc * 1024 + tt * 128:kc * 1024 + (tt + 1) * 128],
                                            wt[:, slot * 512:slot * 512 + fw],
                                            start=(kc == 0), stop=(kc == 15))
                                        if tt == 7:
                                            ins.then_inc(pe.h, 1)
                                INC("pe", 1)
                                MK(("m_wfree", wti), "pe")
                                if kc == 15:
                                    MK(("m_gstop", hf, e, si, ph), "pe")
                            if ph == 0:
                                for tt in range(8):
                                    if R == "scalar":
                                        WT("pe", ("m_gstop", hf, e, si, 0))
                                        if si > 0 or e > 0:
                                            WT("dve", ("m_hev", hf, e, si - 1, 7) if si > 0
                                               else ("m_hev", hf, e - 1,
                                                     (len(SH) if e == 1 else len(RT)) - 1, 7))
                                        elif hf > 0:
                                            lsi = (len(SH) if MOE_EXPERTS == 1 else len(RT)) - 1
                                            WT("dve", ("m_hev", hf - 1, MOE_EXPERTS - 1, lsi, 7))
                                        eng.activation(s_sb[:, tt * 512:tt * 512 + fw],
                                                       ps[tt][:, 0:fw],
                                                       AF.Silu).then_inc(act.h, 1)
                                    INC("act", 1)
                                    MK(("m_aev", hf, e, si, tt), "act")
                        c0 = f0 // 128
                        for tt in range(8):
                            if R == "vector":
                                WT("pe", ("m_gstop", hf, e, si, 1))
                                if (hf, e, si, tt) != (0, 0, 0, tt):
                                    pass
                                if tt >= 2:
                                    WT("act", ("m_tev", hf, e, si, tt - 2))
                                if e == 0:
                                    eng.tensor_tensor(htmp[:, (tt % 2) * 512:(tt % 2) * 512 + fw],
                                                      ps[tt][:, 0:fw],
                                                      s_sb[:, tt * 512:tt * 512 + fw],
                                                      ALU.mult).then_inc(dve.h, 1)
                                else:
                                    eng.tensor_tensor(htA[:, (tt % 2) * 512:(tt % 2) * 512 + fw],
                                                      ps[tt][:, 0:fw],
                                                      s_sb[:, tt * 512:tt * 512 + fw],
                                                      ALU.mult)
                                    eng.tensor_scalar(
                                        htmp[:, (tt % 2) * 512:(tt % 2) * 512 + fw],
                                        htA[:, (tt % 2) * 512:(tt % 2) * 512 + fw],
                                        cwsb[:, tt * 8 + e - 1:tt * 8 + e], None,
                                        ALU.mult).then_inc(dve.h, 1)
                            INC("dve", 1)
                            MK(("m_hev", hf, e, si, tt), "dve")
                            if R == "tensor":
                                WT("dve", ("m_hev", hf, e, si, tt))
                                for j in range(nsub):
                                    ins = eng.transpose(
                                        ps[tt][:, j * 128:(j + 1) * 128],
                                        htmp[:, (tt % 2) * 512 + j * 128:(tt % 2) * 512 + (j + 1) * 128],
                                        ident32[:, :])
                                    if j == nsub - 1:
                                        ins.then_inc(pe.h, 1)
                            INC("pe", 1)
                            MK(("m_tpe", hf, e, si, tt), "pe")
                            if R == "scalar":
                                WT("pe", ("m_tpe", hf, e, si, tt))
                                dst_ap = bass.AP(h_fm, c0 * 1024 + tt * 128,
                                                 [[22 * 1024, 128], [1024, nsub], [1, 128]])
                                eng.activation(
                                    dst_ap,
                                    ps[tt][:, 0:nsub * 128].rearrange("p (s t) -> p s t", t=128),
                                    AF.Copy).then_inc(act.h, 1)
                            INC("act", 1)
                            MK(("m_tev", hf, e, si, tt), "act")

                    for ds in range(4):
                        for ic in range(nch):
                            wti = C["wti"] = C.get("wti", 0) + 1
                            slot = (wti - 1) % 4
                            sn = f"wt{slot}"
                            if R == "gpsimd":
                                WT("pe", ("m_wfree", wti - 4))
                                eng.dma_start(out=wt[:, slot * 512:(slot + 1) * 512],
                                              in_=w2a[ic * 128:(ic + 1) * 128,
                                                      ds * 512:(ds + 1) * 512]).then_inc(sems[sn].h, 16)
                            INC(sn, 16)
                            MK(("m_wld", wti), sn)
                            if R == "tensor":
                                WT(sn, ("m_wld", wti))
                                if ic == 0:
                                    WT("act", ("m_tev", hf, e, len(sls) - 1, 7))
                                    if ds == 0 and False:
                                        pass
                                    if ds > 0:
                                        WT("dve", ("m_yev", hf, e, ds - 1, 7))
                                    elif True:
                                        pass
                                for tt in range(8):
                                    ins = eng.matmul(
                                        ps[tt][:, 0:512],
                                        h_fm[:, ic * 1024 + tt * 128:ic * 1024 + (tt + 1) * 128],
                                        wt[:, slot * 512:(slot + 1) * 512],
                                        start=(ic == 0), stop=(ic == nch - 1))
                                    if tt == 7:
                                        ins.then_inc(pe.h, 1)
                            INC("pe", 1)
                            MK(("m_wfree", wti), "pe")
                            if ic == nch - 1:
                                MK(("m_ygstop", hf, e, ds), "pe")
                        for tt in range(8):
                            if R == "vector":
                                WT("pe", ("m_ygstop", hf, e, ds))
                                if e == 0 and ds == 0 and tt == 0:
                                    WT("qs", ("m_ld", hf))
                                eng.tensor_tensor(
                                    oac[:, tt * 2048 + ds * 512:tt * 2048 + (ds + 1) * 512],
                                    ps[tt][:, 0:512],
                                    oac[:, tt * 2048 + ds * 512:tt * 2048 + (ds + 1) * 512],
                                    ALU.add).then_inc(dve.h, 1)
                            INC("dve", 1)
                            if tt in (3, 7):
                                MK(("m_yev", hf, e, ds, tt), "dve")
                        MK(("m_yev", hf, e, ds), "dve")
                MK(("m_peend", hf), "pe")

                for tt in range(8):
                    if R == "scalar":
                        WT("dve", ("m_yev", hf, MOE_EXPERTS - 1, 3))
                        WT("qs", ("m_ld", hf))
                        eng.dma_start(out=dst[hf * 1024 + tt * 128:hf * 1024 + (tt + 1) * 128, :],
                                      in_=oac[:, tt * 2048:(tt + 1) * 2048]).then_inc(qc.h, 16)
                    INC("qc", 16)
                if MOE_DBG:
                    if R == "gpsimd":
                        WT("dve", ("m_rcw", hf, 7))
                        eng.dma_start(out=pubT[0:128, 1900 + hf * 64:1900 + (hf + 1) * 64],
                                      in_=cwsb[:, :]).then_inc(qg.h, 16)
                        eng.dma_start(out=pubT[128:256, 1900 + hf * 64:1900 + (hf + 1) * 64],
                                      in_=lg[:, :]).then_inc(qg.h, 16)
                        eng.dma_start(out=pubT[256:384, 1900 + hf * 64:1900 + (hf + 1) * 64],
                                      in_=rb[:, 0:64]).then_inc(qg.h, 16)
                    INC("qg", 32)
                MK(("m_st", hf), "qc")

            if MOE_DBG2:
                if R == "gpsimd":
                    WT("dve", ("m_yev", MOE_HALVES - 1, MOE_EXPERTS - 1, 3))
                    for i in range(11):
                        eng.dma_start(out=pubT[i * 128:(i + 1) * 128, :],
                                      in_=h_fm[:, i * 2048:(i + 1) * 2048]).then_inc(qg.h, 16)
                    for i in range(2):
                        eng.dma_start(out=pubT[(11 + i) * 128:(12 + i) * 128, :],
                                      in_=s_sb[:, i * 2048:(i + 1) * 2048]).then_inc(qg.h, 16)
                    for i in range(2):
                        eng.dma_start(out=pubT[(13 + i) * 128:(14 + i) * 128, :],
                                      in_=xnh[:, i * 2048:(i + 1) * 2048]).then_inc(qg.h, 16)
                INC("qg", 16 * 15)
            if R == "sim":
                for k in sems:
                    marks[("m_end", k)] = C[k]

        body("sim")
        with nc.Block() as blk:
            @blk.sync
            def _(sy):
                body("sync", sy)

            @blk.gpsimd
            def _(g):
                body("gpsimd", g)

            @blk.tensor
            def _(te):
                body("tensor", te)

            @blk.scalar
            def _(sc):
                body("scalar", sc)

            @blk.vector
            def _(v):
                body("vector", v)

        for k in sems:
            sems[k].n = marks[("m_end", k)]


def _tern_prepass(nc, W, sems, ones_c, ones_r, pcol, prow):
    import contextlib
    qc = sems["qc"]
    pe, act, dve = sems["pe"], sems["act"], sems["dve"]

    pieces = []
    for kc in range(16):
        for half in range(2):
            pieces.append(("wfcT", kc * 128, half * 2560, 2560))
    for kc in range(40):
        pieces.append(("wpjT", kc * 128, 0, 2048))
    NP_ = len(pieces)
    NEL = float(D * FT)

    with contextlib.ExitStack() as st:
        tb = st.enter_context(nc.sbuf_tensor("T_tb", [128, 2 * 2560], F32))
        stf = st.enter_context(nc.sbuf_tensor("T_stf", [128, 80], F32))
        p1 = st.enter_context(nc.psum_tensor("T_p1", [1, 1], F32))
        pb = st.enter_context(nc.psum_tensor("T_pb", [128, 1], F32))
        qc0, dve0, act0, pe0 = qc.n, dve.n, act.n, pe.n
        tw0, tw1 = sems["wt0"].n, sems["wt1"].n

        with nc.Block() as blk:
            @blk.scalar
            def _(sc):
                qc.wait(sc, qc0)
                for i, (nm, r0, c0, cw) in enumerate(pieces):
                    if i >= 2:
                        dve.wait(sc, dve0 + i - 1)
                    ws = sems["wt" + str(i % 2)]
                    sc.dma_start(out=tb[:, (i % 2) * 2560:(i % 2) * 2560 + cw],
                                 in_=W[nm][r0:r0 + 128, c0:c0 + cw]).then_inc(ws.h, 16)
                    ws.n += 16
                pe.wait(sc, pe0 + 1)
                sc.activation(prow[0:1, 130:131], p1[:, :], AF.Copy,
                              scale=1.0 / NEL).then_inc(act.h, 1)
                dve.wait(sc, dve0 + NP_ + 3)
                sc.mul(prow[0:1, 132:133], prow[0:1, 131:132], 0.7).then_inc(act.h, 1)
                pe.wait(sc, pe0 + 2)
                sc.activation(prow[0:1, 140:141], p1[:, :], AF.Copy,
                              scale=1.0 / NEL).then_inc(act.h, 1)
                dve.wait(sc, dve0 + NP_ + 4)
                sc.mul(prow[0:1, 142:143], prow[0:1, 141:142], 0.7).then_inc(act.h, 1)
                sc.activation(prow[0:1, 143:144], prow[0:1, 131:132],
                              AF.Square).then_inc(act.h, 1)
                act.n += 5
                for j, (col, dst0) in enumerate([(132, 0), (142, 2), (144, 4)]):
                    pe.wait(sc, pe0 + 3 + j)
                    sc.activation(pcol[:, dst0:dst0 + 1], pb[:, :], AF.Copy).then_inc(act.h, 1)
                    act.n += 1
                    if dst0 < 4:
                        sc.activation(pcol[:, dst0 + 1:dst0 + 2], pb[:, :], AF.Copy,
                                      scale=-1.0).then_inc(act.h, 1)
                        act.n += 1

            @blk.vector
            def _(v):
                qc.wait(v, qc0)
                for i in range(NP_):
                    cw = pieces[i][3]
                    ws = sems["wt" + str(i % 2)]
                    ws.wait(v, (tw0 if i % 2 == 0 else tw1) + 16 * (i // 2 + 1))
                    v.tensor_reduce(stf[:, i:i + 1],
                                    tb[:, (i % 2) * 2560:(i % 2) * 2560 + cw],
                                    AX.X, ALU.add,
                                    apply_absolute_value=True).then_inc(dve.h, 1)
                    dve.n += 1
                v.tensor_reduce(stf[:, 72:73], stf[:, 0:32], AX.X, ALU.add).then_inc(dve.h, 1)
                v.tensor_reduce(stf[:, 73:74], stf[:, 32:72], AX.X, ALU.add).then_inc(dve.h, 1)
                dve.n += 2
                act.wait(v, act0 + 1)
                v.tensor_scalar(prow[0:1, 131:132], prow[0:1, 130:131], 1e-5, None,
                                ALU.max).then_inc(dve.h, 1)
                act.wait(v, act0 + 3)
                v.tensor_scalar(prow[0:1, 141:142], prow[0:1, 140:141], 1e-5, None,
                                ALU.max).then_inc(dve.h, 1)
                act.wait(v, act0 + 5)
                v.tensor_mul(prow[0:1, 144:145], prow[0:1, 143:144],
                             prow[0:1, 141:142]).then_inc(dve.h, 1)
                dve.n += 3

            @blk.tensor
            def _(te):
                act.wait(te, act0)
                dve.wait(te, dve0)
                dve.wait(te, dve0 + NP_ + 2)
                te.matmul(p1[:, :], ones_c[:, :], stf[:, 72:73], start=True,
                          stop=True).then_inc(pe.h, 1)
                act.wait(te, act0 + 1)
                te.matmul(p1[:, :], ones_c[:, :], stf[:, 73:74], start=True,
                          stop=True).then_inc(pe.h, 1)
                pe.n += 2
                waits = [(act0 + 2, dve0 + NP_ + 3), (act0 + 5, dve0 + NP_ + 4),
                         (act0 + 5, dve0 + NP_ + 5)]
                for j, (col, (aw, dw)) in enumerate(zip((132, 142, 144), waits)):
                    act.wait(te, aw)
                    dve.wait(te, dw)
                    if j > 0:
                        act.wait(te, act0 + 5 + 2 * j)
                    te.matmul(pb[:, :], ones_r[:, :], prow[0:1, col:col + 1],
                              start=True, stop=True).then_inc(pe.h, 1)
                    pe.n += 1


def _tern_quant(nc, W, sems, pcol, qfcT, qpjT):
    import contextlib
    qc = sems["qc"]
    pool, dve = sems["pool"], sems["dve"]

    pieces = []
    for kc in range(16):
        for half in range(2):
            pieces.append((W["wfcT"], qfcT, kc * 128, half * 2560, 2560, 0))
    for kc in range(40):
        pieces.append((W["wpjT"], qpjT, kc * 128, 0, 2048, 2))
    NP_ = len(pieces)

    with contextlib.ExitStack() as st:
        tb = st.enter_context(nc.sbuf_tensor("Q_tb", [128, 2 * 2560], F32))
        gl = st.enter_context(nc.sbuf_tensor("Q_gl", [128, 4 * 2560], BF16))
        qt_ = st.enter_context(nc.sbuf_tensor("Q_qt", [128, 2 * 2560], BF16))
        qc0, pool0, dve0 = qc.n, pool.n, dve.n
        qw2, qw3 = sems["wt2"].n, sems["wt3"].n

        with nc.Block() as blk:
            @blk.scalar
            def _(sc):
                qc.wait(sc, qc0)
                for i, (srcT, dstT, r0, c0, cw, tc) in enumerate(pieces):
                    if i >= 2:
                        pool.wait(sc, pool0 + (i - 1))
                    ws = sems["wt" + str(2 + i % 2)]
                    sc.dma_start(out=tb[:, (i % 2) * 2560:(i % 2) * 2560 + cw],
                                 in_=srcT[r0:r0 + 128, c0:c0 + cw]).then_inc(ws.h, 16)
                    ws.n += 16
                    dve.wait(sc, dve0 + (i + 1))
                    sc.dma_start(out=dstT[r0:r0 + 128, c0:c0 + cw],
                                 in_=qt_[:, (i % 2) * 2560:(i % 2) * 2560 + cw]
                                 ).then_inc(qc.h, 16)
                    qc.n += 16

            @blk.gpsimd
            def _(g):
                qc.wait(g, qc0)
                for i, (srcT, dstT, r0, c0, cw, tc) in enumerate(pieces):
                    ws = sems["wt" + str(2 + i % 2)]
                    ws.wait(g, (qw2 if i % 2 == 0 else qw3) + 16 * (i // 2 + 1))
                    if i >= 2:
                        dve.wait(g, dve0 + (i - 1))
                    sl = slice((i % 2) * 2560, (i % 2) * 2560 + cw)
                    g.tensor_scalar(gl[:, (i % 2) * 2560:(i % 2) * 2560 + cw],
                                    tb[:, sl], pcol[:, tc:tc + 1], None, ALU.is_gt)
                    g.tensor_scalar(gl[:, 5120 + (i % 2) * 2560:5120 + (i % 2) * 2560 + cw],
                                    tb[:, sl], pcol[:, tc + 1:tc + 2], None,
                                    ALU.is_lt).then_inc(pool.h, 1)
                    pool.n += 1

            @blk.vector
            def _(v):
                for i, (srcT, dstT, r0, c0, cw, tc) in enumerate(pieces):
                    pool.wait(v, pool0 + (i + 1))
                    if i >= 1:
                        qc.wait(v, qc0 + 16 * i)
                    v.tensor_sub(qt_[:, (i % 2) * 2560:(i % 2) * 2560 + cw],
                                 gl[:, (i % 2) * 2560:(i % 2) * 2560 + cw],
                                 gl[:, 5120 + (i % 2) * 2560:5120 + (i % 2) * 2560 + cw]
                                 ).then_inc(dve.h, 1)
                    dve.n += 1


def _phase_attn(nc, lay, gatT, pubaT, gataT, W, sems, ident16, ones_c16,
                ones_r, pcol, prope_s, cc_idx):
    import contextlib
    qs, qc, qg, cc = sems["qs"], sems["qc"], sems["qg"], sems["cc"]
    pe, act, dve = sems["pe"], sems["act"], sems["dve"]
    base = {k: s.n for k, s in sems.items()}
    marks = {}
    L1 = (lay == 1)
    PR = PUB1 if L1 else PUB0
    S0 = 1.0 / np.sqrt(128.0)

    with contextlib.ExitStack() as st:
        wsl = st.enter_context(nc.sbuf_tensor(f"P{lay}_wsl", [128, 48 * 128], BF16))
        gwt = st.enter_context(nc.sbuf_tensor(f"P{lay}_gwt", [128, 128], BF16))
        xnb = st.enter_context(nc.sbuf_tensor(f"P{lay}_xnb", [128, 16 * 512], BF16))
        kh_a = st.enter_context(nc.sbuf_tensor(f"P{lay}_kha", [128, S], BF16))
        qh_a = st.enter_context(nc.sbuf_tensor(f"P{lay}_qha", [128, S], BF16))
        vT_h = st.enter_context(nc.sbuf_tensor(f"P{lay}_vth", [128, S], BF16))
        vtmp = st.enter_context(nc.sbuf_tensor(f"P{lay}_vtmp", [128, 512], BF16))
        probsT = st.enter_context(nc.sbuf_tensor(f"P{lay}_pT", [128, 2 * 512], BF16))
        ptmp = st.enter_context(nc.sbuf_tensor(f"P{lay}_ptmp", [128, 512], BF16))
        rs_sb = st.enter_context(nc.sbuf_tensor(f"P{lay}_rs", [1, 512], F32))
        rb_sb = st.enter_context(nc.sbuf_tensor(f"P{lay}_rb", [128, 512], F32))
        fac = st.enter_context(nc.sbuf_tensor(f"P{lay}_fac", [128, 512], F32))
        av_sb = st.enter_context(nc.sbuf_tensor(f"P{lay}_av", [128, 2 * 512], BF16))
        if L1:
            kh_b = st.enter_context(nc.sbuf_tensor("P1_khb", [128, S], BF16))
            qh_b = st.enter_context(nc.sbuf_tensor("P1_qhb", [128, S], BF16))
            ckvb = st.enter_context(nc.sbuf_tensor("P1_ckvb", [128, 4 * 512], BF16))
            cqb = st.enter_context(nc.sbuf_tensor("P1_cqb", [128, 8 * 512], BF16))
            tabb = st.enter_context(nc.sbuf_tensor("P1_tab", [128, 4 * 512], F32))
            kraw = st.enter_context(nc.sbuf_tensor("P1_kraw", [128, 512], BF16))
            tmp1 = st.enter_context(nc.sbuf_tensor("P1_t1", [128, 512], F32))
            tmp2 = st.enter_context(nc.sbuf_tensor("P1_t2", [128, 512], F32))
        else:
            gate_sb = st.enter_context(nc.sbuf_tensor("P0_gate", [128, 512], F32))
        P = [st.enter_context(nc.psum_tensor(f"P{lay}_P{i}", [128, 512], F32))
             for i in range(5)]
        PT = st.enter_context(nc.psum_tensor(f"P{lay}_PT", [128, 512], BF16))
        PS = st.enter_context(nc.psum_tensor(f"P{lay}_PS", [1, 512], F32))
        MB = st.enter_context(nc.sbuf_tensor(f"P{lay}_mb", [128, 4 * 512], BF16))

        if L1:
            WTAB = [("wukT", 4, 0, "ckvb"), ("wuvT", 4, 4, "ckvb"),
                    ("wuqT", 8, 8, "cqb"), ("wkrT", 16, 16, "xnb")]
        else:
            WTAB = [("wqT", 16, 0, "xnb"), ("wkT", 16, 16, "xnb"),
                    ("wvT", 16, 32, "xnb")]

        def body(R, eng=None):
            C = dict(base)

            def INC(sname, k, ins=None):
                C[sname] += k
                if ins is not None:
                    ins.then_inc(sems[sname].h, k)
                return C[sname]

            def MK(key, sname):
                if R == "sim":
                    marks[key] = C[sname]

            def WT(sname, key_or_val):
                if R == "sim":
                    return
                v = marks[key_or_val] if isinstance(key_or_val, tuple) else key_or_val
                sems[sname].wait(eng, v)

            if R != "sim":
                sems["qc"].wait(eng, base["qc"])
            if R == "sync":
                eng.dma_start(out=MB[:, :], in_=W["masks"]).then_inc(qs.h, 16)
            INC("qs", 16)
            MK(("p_mb",), "qs")

            for h in range(HPC):
                if h > 0:
                    WT("pe", ("p_hd_pe", h - 1))
                for mi, (nm, nkc, tb0, _r) in enumerate(WTAB):
                    for kc in range(nkc):
                        if R == "gpsimd":
                            eng.dma_start(
                                out=wsl[:, (tb0 + kc) * 128:(tb0 + kc + 1) * 128],
                                in_=W[nm][kc * 128:(kc + 1) * 128, h * 128:(h + 1) * 128]
                            ).then_inc(qg.h, 16)
                        INC("qg", 16)
                if lay == 0 and h == 0:
                    if R == "gpsimd":
                        eng.dma_start(out=gwt[:, :], in_=W["gwT"]).then_inc(qg.h, 16)
                    INC("qg", 16)
                MK(("p_wsl", h), "qg")

                for ts in range(16):
                    r, lc = ts // 4, (ts % 4) * 512
                    if R == "sync":
                        if h == 0 and ts == 0:
                            WT("cc", cc_idx)
                        if not (h == 0 and ts == 0):
                            WT("pe", ("p_pj", h, ts - 1) if ts > 0 else ("p_pj", h - 1, 15))
                            if L1:
                                WT("dve", ("p_rpq", h, ts - 1) if ts > 0 else ("p_rpq", h - 1, 15))
                        for kc in range(16):
                            eng.dma_start(out=xnb[:, kc * 512:(kc + 1) * 512],
                                          in_=gatT[r * PR + kc * 128:r * PR + (kc + 1) * 128,
                                                   lc:lc + 512]).then_inc(qs.h, 16)
                        if L1:
                            for i in range(4):
                                eng.dma_start(out=ckvb[:, i * 512:(i + 1) * 512],
                                              in_=gatT[r * PR + D + i * 128:r * PR + D + (i + 1) * 128,
                                                       lc:lc + 512]).then_inc(qs.h, 16)
                            for i in range(8):
                                eng.dma_start(out=cqb[:, i * 512:(i + 1) * 512],
                                              in_=gatT[r * PR + D + DC + i * 128:
                                                       r * PR + D + DC + (i + 1) * 128,
                                                       lc:lc + 512]).then_inc(qs.h, 16)
                            for ti, tn in enumerate(("cosq", "sinq", "cosk", "sink")):
                                eng.dma_start(out=tabb[:, ti * 512:(ti + 1) * 512],
                                              in_=W[tn][:, ts * 512:(ts + 1) * 512]
                                              ).then_inc(qs.h, 16)
                    INC("qs", 16 * (16 + (16 if L1 else 0)))
                    MK(("p_ld", h, ts), "qs")

                    if R == "tensor":
                        WT("qs", ("p_ld", h, ts))
                        if h == 0 and ts == 0:
                            WT("qg", ("p_wsl", 0))
                            WT("act", base["act"])
                            WT("dve", base["dve"])
                        elif ts == 0:
                            WT("qg", ("p_wsl", h))
                            WT("act", ("p_hd_act", h - 1))
                            WT("dve", ("p_hd_dve", h - 1))
                        if not (h == 0 and ts == 0):
                            WT("act", ("p_pjev", h, ts - 1) if ts > 0 else ("p_pjev", h - 1, 15))
                        if L1:
                            plan = [(0, 0, 4, "ckvb"), (2, 4, 4, "ckvb"),
                                    (3, 8, 8, "cqb"), (1, 16, 16, "xnb")]
                        else:
                            plan = [(0, 0, 16, "xnb"), (1, 16, 16, "xnb"), (2, 32, 16, "xnb")]
                        rhs_map = {"xnb": xnb, "ckvb": (ckvb if L1 else None),
                                   "cqb": (cqb if L1 else None)}
                        for pi, tb0, nkc, rn in plan:
                            rbuf = rhs_map[rn]
                            for kc in range(nkc):
                                ins = eng.matmul(P[pi][:, :],
                                                 wsl[:, (tb0 + kc) * 128:(tb0 + kc + 1) * 128],
                                                 rbuf[:, kc * 512:(kc + 1) * 512],
                                                 start=(kc == 0), stop=(kc == nkc - 1))
                            ins.then_inc(pe.h, 1)
                        INC("pe", len(plan))
                    else:
                        INC("pe", 4 if L1 else 3)
                    MK(("p_pj", h, ts), "pe")

                    if R == "scalar":
                        WT("pe", ("p_pj", h, ts))
                        if L1 and not (h == 0 and ts == 0):
                            WT("dve", ("p_rpk", h, ts - 1) if ts > 0 else ("p_rpk", h - 1, 15))
                        if L1:
                            eng.activation(kh_a[:, ts * 512:(ts + 1) * 512], P[0][:, :],
                                           AF.Identity, bias=pcol[:, 18 + h:19 + h],
                                           scale=1.0 / 16.0).then_inc(act.h, 1)
                            eng.activation(vtmp[:, :], P[2][:, :], AF.Identity,
                                           bias=pcol[:, 22 + h:23 + h]).then_inc(act.h, 1)
                            eng.activation(qh_a[:, ts * 512:(ts + 1) * 512], P[3][:, :],
                                           AF.Identity, bias=pcol[:, 26 + h:27 + h]
                                           ).then_inc(act.h, 1)
                            eng.activation(kraw[:, :], P[1][:, :], AF.Identity,
                                           bias=pcol[:, 30 + h:31 + h]).then_inc(act.h, 1)
                        else:
                            eng.activation(qh_a[:, ts * 512:(ts + 1) * 512], P[0][:, :],
                                           AF.Copy).then_inc(act.h, 1)
                            eng.activation(kh_a[:, ts * 512:(ts + 1) * 512], P[1][:, :],
                                           AF.Copy, scale=S0).then_inc(act.h, 1)
                            eng.activation(vtmp[:, :], P[2][:, :], AF.Copy).then_inc(act.h, 1)
                    INC("act", 4 if L1 else 3)
                    MK(("p_pjev", h, ts), "act")

                    if R == "tensor":
                        WT("act", ("p_pjev", h, ts))
                        if not (h == 0 and ts == 0):
                            WT("act", ("p_vtev", h, ts - 1) if ts > 0 else ("p_vtev", h - 1, 15))
                        for j in range(4):
                            ins = eng.transpose(PT[:, j * 128:(j + 1) * 128],
                                                vtmp[:, j * 128:(j + 1) * 128],
                                                ident16[:, :])
                        ins.then_inc(pe.h, 1)
                    INC("pe", 1)
                    MK(("p_vt", h, ts), "pe")
                    if R == "scalar":
                        WT("pe", ("p_vt", h, ts))
                        eng.activation(vT_h[:, ts * 512:(ts + 1) * 512], PT[:, :],
                                       AF.Copy).then_inc(act.h, 1)
                    INC("act", 1)
                    MK(("p_vtev", h, ts), "act")

                    if L1:
                        if R == "tensor":
                            WT("act", ("p_pjev", h, ts))
                            if not (h == 0 and ts == 0):
                                WT("dve", ("p_rpq", h, ts - 1) if ts > 0 else ("p_rpq", h - 1, 15))
                            eng.matmul(P[4][:, :], prope_s[:, :], kraw[:, :],
                                       start=True, stop=True).then_inc(pe.h, 1)
                        INC("pe", 1)
                        MK(("p_rotk", h, ts), "pe")
                        if R == "vector":
                            WT("act", ("p_pjev", h, ts))
                            WT("qs", ("p_ld", h, ts))
                            eng.tensor_mul(tmp1[:, :], kraw[:, :], tabb[:, 2 * 512:3 * 512])
                            WT("pe", ("p_rotk", h, ts))
                            eng.tensor_mul(tmp2[:, :], P[4][:, :], tabb[:, 3 * 512:4 * 512])
                            eng.tensor_add(kh_b[:, ts * 512:(ts + 1) * 512], tmp1[:, :],
                                           tmp2[:, :]).then_inc(dve.h, 1)
                        INC("dve", 1)
                        MK(("p_rpk", h, ts), "dve")
                        if R == "tensor":
                            WT("dve", ("p_rpk", h, ts))
                            eng.matmul(P[4][:, :], prope_s[:, :],
                                       qh_a[:, ts * 512:(ts + 1) * 512],
                                       start=True, stop=True).then_inc(pe.h, 1)
                        INC("pe", 1)
                        MK(("p_rotq", h, ts), "pe")
                        if R == "vector":
                            eng.tensor_mul(tmp1[:, :], qh_a[:, ts * 512:(ts + 1) * 512],
                                           tabb[:, 0:512])
                            WT("pe", ("p_rotq", h, ts))
                            eng.tensor_mul(tmp2[:, :], P[4][:, :], tabb[:, 512:1024])
                            eng.tensor_add(qh_b[:, ts * 512:(ts + 1) * 512], tmp1[:, :],
                                           tmp2[:, :]).then_inc(dve.h, 1)
                        INC("dve", 1)
                        MK(("p_rpq", h, ts), "dve")

                kidx = 0
                for qt in range(16):
                    nkt = 4 * (qt + 1)
                    for kt in range(nkt):
                        ki = kidx + kt
                        diag = kt >= 4 * qt
                        j = kt - 4 * qt
                        if R == "tensor":
                            if ki >= 2:
                                WT("act", ("a_ex", h, ki - 2))
                            if h == 0 and qt == 0 and kt == 0:
                                WT("act", ("p_vtev", h, 15))
                                if L1:
                                    WT("dve", ("p_rpq", h, 15))
                            if h > 0 and qt == 0 and kt == 0:
                                WT("act", ("p_vtev", h, 15))
                                if L1:
                                    WT("dve", ("p_rpq", h, 15))
                            if L1:
                                eng.matmul(P[ki % 2][:, :],
                                           kh_a[:, kt * 128:(kt + 1) * 128],
                                           qh_a[:, qt * 512:(qt + 1) * 512],
                                           start=True, stop=False)
                                eng.matmul(P[ki % 2][:, :],
                                           kh_b[:, kt * 128:(kt + 1) * 128],
                                           qh_b[:, qt * 512:(qt + 1) * 512],
                                           start=False, stop=True).then_inc(pe.h, 1)
                            else:
                                eng.matmul(P[ki % 2][:, :],
                                           kh_a[:, kt * 128:(kt + 1) * 128],
                                           qh_a[:, qt * 512:(qt + 1) * 512],
                                           start=True, stop=True).then_inc(pe.h, 1)
                        INC("pe", 1)
                        MK(("a_sc", h, ki), "pe")
                        if R == "scalar":
                            WT("pe", ("a_sc", h, ki))
                            if ki >= 2:
                                WT("pe", ("a_pv", h, ki - 2))
                            if diag:
                                prev = marks.get(("a_dprev", h, ki))
                                if prev is not None:
                                    WT("dve", prev)
                                eng.activation(ptmp[:, :], P[ki % 2][:, :],
                                               AF.Exp).then_inc(act.h, 1)
                            else:
                                eng.activation(probsT[:, (ki % 2) * 512:(ki % 2 + 1) * 512],
                                               P[ki % 2][:, :], AF.Exp).then_inc(act.h, 1)
                        INC("act", 1)
                        MK(("a_ex", h, ki), "act")
                        if diag:
                            if R == "vector":
                                WT("act", ("a_ex", h, ki))
                                eng.tensor_mul(probsT[:, (ki % 2) * 512:(ki % 2 + 1) * 512],
                                               ptmp[:, :],
                                               MB[:, j * 512:(j + 1) * 512]).then_inc(dve.h, 1)
                            INC("dve", 1)
                            MK(("a_mk", h, ki), "dve")
                            if R == "sim":
                                marks[("a_dprev", h, ki + 1 if kt < nkt - 1 else ki + 4 * qt + 5)] = C["dve"]
                        if R == "tensor":
                            if diag:
                                WT("dve", ("a_mk", h, ki))
                            else:
                                WT("act", ("a_ex", h, ki))
                            if kt == 0:
                                if (h, qt) != (0, 0):
                                    WT("dve", ("a_avm", h, qt - 1) if qt > 0 else ("a_avm", h - 1, 15))
                                    WT("dve", ("a_rc", h, qt - 1) if qt > 0 else ("a_rc", h - 1, 15))
                            eng.matmul(P[2][:, :], vT_h[:, kt * 128:(kt + 1) * 128],
                                       probsT[:, (ki % 2) * 512:(ki % 2 + 1) * 512],
                                       start=(kt == 0), stop=(kt == nkt - 1))
                            ins = eng.matmul(PS[:, :], ones_c16[:, :],
                                             probsT[:, (ki % 2) * 512:(ki % 2 + 1) * 512],
                                             start=(kt == 0), stop=(kt == nkt - 1))
                            ins.then_inc(pe.h, 1)
                        INC("pe", 1)
                        MK(("a_pv", h, ki), "pe")
                    kidx += nkt
                    if R == "vector":
                        WT("pe", ("a_pv", h, kidx - 1))
                        eng.reciprocal(rs_sb[:, :], PS[:, :]).then_inc(dve.h, 1)
                    INC("dve", 1)
                    MK(("a_rc", h, qt), "dve")
                    if R == "tensor":
                        WT("dve", ("a_rc", h, qt))
                        if (h, qt) != (0, 0):
                            WT("act", ("a_rbev", h, qt - 1) if qt > 0 else ("a_rbev", h - 1, 15))
                        eng.matmul(P[3][:, :], ones_r[:, :], rs_sb[:, :],
                                   start=True, stop=True).then_inc(pe.h, 1)
                        if lay == 0:
                            if (h, qt) != (0, 0):
                                WT("act", ("a_ggev", h, qt - 1) if qt > 0 else ("a_ggev", h - 1, 15))
                            eng.matmul(P[4][:, :], gwt[:, :],
                                       qh_a[:, qt * 512:(qt + 1) * 512],
                                       start=True, stop=True).then_inc(pe.h, 1)
                    INC("pe", 2 if lay == 0 else 1)
                    MK(("a_rb", h, qt), "pe")
                    if R == "scalar":
                        WT("pe", ("a_rb", h, qt))
                        eng.activation(rb_sb[:, :], P[3][:, :], AF.Copy).then_inc(act.h, 1)
                    INC("act", 1)
                    MK(("a_rbev", h, qt), "act")
                    if lay == 0:
                        if R == "scalar":
                            eng.activation(gate_sb[:, :], P[4][:, :], AF.Sigmoid,
                                           bias=pcol[:, 5:6]).then_inc(act.h, 1)
                        INC("act", 1)
                        MK(("a_ggev", h, qt), "act")
                    if R == "vector":
                        WT("act", ("a_rbev", h, qt))
                        if lay == 0:
                            WT("act", ("a_ggev", h, qt))
                            eng.tensor_mul(fac[:, :], gate_sb[:, :], rb_sb[:, :])
                            srcf = fac
                        else:
                            srcf = rb_sb
                        if not (h == 0 and qt < 2):
                            WT("qc", ("a_st", h, qt - 2) if qt >= 2 else ("a_st", h - 1, 14 + qt))
                        eng.tensor_mul(av_sb[:, (qt % 2) * 512:(qt % 2 + 1) * 512],
                                       P[2][:, :], srcf[:, :]).then_inc(dve.h, 1)
                    INC("dve", 1)
                    MK(("a_avm", h, qt), "dve")
                    if R == "scalar":
                        WT("dve", ("a_avm", h, qt))
                        eng.dma_start(out=pubaT[h * 128:(h + 1) * 128,
                                                qt * 512:(qt + 1) * 512],
                                      in_=av_sb[:, (qt % 2) * 512:(qt % 2 + 1) * 512]
                                      ).then_inc(qc.h, 16)
                    INC("qc", 16)
                    MK(("a_st", h, qt), "qc")
                MK(("p_hd_pe", h), "pe")
                MK(("p_hd_act", h), "act")
                MK(("p_hd_dve", h), "dve")

            if R == "gpsimd":
                WT("qc", ("a_st", HPC - 1, 15))
                if ATTN_CC:
                    eng.collective_compute(
                        "AllGather", ALU.bypass, replica_groups=RGROUPS,
                        ins=[pubaT.opt()], outs=[gataT.opt()]).then_inc(cc.h, 1)
                else:
                    eng.engine_nop().then_inc(cc.h, 1)
            INC("cc", 1)
            if R == "sim":
                for k in sems:
                    marks[("p_end", k)] = C[k]

        body("sim")
        with nc.Block() as blk:
            @blk.sync
            def _(sy):
                body("sync", sy)

            @blk.gpsimd
            def _(g):
                body("gpsimd", g)

            @blk.tensor
            def _(te):
                body("tensor", te)

            @blk.scalar
            def _(sc):
                body("scalar", sc)

            @blk.vector
            def _(v):
                body("vector", v)

        for k in sems:
            sems[k].n = marks[("p_end", k)]


def _phase_oproj(nc, gataT, srcT, dstT, woTap, sems, cc_idx, tag=0):
    import contextlib
    qs, qc, qg = sems["qs"], sems["qc"], sems["qg"]
    pe, act, dve = sems["pe"], sems["act"], sems["dve"]
    base = {k: s.n for k, s in sems.items()}
    marks = {}

    with contextlib.ExitStack() as st:
        woT_sb = st.enter_context(nc.sbuf_tensor(f"O_wo{tag}", [128, 16 * 2048], BF16))
        avb = st.enter_context(nc.sbuf_tensor(f"O_avb{tag}", [128, 16 * 1024], BF16))
        x1b = st.enter_context(nc.sbuf_tensor(f"O_x1b{tag}", [128, 2 * 2048], F32))
        outb = st.enter_context(nc.sbuf_tensor(f"O_outb{tag}", [128, 2 * 2048], F32))
        po = [st.enter_context(nc.psum_tensor(f"O_po{tag}_{i}", [128, 512], F32))
              for i in range(4)]

        def body(R, eng=None):
            C = dict(base)

            def INC(sname, k, ins=None):
                C[sname] += k
                if ins is not None:
                    ins.then_inc(sems[sname].h, k)
                return C[sname]

            def MK(key, sname):
                if R == "sim":
                    marks[key] = C[sname]

            def WT(sname, key_or_val):
                if R == "sim":
                    return
                v = marks[key_or_val] if isinstance(key_or_val, tuple) else key_or_val
                sems[sname].wait(eng, v)

            if R != "sim":
                sems["qc"].wait(eng, base["qc"])
            if R == "gpsimd":
                for kc in range(16):
                    eng.dma_start(out=woT_sb[:, kc * 2048:(kc + 1) * 2048],
                                  in_=woTap[kc * 128:(kc + 1) * 128, :]).then_inc(qg.h, 16)
            INC("qg", 16 * 16)
            MK(("o_wo",), "qg")

            if R == "gpsimd":
                own0 = (nc.partition_id() % 4) * 2048
                offs = [own0, own0 + 1024]
                gv = gataT.rearrange("(c p) t -> p c t", p=128)
                av = avb[:, :].rearrange("p (c t) -> p c t", t=1024)

            for tg in range(2):
                if R == "gpsimd":
                    if tg == 0:
                        WT("cc", cc_idx)
                    else:
                        WT("pe", ("o_pj", 7, 3))
                    for grp in range(2):
                        eng.dma_start(out=av[:, grp * 8:(grp + 1) * 8, :],
                                      in_=gv[:, grp * 8:(grp + 1) * 8,
                                             bass.ds(offs[tg], 1024)]
                                      ).then_inc(qg.h, 16)
                INC("qg", 16 * 2)
                MK(("o_avb", tg), "qg")
                for tt8 in range(8):
                    tt = tg * 8 + tt8
                    sxn = "wt" + str(tt % 2)
                    if R == "sync":
                        if tt == 0:
                            WT("qc", base["qc"])
                        if tt >= 2:
                            WT("dve", ("o_ev", tt - 2, 3))
                        eng.dma_start(out=x1b[:, (tt % 2) * 2048:(tt % 2 + 1) * 2048],
                                      in_=srcT[tt * 128:(tt + 1) * 128, :]).then_inc(sems[sxn].h, 16)
                    INC(sxn, 16)
                    MK(("o_x1", tt), sxn)
                    for ds in range(4):
                        if R == "tensor":
                            if tt == 0 and ds == 0 and tg == 0:
                                WT("qg", ("o_wo",))
                                WT("act", base["act"])
                                WT("dve", base["dve"])
                            WT("qg", ("o_avb", tg))
                            if tt8 > 0 or tg > 0:
                                pt = tt - 1
                                WT("dve", ("o_ev", pt, ds))
                            for kc in range(16):
                                ins = eng.matmul(po[ds][:, :],
                                                 avb[:, kc * 1024 + tt8 * 128:kc * 1024 + (tt8 + 1) * 128],
                                                 woT_sb[:, kc * 2048 + ds * 512:kc * 2048 + (ds + 1) * 512],
                                                 start=(kc == 0), stop=(kc == 15))
                            ins.then_inc(pe.h, 1)
                        INC("pe", 1)
                        MK(("o_pj", tt, ds), "pe")
                        if R == "vector":
                            WT("pe", ("o_pj", tt, ds))
                            if ds == 0:
                                WT("wt" + str(tt % 2), ("o_x1", tt))
                                if tt >= 1:
                                    WT("qc", ("o_st", tt - 1))
                            eng.tensor_add(outb[:, (tt % 2) * 2048 + ds * 512:
                                                (tt % 2) * 2048 + (ds + 1) * 512],
                                           po[ds][:, :],
                                           x1b[:, (tt % 2) * 2048 + ds * 512:
                                               (tt % 2) * 2048 + (ds + 1) * 512]
                                           ).then_inc(dve.h, 1)
                        INC("dve", 1)
                        MK(("o_ev", tt, ds), "dve")
                    if R == "scalar":
                        WT("dve", ("o_ev", tt, 3))
                        eng.dma_start(out=dstT[tt * 128:(tt + 1) * 128, :],
                                      in_=outb[:, (tt % 2) * 2048:(tt % 2 + 1) * 2048]
                                      ).then_inc(qc.h, 16)
                    INC("qc", 16)
                    MK(("o_st", tt), "qc")

            if R == "sim":
                for k in sems:
                    marks[("o_end", k)] = C[k]

        body("sim")
        with nc.Block() as blk:
            @blk.sync
            def _(sy):
                body("sync", sy)

            @blk.gpsimd
            def _(g):
                body("gpsimd", g)

            @blk.tensor
            def _(te):
                body("tensor", te)

            @blk.scalar
            def _(sc):
                body("scalar", sc)

            @blk.vector
            def _(v):
                body("vector", v)

        for k in sems:
            sems[k].n = marks[("o_end", k)]


def _phase_ffn1(nc, src, dst, pubT, qfcTap, qpjTap, W, sems, pcol, bo_bc):
    import contextlib
    qs, qc, qg = sems["qs"], sems["qc"], sems["qg"]
    pe, act, dve = sems["pe"], sems["act"], sems["dve"]
    base = {k: s.n for k, s in sems.items()}
    marks = {}

    with contextlib.ExitStack() as st:
        xnh = st.enter_context(nc.sbuf_tensor("F_xnh", [128, 16 * 1024], BF16))
        oac = st.enter_context(nc.sbuf_tensor("F_oac", [128, 8 * 2048], F32))
        rtmp = st.enter_context(nc.sbuf_tensor("F_rt", [128, 2 * 512], F32))
        htmp = st.enter_context(nc.sbuf_tensor("F_ht", [128, 2 * 512], F32))
        ytmp = st.enter_context(nc.sbuf_tensor("F_yt", [128, 2 * 512], F32))
        wt = st.enter_context(nc.sbuf_tensor("F_wt", [128, 4 * 512], BF16))
        h_fm = st.enter_context(nc.sbuf_tensor("F_hfm", [128, 20 * 1024], BF16))
        ident32 = st.enter_context(nc.sbuf_tensor("F_id32", [128, 128], F32))
        ps = [st.enter_context(nc.psum_tensor(f"F_ps{i}", [128, 512], F32))
              for i in range(8)]

        def body(R, eng=None):
            C = dict(base)

            def INC(sname, k, ins=None):
                C[sname] += k
                if ins is not None:
                    ins.then_inc(sems[sname].h, k)
                return C[sname]

            def MK(key, sname):
                if R == "sim":
                    marks[key] = C[sname]

            def WT(sname, key_or_val):
                if R == "sim":
                    return
                v = marks[key_or_val] if isinstance(key_or_val, tuple) else key_or_val
                sems[sname].wait(eng, v)

            if R != "sim":
                sems["qc"].wait(eng, base["qc"])
            if R == "sync":
                eng.dma_start(out=ident32[:, :], in_=W["ident32"]).then_inc(qs.h, 16)
            INC("qs", 16)

            for hf in range(2):
                if hf == 0:
                    if R == "sync":
                        WT("qc", base["qc"])
                else:
                    if R != "sim":
                        WT("qc", ("f_st", 0))
                    if R == "sync":
                        WT("pe", ("f_pe_end", 0))
                for tt in range(8):
                    if R == "sync":
                        eng.dma_start(out=oac[:, tt * 2048:(tt + 1) * 2048],
                                      in_=src[hf * 1024 + tt * 128:hf * 1024 + (tt + 1) * 128, :]
                                      ).then_inc(qs.h, 16)
                    INC("qs", 16)
                MK(("f_oacld", hf), "qs")
                for tt in range(8):
                    if R == "vector":
                        WT("qs", ("f_ld", hf))
                        eng.tensor_add(oac[:, tt * 2048:(tt + 1) * 2048],
                                       oac[:, tt * 2048:(tt + 1) * 2048],
                                       bo_bc[:, :]).then_inc(dve.h, 1)
                    INC("dve", 1)
                MK(("f_bo", hf), "dve")
                for kc in range(16):
                    if R == "sync":
                        eng.dma_start(out=xnh[:, kc * 1024:(kc + 1) * 1024],
                                      in_=pubT[kc * 128:(kc + 1) * 128,
                                               hf * 1024:(hf + 1) * 1024]).then_inc(qs.h, 16)
                    INC("qs", 16)
                MK(("f_ld", hf), "qs")

                for fh in range(2):
                    for si in range(5):
                        for kc in range(16):
                            wti = C["wti"] = C.get("wti", 0) + 1
                            slot = (wti - 1) % 4
                            sn = f"wt{slot}"
                            if R == "gpsimd":
                                if wti > 4:
                                    WT("pe", ("f_wfree", wti - 4))
                                eng.dma_start(out=wt[:, slot * 512:(slot + 1) * 512],
                                              in_=qfcTap[kc * 128:(kc + 1) * 128,
                                                         fh * 2560 + si * 512:
                                                         fh * 2560 + (si + 1) * 512]
                                              ).then_inc(sems[sn].h, 16)
                            INC(sn, 16)
                            MK(("f_wld", wti), sn)
                            if R == "tensor":
                                WT(sn, ("f_wld", wti))
                                if kc == 0:
                                    WT("qs", ("f_ld", hf))
                                    if hf == 0 and fh == 0 and si == 0:
                                        WT("act", base["act"])
                                        WT("dve", base["dve"])
                                    elif si == 0 and fh == 0:
                                        WT("dve", ("f_yev", hf - 1, 1, 3))
                                    elif si == 0:
                                        WT("dve", ("f_yev", hf, fh - 1, 3))
                                    else:
                                        WT("act", ("f_hev", hf, fh, si - 1, 7))
                                for tt in range(8):
                                    ins = eng.matmul(
                                        ps[tt][:, :],
                                        xnh[:, kc * 1024 + tt * 128:kc * 1024 + (tt + 1) * 128],
                                        wt[:, slot * 512:(slot + 1) * 512],
                                        start=(kc == 0), stop=(kc == 15))
                                    if tt == 7:
                                        ins.then_inc(pe.h, 1)
                            INC("pe", 1)
                            MK(("f_wfree", wti), "pe")
                            if kc == 15:
                                MK(("f_gstop", hf, fh, si), "pe")
                        for tt in range(8):
                            if R == "scalar":
                                WT("pe", ("f_gstop", hf, fh, si))
                                if tt >= 2 or si > 0 or fh > 0 or hf > 0:
                                    pk = ("f_tp", hf, fh, si, tt - 2)
                                    if tt < 2:
                                        pk = None
                                    if pk is not None and pk in marks:
                                        WT("pe", pk)
                                eng.activation(rtmp[:, (tt % 2) * 512:(tt % 2 + 1) * 512],
                                               ps[tt][:, :], AF.Relu).then_inc(act.h, 1)
                                eng.activation(htmp[:, (tt % 2) * 512:(tt % 2 + 1) * 512],
                                               rtmp[:, (tt % 2) * 512:(tt % 2 + 1) * 512],
                                               AF.Square).then_inc(act.h, 1)
                            INC("act", 2)
                            MK(("f_sq", hf, fh, si, tt), "act")
                            if R == "tensor":
                                WT("act", ("f_sq", hf, fh, si, tt))
                                for j in range(4):
                                    ins = eng.transpose(
                                        ps[tt][:, j * 128:(j + 1) * 128],
                                        htmp[:, (tt % 2) * 512 + j * 128:(tt % 2) * 512 + (j + 1) * 128],
                                        ident32[:, :])
                                ins.then_inc(pe.h, 1)
                            INC("pe", 1)
                            MK(("f_tp", hf, fh, si, tt), "pe")
                            if R == "scalar":
                                WT("pe", ("f_tp", hf, fh, si, tt))
                                if hf > 0 or fh > 0:
                                    WT("pe", ("f_ygstop", hf if fh > 0 else hf - 1,
                                              0 if fh > 0 else 1, 3))
                                dst_ap = bass.AP(h_fm, si * 4 * 1024 + tt * 128,
                                                 [[20 * 1024, 128], [1024, 4], [1, 128]])
                                eng.activation(dst_ap,
                                               ps[tt][:, :].rearrange("p (s t) -> p s t", t=128),
                                               AF.Copy).then_inc(act.h, 1)
                            INC("act", 1)
                            MK(("f_hev", hf, fh, si, tt), "act")

                    for ds in range(4):
                        for ic in range(20):
                            wti = C["wti"] = C.get("wti", 0) + 1
                            slot = (wti - 1) % 4
                            sn = f"wt{slot}"
                            if R == "gpsimd":
                                WT("pe", ("f_wfree", wti - 4))
                                eng.dma_start(out=wt[:, slot * 512:(slot + 1) * 512],
                                              in_=qpjTap[fh * 2560 + ic * 128:
                                                         fh * 2560 + (ic + 1) * 128,
                                                         ds * 512:(ds + 1) * 512]
                                              ).then_inc(sems[sn].h, 16)
                            INC(sn, 16)
                            MK(("f_wld", wti), sn)
                            if R == "tensor":
                                WT(sn, ("f_wld", wti))
                                if ic == 0:
                                    WT("act", ("f_hev", hf, fh, 4, 7))
                                    if ds > 0:
                                        WT("dve", ("f_yev", hf, fh, ds - 1))
                                for tt in range(8):
                                    ins = eng.matmul(
                                        ps[tt][:, :],
                                        h_fm[:, ic * 1024 + tt * 128:ic * 1024 + (tt + 1) * 128],
                                        wt[:, slot * 512:(slot + 1) * 512],
                                        start=(ic == 0), stop=(ic == 19))
                                    if tt == 7:
                                        ins.then_inc(pe.h, 1)
                            INC("pe", 1)
                            MK(("f_wfree", wti), "pe")
                            if ic == 19:
                                MK(("f_ygstop", hf, fh, ds), "pe")
                        for tt in range(8):
                            if R == "vector":
                                WT("pe", ("f_ygstop", hf, fh, ds))
                                eng.tensor_scalar(ytmp[:, (tt % 2) * 512:(tt % 2 + 1) * 512],
                                                  ps[tt][:, :], pcol[:, 4:5], None,
                                                  ALU.mult)
                                eng.tensor_add(
                                    oac[:, tt * 2048 + ds * 512:tt * 2048 + (ds + 1) * 512],
                                    ytmp[:, (tt % 2) * 512:(tt % 2 + 1) * 512],
                                    oac[:, tt * 2048 + ds * 512:tt * 2048 + (ds + 1) * 512]
                                    ).then_inc(dve.h, 1)
                            INC("dve", 1)
                            if tt in (3, 7):
                                MK(("f_yev", hf, fh, ds, tt), "dve")
                        MK(("f_yev", hf, fh, ds), "dve")
                MK(("f_pe_end", hf), "pe")
                if R == "vector":
                    eng.memset(ytmp[:, 0:4], 0.0).then_inc(dve.h, 1)
                    eng.memset(ytmp[:, 4:8], 0.0).then_inc(dve.h, 1)
                INC("dve", 2)
                MK(("f_flush", hf), "dve")

                for tt in range(8):
                    if R == "scalar":
                        WT("dve", ("f_flush", hf))
                        eng.dma_start(out=dst[hf * 1024 + tt * 128:hf * 1024 + (tt + 1) * 128, :],
                                      in_=oac[:, tt * 2048:(tt + 1) * 2048]).then_inc(qc.h, 16)
                    INC("qc", 16)
                MK(("f_st", hf), "qc")

            if R == "sim":
                for k in sems:
                    marks[("f_end", k)] = C[k]

        body("sim")
        with nc.Block() as blk:
            @blk.sync
            def _(sy):
                body("sync", sy)

            @blk.gpsimd
            def _(g):
                body("gpsimd", g)

            @blk.tensor
            def _(te):
                body("tensor", te)

            @blk.scalar
            def _(sc):
                body("scalar", sc)

            @blk.vector
            def _(v):
                body("vector", v)

        for k in sems:
            sems[k].n = marks[("f_end", k)]


_NC_CACHE = {}


def _get_program(layers, taps=()):
    key = (tuple(layers), tuple(taps))
    if key not in _NC_CACHE:
        _NC_CACHE[key] = build(layers, taps)
    return _NC_CACHE[key]


def _assemble(outs):
    x = np.empty((B, S, D), np.float32)
    for c in range(NCORES):
        b, p = c // 4, c % 4
        x[b, p * TOK:(p + 1) * TOK] = outs[c]
    return x


FUSED = True


def kernel(**inputs):
    ins = {k: np.asarray(v) for k, v in inputs.items()}
    if FUSED:
        maps = host_prep(ins, layers=(0, 1))
        nc = _get_program((0, 1))
        res = run_bass_kernel_spmd(nc, maps, core_ids=list(range(NCORES)))
        return _assemble([res.results[c]["out"] for c in range(NCORES)])
    maps0 = host_prep(ins, layers=(0,))
    res0 = run_bass_kernel_spmd(_get_program((0,)), maps0,
                                core_ids=list(range(NCORES)))
    x1 = _assemble([res0.results[c]["out"] for c in range(NCORES)])
    ins2 = dict(ins)
    ins2["x1"] = x1
    maps1 = host_prep(ins2, layers=(1,))
    res1 = run_bass_kernel_spmd(_get_program((1,)), maps1,
                                core_ids=list(range(NCORES)))
    return _assemble([res1.results[c]["out"] for c in range(NCORES)])


# version 04
# speedup vs baseline: 1.2701x; 1.2701x over previous
import numpy as np
import ml_dtypes
import concourse.bass as bass
import concourse.mybir as mybir
from concourse.bass_utils import run_bass_kernel_spmd

F32 = mybir.dt.float32
BF16 = mybir.dt.bfloat16
AF = mybir.ActivationFunctionType
ALU = mybir.AluOpType
AX = mybir.AxisListType
BF = ml_dtypes.bfloat16

B, S, D = 2, 8192, 2048
NCORES = 8
TOK = 2048
NTT = TOK // 128
HPC = 4
DH = 128
NKC = D // 128
NQT = S // 512
FSH, FRT = 2816, 1408
DC, DQC = 512, 1024
FT = 5120
EPS = 1e-5
MOE_EXPERTS = 9
MOE_HALVES = 2
ATTN_CC = True
MOE_DBG = False
MOE_DBG2 = False
RGROUPS = [[0, 1, 2, 3], [4, 5, 6, 7]]
PUB0, PUB1 = D, D + DC + DQC


class Sem:
    __slots__ = ("h", "n", "maxw", "name")

    def __init__(self, h, name):
        self.h = h
        self.n = 0
        self.maxw = 0
        self.name = name

    def inc(self, ins, k):
        ins.then_inc(self.h, k)
        self.n += k
        return self.n

    def wait(self, eng, v):
        if v > self.maxw:
            self.maxw = v
        eng.wait_ge(self.h, v)


def _slices(total, step):
    return [(o, min(step, total - o)) for o in range(0, total, step)]


def _host_consts():
    c = {}
    c["ident16"] = np.eye(128, dtype=BF)
    c["ident32"] = np.eye(128, dtype=np.float32)
    P = np.zeros((128, 128), np.float32)
    for i in range(64):
        P[2 * i, 2 * i + 1] = -1.0
        P[2 * i + 1, 2 * i] = 1.0
    c["prope"] = P.T.astype(BF)
    m = np.zeros((128, 4 * 512), np.float32)
    qq = np.arange(512)
    for j in range(4):
        for kk in range(128):
            m[kk, j * 512:(j + 1) * 512] = (128 * j + kk <= qq)
    c["masks"] = m.astype(BF)
    idx = np.arange(64, dtype=np.float64)
    inv = (10000.0 ** (-2.0 * idx / 128.0))
    pos = np.arange(S, dtype=np.float64)
    theta = pos[:, None] * inv[None, :]
    cos = np.cos(theta).astype(np.float32)
    sin = np.sin(theta).astype(np.float32)
    c["cosq"] = np.ascontiguousarray(np.repeat(cos.T, 2, axis=0))
    c["sinq"] = np.ascontiguousarray(np.repeat(sin.T, 2, axis=0))
    c["cosk"] = np.ascontiguousarray(c["cosq"] / np.float32(16.0))
    c["sink"] = np.ascontiguousarray(c["sinq"] / np.float32(16.0))
    return c


def host_prep(ins, layers=(0, 1)):
    cst = _host_consts()
    shared = {}
    if 0 in layers:
        shared.update(
            sw1=ins["L0_ffn_sw1"], sw2=ins["L0_ffn_sw2"], sw3=ins["L0_ffn_sw3"],
            rw1=ins["L0_ffn_rw1"], rw2=ins["L0_ffn_rw2"], rw3=ins["L0_ffn_rw3"],
            gwf=ins["L0_ffn_gate_w"],
            gwT=np.ascontiguousarray(ins["L0_attn_gate_w"].T),
            gbcol=np.ascontiguousarray(ins["L0_attn_gate_b"].reshape(128, 1)),
            woT=np.ascontiguousarray(ins["L0_attn_o_w"].T),
        )
    if 1 in layers:
        shared.update(
            wdkvT=np.ascontiguousarray(ins["L1_attn_W_DKV_w"].T),
            bdkv=np.ascontiguousarray(ins["L1_attn_W_DKV_b"].reshape(DC, 1)),
            wdqT=np.ascontiguousarray(ins["L1_attn_W_DQ_w"].T),
            bdq=np.ascontiguousarray(ins["L1_attn_W_DQ_b"].reshape(DQC, 1)),
            woT1=np.ascontiguousarray(ins["L1_attn_W_O_w"].T),
            borow=np.ascontiguousarray(ins["L1_attn_W_O_b"].reshape(1, D)),
            wfcT=np.ascontiguousarray(ins["L1_ffn_w_fc"].T),
            wpjT=np.ascontiguousarray(ins["L1_ffn_w_proj"].T),
            cosq=cst["cosq"], sinq=cst["sinq"], cosk=cst["cosk"], sink=cst["sink"],
            prope=cst["prope"],
        )
    shared.update(ident16=cst["ident16"], ident32=cst["ident32"], masks=cst["masks"])

    if 0 in layers:
        qT = np.ascontiguousarray(ins["L0_attn_q_w"].T)
        kT = np.ascontiguousarray(ins["L0_attn_k_w"].T)
        vT = np.ascontiguousarray(ins["L0_attn_v_w"].T)
    if 1 in layers:
        ukT = np.ascontiguousarray(ins["L1_attn_W_UK_w"].T)
        uvT = np.ascontiguousarray(ins["L1_attn_W_UV_w"].T)
        uqT = np.ascontiguousarray(ins["L1_attn_W_UQ_w"].T)
        krT = np.ascontiguousarray(ins["L1_attn_W_KR_w"].T)

    xin_key = "L0_attn_x" if 0 in layers else "x1"
    xfull = np.asarray(ins[xin_key])
    if xfull.ndim == 3:
        xfull = xfull.reshape(B, S, D)

    maps = []
    for c in range(NCORES):
        b, p = c // 4, c % 4
        hs = slice(p * 512, (p + 1) * 512)
        m = dict(shared)
        m["xin"] = np.ascontiguousarray(xfull[b, p * TOK:(p + 1) * TOK, :], dtype=np.float32)
        if 0 in layers:
            m["wqT"] = np.ascontiguousarray(qT[:, hs])
            m["wkT"] = np.ascontiguousarray(kT[:, hs])
            m["wvT"] = np.ascontiguousarray(vT[:, hs])
        if 1 in layers:
            m["wukT"] = np.ascontiguousarray(ukT[:, hs])
            m["buk"] = np.ascontiguousarray(ins["L1_attn_W_UK_b"][hs].reshape(512, 1))
            m["wuvT"] = np.ascontiguousarray(uvT[:, hs])
            m["buv"] = np.ascontiguousarray(ins["L1_attn_W_UV_b"][hs].reshape(512, 1))
            m["wuqT"] = np.ascontiguousarray(uqT[:, hs])
            m["buq"] = np.ascontiguousarray(ins["L1_attn_W_UQ_b"][hs].reshape(512, 1))
            m["wkrT"] = np.ascontiguousarray(krT[:, hs])
            m["bkr"] = np.ascontiguousarray(ins["L1_attn_W_KR_b"][hs].reshape(512, 1))
        maps.append(m)
    return maps


def build(layers=(0, 1), taps=(), upto=None):
    import contextlib
    _pidx = [0]

    def _go():
        _pidx[0] += 1
        return upto is None or _pidx[0] <= upto

    nc = bass.Bass("TRN2", target_bir_lowering=False)
    HASL0, HASL1 = (0 in layers), (1 in layers)

    xin = nc.dram_tensor("xin", [TOK, D], F32, kind="ExternalInput").ap()
    out = nc.dram_tensor("out", [TOK, D], F32, kind="ExternalOutput").ap()
    if HASL0 and HASL1:
        r1 = nc.dram_tensor("r1", [TOK, D], F32).ap()
    else:
        r1 = out if HASL0 else None
    l_in = {0: xin, 1: (r1 if HASL0 else xin)}
    l_out = {0: r1, 1: out}

    W = {}

    def din(name, shape, dt=F32):
        W[name] = nc.dram_tensor(name, list(shape), dt, kind="ExternalInput").ap()
        return W[name]

    din("ident16", (128, 128), BF16)
    din("ident32", (128, 128), F32)
    din("masks", (128, 4 * 512), BF16)
    if HASL0:
        din("sw1", (D, FSH)); din("sw2", (FSH, D)); din("sw3", (D, FSH))
        din("rw1", (8, D, FRT)); din("rw2", (8, FRT, D)); din("rw3", (8, D, FRT))
        din("gwf", (D, 8))
        din("gwT", (128, 128)); din("gbcol", (128, 1))
        din("wqT", (D, 512)); din("wkT", (D, 512)); din("wvT", (D, 512))
        din("woT", (D, D))
    if HASL1:
        din("wdkvT", (D, DC)); din("bdkv", (DC, 1))
        din("wdqT", (D, DQC)); din("bdq", (DQC, 1))
        din("wukT", (DC, 512)); din("buk", (512, 1))
        din("wuvT", (DC, 512)); din("buv", (512, 1))
        din("wuqT", (DQC, 512)); din("buq", (512, 1))
        din("wkrT", (D, 512)); din("bkr", (512, 1))
        din("woT1", (D, D)); din("borow", (1, D))
        din("wfcT", (D, FT)); din("wpjT", (FT, D))
        din("prope", (128, 128), BF16)
        din("cosq", (128, S)); din("sinq", (128, S))
        din("cosk", (128, S)); din("sink", (128, S))

    pub, gat, puba, gata = {}, {}, {}, {}
    for lay in layers:
        pr = PUB0 if lay == 0 else PUB1
        pub[lay] = nc.dram_tensor(f"pub{lay}", [pr, TOK], BF16).ap()
        gat[lay] = nc.dram_tensor(f"gat{lay}", [4 * pr, TOK], BF16).ap()
        puba[lay] = nc.dram_tensor(f"puba{lay}", [HPC * DH, S], BF16).ap()
        gata[lay] = nc.dram_tensor(f"gata{lay}", [4 * HPC * DH, S], BF16).ap()
    if HASL1:
        qfcT = nc.dram_tensor("qfcT", [D, FT], BF16).ap()
        qpjT = nc.dram_tensor("qpjT", [FT, D], BF16).ap()

    tap_out = {}
    for t in taps:
        srcs = {"pub0": (pub.get(0), BF16), "pub1": (pub.get(1), BF16),
                "puba0": (puba.get(0), BF16), "puba1": (puba.get(1), BF16),
                "gata0": (gata.get(0), BF16),
                "r1": (r1 if (HASL0 and HASL1) else None, F32)}
        src, dt = srcs[t]
        tap_out[t] = (nc.dram_tensor("tap_" + t, list(src.tensor.shape), dt,
                                     kind="ExternalOutput").ap(), src)

    ctx = contextlib.ExitStack()
    with ctx:
        sems = {}
        for name in ("qs", "qc", "qg", "cc", "pe", "act", "dve", "pool",
                     "wt0", "wt1", "wt2", "wt3"):
            sems[name] = Sem(ctx.enter_context(nc.semaphore(name)), name)
        qs, qc, qg, cc = sems["qs"], sems["qc"], sems["qg"], sems["cc"]
        pe, act, dve, pool = sems["pe"], sems["act"], sems["dve"], sems["pool"]

        ident16 = ctx.enter_context(nc.sbuf_tensor("ident16s", [128, 128], BF16))
        ident32 = ctx.enter_context(nc.sbuf_tensor("ident32s", [128, 128], F32))
        masks = ctx.enter_context(nc.sbuf_tensor("masks_s", [128, 4 * 512], BF16))
        ones_c = ctx.enter_context(nc.sbuf_tensor("ones_c", [128, 1], F32))
        ones_c16 = ctx.enter_context(nc.sbuf_tensor("ones_c16", [128, 1], BF16))
        ones_r = ctx.enter_context(nc.sbuf_tensor("ones_r", [1, 128], F32))
        pcol = ctx.enter_context(nc.sbuf_tensor("pcol", [128, 40], F32))
        prow = ctx.enter_context(nc.sbuf_tensor("prow", [1, 256], F32))
        bo_bc = ctx.enter_context(nc.sbuf_tensor("bo_bc", [128, D], F32)) if HASL1 else None
        prope_s = ctx.enter_context(nc.sbuf_tensor("prope_s", [128, 128], BF16)) if HASL1 else None
        borow_s = ctx.enter_context(nc.sbuf_tensor("borow_s", [1, D], F32)) if HASL1 else None

        with nc.Block() as blk:
            @blk.sync
            def _(sy):
                sy.dma_start(out=ident16[:, :], in_=W["ident16"]).then_inc(qs.h, 16)
                sy.dma_start(out=ident32[:, :], in_=W["ident32"]).then_inc(qs.h, 16)
                sy.dma_start(out=masks[:, :], in_=W["masks"]).then_inc(qs.h, 16)
                qs.n += 48
                if HASL0:
                    sy.dma_start(out=pcol[:, 5:6], in_=W["gbcol"]).then_inc(qs.h, 16)
                    qs.n += 16
                if HASL1:
                    for i in range(4):
                        sy.dma_start(out=pcol[:, 6 + i:7 + i],
                                     in_=W["bdkv"][i * 128:(i + 1) * 128, :]).then_inc(qs.h, 16)
                    for i in range(8):
                        sy.dma_start(out=pcol[:, 10 + i:11 + i],
                                     in_=W["bdq"][i * 128:(i + 1) * 128, :]).then_inc(qs.h, 16)
                    for nm, base in (("buk", 18), ("buv", 22), ("buq", 26), ("bkr", 30)):
                        for i in range(4):
                            sy.dma_start(out=pcol[:, base + i:base + i + 1],
                                         in_=W[nm][i * 128:(i + 1) * 128, :]).then_inc(qs.h, 16)
                    sy.dma_start(out=prope_s[:, :], in_=W["prope"]).then_inc(qs.h, 16)
                    sy.dma_start(out=borow_s[:, :], in_=W["borow"]).then_inc(qs.h, 16)
                    qs.n += 16 * (4 + 8 + 16 + 2)

            @blk.vector
            def _(v):
                v.memset(ones_c[:, :], 1.0)
                v.memset(ones_c16[:, :], 1.0)
                v.memset(pcol[:, 39:40], EPS)
                v.memset(ones_r[:, :], 1.0).then_inc(dve.h, 1)
                dve.n += 1

            @blk.scalar
            def _(sc):
                qs.wait(sc, qs.n)
                dve.wait(sc, dve.n)
                if HASL1:
                    for i in range(4):
                        sc.mul(pcol[:, 18 + i:19 + i], pcol[:, 18 + i:19 + i], 1.0 / 16.0)

        if HASL1:
            with nc.psum_tensor("pbo", [128, 512], F32) as pbo, nc.Block() as blk:
                @blk.tensor
                def _(te):
                    qs.wait(te, qs.n)
                    dve.wait(te, dve.n)
                    for i in range(4):
                        act.wait(te, act.n + i)
                        te.matmul(pbo[:, :], ones_r[:, 0:128], borow_s[0:1, i * 512:(i + 1) * 512],
                                  start=True, stop=True).then_inc(pe.h, 1)
                        pe.n += 1

                @blk.scalar
                def _(sc):
                    for i in range(4):
                        pe.wait(sc, pe.n - 3 + i)
                        sc.activation(bo_bc[:, i * 512:(i + 1) * 512], pbo[:, :],
                                      AF.Copy).then_inc(act.h, 1)
                        act.n += 1

        if HASL0:
            if _go():
                _phase_ln_pub(nc, 0, l_in[0], pub[0], gat[0], W, sems,
                              ident16, ones_c, ones_r, pcol, bo_bc)
            if _go():
                _phase_moe(nc, l_in[0], l_out[0], pub[0], W, sems, pcol)
            if HASL1 and _go():
                _tern_prepass(nc, W, sems, ones_c, ones_r, pcol, prow)
            if _go():
                _phase_attn(nc, 0, gat[0], puba[0], gata[0], W, sems,
                            ident16, ones_c16, ones_r, pcol, prope_s, sems["cc"].n)
            if _go():
                _phase_oproj(nc, gata[0], l_out[0], l_out[0], W["woT"], sems,
                             sems["cc"].n, tag=0)
            if HASL1 and _go():
                _tern_quant(nc, W, sems, pcol, qfcT, qpjT)
        if HASL1:
            if not HASL0:
                if _go():
                    _tern_prepass(nc, W, sems, ones_c, ones_r, pcol, prow)
                if _go():
                    _tern_quant(nc, W, sems, pcol, qfcT, qpjT)
            if _go():
                _phase_ln_pub(nc, 1, l_in[1], pub[1], gat[1], W, sems,
                              ident16, ones_c, ones_r, pcol, bo_bc)
            if _go():
                _phase_ffn1(nc, l_in[1], l_out[1], pub[1], qfcT, qpjT, W, sems,
                            pcol, bo_bc)
            if _go():
                _phase_attn(nc, 1, gat[1], puba[1], gata[1], W, sems,
                            ident16, ones_c16, ones_r, pcol, prope_s, sems["cc"].n)
            if _go():
                _phase_oproj(nc, gata[1], l_out[1], out, W["woT1"], sems,
                             sems["cc"].n, tag=1)

        with nc.Block() as blk:
            @blk.gpsimd
            def _(g):
                qc.wait(g, qc.n)
                qs.wait(g, qs.n)
                for t, (dst, src) in tap_out.items():
                    g.dma_start(out=dst, in_=src).then_inc(qg.h, 16)
                    qg.n += 16
                qg.wait(g, qg.n)

            @blk.sync
            def _(sy):
                qc.wait(sy, qc.n)

        for s in sems.values():
            assert s.maxw <= s.n, f"sem {s.name}: wait {s.maxw} > total {s.n}"
    return nc


def _phase_ln_pub(nc, lay, src, pubT, gatT, W, sems, ident16, ones_c, ones_r,
                  pcol, bo_bc):
    qs, qc, qg, cc = sems["qs"], sems["qc"], sems["qg"], sems["cc"]
    pe, act, dve, pool = sems["pe"], sems["act"], sems["dve"], sems["pool"]
    L1 = (lay == 1)
    APT = 21 if L1 else 9
    PPT = 16 if L1 else 4
    NPUB = 2 if L1 else 1
    qs0, qc0, qg0, pe0, act0, dve0 = qs.n, qc.n, qg.n, pe.n, act.n, dve.n
    w0_, w1_ = sems["wt0"].n, sems["wt1"].n
    D_ = D

    import contextlib
    with contextlib.ExitStack() as st:
        xbuf = st.enter_context(nc.sbuf_tensor(f"A{lay}_xbuf", [128, 2 * D_], F32))
        xsq = st.enter_context(nc.sbuf_tensor(f"A{lay}_xsq", [128, 2 * D_], F32))
        xnt = st.enter_context(nc.sbuf_tensor(f"A{lay}_xnt", [128, 2 * D_], BF16))
        stage = st.enter_context(nc.sbuf_tensor(f"A{lay}_stage", [128, 2 * D_], BF16))
        stf = st.enter_context(nc.sbuf_tensor(f"A{lay}_stf", [128, 114], F32))
        trpb = [st.enter_context(nc.psum_tensor(f"A{lay}_trp{i}", [128, 512], BF16))
                for i in range(2)]
        if L1:
            ckst = st.enter_context(nc.sbuf_tensor("A1_ckst", [128, 2 * 1536], BF16))
            wdkv_sb = st.enter_context(nc.sbuf_tensor("A1_wdkv", [128, 16 * 512], BF16))
            wdq_sb = st.enter_context(nc.sbuf_tensor("A1_wdq", [128, 16 * 1024], BF16))
            pck = [st.enter_context(nc.psum_tensor(f"A1_pck{i}", [128, 128], F32))
                   for i in range(4)]

        with nc.Block() as blk:
            @blk.sync
            def _(sy):
                qc.wait(sy, qc0)
                for t in range(NTT):
                    if t >= 2:
                        dve.wait(sy, dve0 + 5 * (t - 1))
                    wsem = sems["wt" + str(t % 2)]
                    sy.dma_start(out=xbuf[:, (t % 2) * D_:(t % 2 + 1) * D_],
                                 in_=src[t * 128:(t + 1) * 128, :]).then_inc(wsem.h, 16)
                    wsem.n += 16

            @blk.gpsimd
            def _(g):
                qc.wait(g, qc0)
                if L1:
                    for kc in range(16):
                        g.dma_start(out=wdkv_sb[:, kc * 512:(kc + 1) * 512],
                                    in_=W["wdkvT"][kc * 128:(kc + 1) * 128, :]).then_inc(qg.h, 16)
                        qg.n += 16
                    for kc in range(16):
                        g.dma_start(out=wdq_sb[:, kc * 1024:(kc + 1) * 1024],
                                    in_=W["wdqT"][kc * 128:(kc + 1) * 128, :]).then_inc(qg.h, 16)
                        qg.n += 16
                qc.wait(g, qc0 + 16 * NPUB * NTT)
                g.collective_compute(
                    "AllGather", ALU.bypass, replica_groups=RGROUPS,
                    ins=[pubT.opt()], outs=[gatT.opt()]).then_inc(cc.h, 1)
                cc.n += 1

            @blk.vector
            def _(v):
                qc.wait(v, qc0)
                for t in range(NTT):
                    sl = slice((t % 2) * D_, (t % 2 + 1) * D_)
                    wsem = sems["wt" + str(t % 2)]
                    wsem.wait(v, (w0_ if t % 2 == 0 else w1_) + 16 * (t // 2 + 1))
                    v.reduce_sum(stf[:, t:t + 1], xbuf[:, sl], axis=AX.X).then_inc(dve.h, 1)
                    act.wait(v, act0 + APT * t + 1)
                    v.reduce_sum(stf[:, 16 + t:17 + t], xsq[:, sl], axis=AX.X).then_inc(dve.h, 1)
                    act.wait(v, act0 + APT * t + 4)
                    v.tensor_scalar(stf[:, 64 + t:65 + t], stf[:, 112 + t % 2:113 + t % 2],
                                    stf[:, 48 + t:49 + t], None,
                                    ALU.subtract).then_inc(dve.h, 1)
                    act.wait(v, act0 + APT * t + 5)
                    v.reciprocal(stf[:, 96 + t:97 + t], stf[:, 80 + t:81 + t]).then_inc(dve.h, 1)
                    v.tensor_scalar(xnt[:, sl], xbuf[:, sl], stf[:, 32 + t:33 + t],
                                    stf[:, 96 + t:97 + t], ALU.subtract,
                                    ALU.mult).then_inc(dve.h, 1)
                    dve.n += 5

            @blk.tensor
            def _(te):
                qc.wait(te, qc0)
                act.wait(te, act0)
                if L1:
                    qg.wait(te, qg0 + 16 * 32)
                for t in range(NTT):
                    sl16 = slice((t % 2) * D_, (t % 2 + 1) * D_)
                    dve.wait(te, dve0 + 5 * t + 5)
                    for g in range(4):
                        G = 4 * t + g
                        if G >= 2:
                            t2, g2 = (G - 2) // 4, (G - 2) % 4
                            act.wait(te, act0 + APT * t2 + 6 + g2)
                        for j in range(4):
                            ins = te.transpose(trpb[g % 2][:, j * 128:(j + 1) * 128],
                                               xnt[:, (t % 2) * D_ + (4 * g + j) * 128:
                                                   (t % 2) * D_ + (4 * g + j + 1) * 128],
                                               ident16[:, :])
                            if j == 3:
                                ins.then_inc(pe.h, 1)
                                pe.n += 1
                    if L1:
                        act.wait(te, act0 + APT * t + 9)
                        for r, (wsb, wstep, mcs) in enumerate(
                                [(wdkv_sb, 512, range(4)),
                                 (wdq_sb, 1024, range(4)),
                                 (wdq_sb, 1024, range(4, 8))]):
                            for i, mc in enumerate(mcs):
                                if r > 0:
                                    act.wait(te, act0 + APT * t + 9 + 4 * (r - 1) + i + 1)
                                for kc in range(16):
                                    ins = te.matmul(
                                        pck[i][:, :],
                                        wsb[:, kc * wstep + mc * 128:kc * wstep + (mc + 1) * 128],
                                        stage[:, (t % 2) * D_ + kc * 128:(t % 2) * D_ + (kc + 1) * 128],
                                        start=(kc == 0), stop=(kc == 15))
                                ins.then_inc(pe.h, 1)
                                pe.n += 1

            @blk.scalar
            def _(sc):
                qc.wait(sc, qc0)
                for t in range(NTT):
                    sl = slice((t % 2) * D_, (t % 2 + 1) * D_)
                    wsem = sems["wt" + str(t % 2)]
                    wsem.wait(sc, (w0_ if t % 2 == 0 else w1_) + 16 * (t // 2 + 1))
                    sc.activation(xsq[:, sl], xbuf[:, sl], AF.Square).then_inc(act.h, 1)
                    dve.wait(sc, dve0 + 5 * t + 1)
                    sc.activation(stf[:, 32 + t:33 + t], stf[:, t:t + 1], AF.Copy,
                                  scale=1.0 / D_).then_inc(act.h, 1)
                    sc.activation(stf[:, 48 + t:49 + t], stf[:, 32 + t:33 + t],
                                  AF.Square).then_inc(act.h, 1)
                    dve.wait(sc, dve0 + 5 * t + 2)
                    sc.activation(stf[:, 112 + t % 2:113 + t % 2], stf[:, 16 + t:17 + t],
                                  AF.Copy, scale=1.0 / D_).then_inc(act.h, 1)
                    dve.wait(sc, dve0 + 5 * t + 3)
                    sc.activation(stf[:, 80 + t:81 + t], stf[:, 64 + t:65 + t], AF.Sqrt,
                                  bias=pcol[:, 39:40]).then_inc(act.h, 1)
                    act.n += 5
                    if t >= 1:
                        qc.wait(sc, qc0 + 16 * NPUB * t)
                    for g in range(4):
                        pe.wait(sc, pe0 + PPT * t + g + 1)
                        sc.activation(stage[:, (t % 2) * D_ + g * 512:(t % 2) * D_ + (g + 1) * 512],
                                      trpb[g % 2][:, :], AF.Copy).then_inc(act.h, 1)
                        act.n += 1
                    if L1:
                        for mc in range(4):
                            pe.wait(sc, pe0 + PPT * t + 4 + mc + 1)
                            sc.activation(ckst[:, (t % 2) * 1536 + mc * 128:(t % 2) * 1536 + (mc + 1) * 128],
                                          pck[mc][:, :], AF.Identity,
                                          bias=pcol[:, 6 + mc:7 + mc]).then_inc(act.h, 1)
                            act.n += 1
                        for mc in range(8):
                            pe.wait(sc, pe0 + PPT * t + 8 + mc + 1)
                            sc.activation(ckst[:, (t % 2) * 1536 + 512 + mc * 128:
                                              (t % 2) * 1536 + 512 + (mc + 1) * 128],
                                          pck[mc % 4][:, :], AF.Identity,
                                          bias=pcol[:, 10 + mc:11 + mc]).then_inc(act.h, 1)
                            act.n += 1
                    sc.dma_start(
                        out=pubT[0:D_, t * 128:(t + 1) * 128].rearrange("(c p) t -> p c t", p=128),
                        in_=stage[:, sl].rearrange("p (c t) -> p c t", t=128)).then_inc(qc.h, 16)
                    qc.n += 16
                    if L1:
                        sc.dma_start(
                            out=pubT[D_:D_ + 1536, t * 128:(t + 1) * 128].rearrange("(c p) t -> p c t", p=128),
                            in_=ckst[:, (t % 2) * 1536:(t % 2 + 1) * 1536].rearrange(
                                "p (c t) -> p c t", t=128)).then_inc(qc.h, 16)
                        qc.n += 16



def _phase_moe(nc, src, dst, pubT, W, sems, pcol, bo_bc=None):
    import contextlib
    qs, qc, qg = sems["qs"], sems["qc"], sems["qg"]
    pe, act, dve = sems["pe"], sems["act"], sems["dve"]
    base = {k: s.n for k, s in sems.items()}
    marks = {}
    SH = _slices(FSH, 512)
    RT = _slices(FRT, 512)

    with contextlib.ExitStack() as st:
        xnh = st.enter_context(nc.sbuf_tensor("M_xnh", [128, 16 * 1024], BF16))
        oac = st.enter_context(nc.sbuf_tensor("M_oac", [128, 8 * 2048], F32))
        s_sb = st.enter_context(nc.sbuf_tensor("M_s", [128, 8 * 512], BF16))
        htmp = st.enter_context(nc.sbuf_tensor("M_ht", [128, 2 * 512], F32))
        htA = st.enter_context(nc.sbuf_tensor("M_htA", [128, 2 * 512], F32))
        wt = st.enter_context(nc.sbuf_tensor("M_wt", [128, 4 * 2048], BF16))
        h_fm = st.enter_context(nc.sbuf_tensor("M_hfm", [128, 22 * 1024], BF16))
        cwsb = st.enter_context(nc.sbuf_tensor("M_cw", [128, 64], F32))
        rb = st.enter_context(nc.sbuf_tensor("M_rb", [128, 4 * 64], F32))
        rst = st.enter_context(nc.sbuf_tensor("M_rst", [128, 6 * 8], F32))
        lg = st.enter_context(nc.sbuf_tensor("M_lg", [128, 64], F32))
        gwf_sb = st.enter_context(nc.sbuf_tensor("M_gwf", [128, 16 * 8], BF16))
        ident32 = st.enter_context(nc.sbuf_tensor("M_id32", [128, 128], F32))
        ps = [st.enter_context(nc.psum_tensor(f"M_ps{i}", [128, 512], F32))
              for i in range(8)]

        def body(R, eng=None):
            C = dict(base)

            def INC(sname, k, ins=None):
                C[sname] += k
                if ins is not None:
                    ins.then_inc(sems[sname].h, k)
                return C[sname]

            def MK(key, sname):
                if R == "sim":
                    marks[key] = C[sname]

            def WT(sname, key_or_val):
                if R == "sim":
                    return
                v = marks[key_or_val] if isinstance(key_or_val, tuple) else key_or_val
                sems[sname].wait(eng, v)

            if R != "sim":
                sems["qc"].wait(eng, base["qc"])
            if R == "gpsimd":
                eng.dma_start(out=gwf_sb[:, :],
                              in_=W["gwf"].rearrange("(c p) e -> p c e", p=128)
                              ).then_inc(qg.h, 16)
            if R == "sync":
                eng.dma_start(out=ident32[:, :], in_=W["ident32"]).then_inc(qs.h, 16)
            INC("qg", 16)
            INC("qs", 16)
            MK(("m_pre", "qg"), "qg")
            MK(("m_pre", "qs"), "qs")

            for hf in range(MOE_HALVES):
                if hf == 1:
                    WT("qc", ("m_st", 0))
                    WT("pe", ("m_peend", 0))
                elif R == "sync":
                    WT("qc", base["qc"])
                for tt in range(8):
                    if R == "sync":
                        eng.dma_start(out=oac[:, tt * 2048:(tt + 1) * 2048],
                                      in_=src[hf * 1024 + tt * 128:hf * 1024 + (tt + 1) * 128, :]
                                      ).then_inc(qs.h, 16)
                    INC("qs", 16)
                for kc in range(16):
                    if R == "sync":
                        eng.dma_start(out=xnh[:, kc * 1024:(kc + 1) * 1024],
                                      in_=pubT[kc * 128:(kc + 1) * 128,
                                               hf * 1024:(hf + 1) * 1024]).then_inc(qs.h, 16)
                    INC("qs", 16)
                MK(("m_ld", hf), "qs")

                if R == "tensor":
                    WT("qs", ("m_ld", hf))
                    WT("qg", ("m_pre", "qg"))
                    if hf == 0:
                        WT("act", base["act"])
                        WT("dve", base["dve"])
                    if hf == 1:
                        WT("dve", ("m_yev", hf - 1, MOE_EXPERTS - 1, 3))
                    for kc in range(16):
                        for tt in range(8):
                            ins = eng.matmul(ps[tt][:, 0:8],
                                             xnh[:, kc * 1024 + tt * 128:kc * 1024 + (tt + 1) * 128],
                                             gwf_sb[:, kc * 8:(kc + 1) * 8],
                                             start=(kc == 0), stop=(kc == 15))
                            if kc == 15:
                                ins.then_inc(pe.h, 1)
                for tt in range(8):
                    INC("pe", 1)
                    MK(("m_rpe", hf, tt), "pe")
                for tt in range(8):
                    sl8 = slice(tt * 8, (tt + 1) * 8)
                    c = lambda j: rst[:, tt * 6 + j:tt * 6 + j + 1]
                    if R == "scalar":
                        WT("pe", ("m_rpe", hf, tt))
                        eng.activation(lg[:, sl8], ps[tt][:, 0:8], AF.Copy).then_inc(act.h, 1)
                    INC("act", 1)
                    MK(("m_rev", hf, tt), "act")
                    if R == "vector":
                        WT("act", ("m_rev", hf, tt))
                        eng.reduce_max(c(0), lg[:, sl8], axis=AX.X).then_inc(dve.h, 1)
                        sems["dve"].wait(eng, C["dve"] + 1)
                        eng.tensor_scalar(c(1), c(0), -1.0, None, ALU.mult).then_inc(dve.h, 1)
                    INC("dve", 2)
                    MK(("m_rneg", hf, tt), "dve")
                    if R == "scalar":
                        WT("dve", ("m_rneg", hf, tt))
                        eng.activation(rb[:, sl8], lg[:, sl8], AF.Exp,
                                       bias=c(1)).then_inc(act.h, 1)
                    INC("act", 1)
                    MK(("m_rexp", hf, tt), "act")
                    if R == "vector":
                        WT("act", ("m_rexp", hf, tt))
                        eng.reduce_sum(c(2), rb[:, sl8], axis=AX.X).then_inc(dve.h, 1)
                        sems["dve"].wait(eng, C["dve"] + 1)
                        eng.reciprocal(c(3), c(2)).then_inc(dve.h, 1)
                        eng.reduce_max(c(4), rb[:, sl8], axis=AX.X).then_inc(dve.h, 1)
                        sems["dve"].wait(eng, C["dve"] + 3)
                        eng.tensor_scalar(rb[:, 64 + tt * 8:64 + (tt + 1) * 8], rb[:, sl8],
                                          c(4), None, ALU.subtract).then_inc(dve.h, 1)
                    INC("dve", 4)
                    MK(("m_rd", hf, tt), "dve")
                    if R == "scalar":
                        WT("dve", ("m_rd", hf, tt))
                        eng.activation(rb[:, 128 + tt * 8:128 + (tt + 1) * 8],
                                       rb[:, 64 + tt * 8:64 + (tt + 1) * 8],
                                       AF.Sign).then_inc(act.h, 1)
                    INC("act", 1)
                    MK(("m_rsg", hf, tt), "act")
                    if R == "vector":
                        WT("act", ("m_rsg", hf, tt))
                        eng.tensor_scalar(rb[:, 64 + tt * 8:64 + (tt + 1) * 8],
                                          rb[:, 128 + tt * 8:128 + (tt + 1) * 8], -1.0,
                                          None, ALU.mult).then_inc(dve.h, 1)
                        sems["dve"].wait(eng, C["dve"] + 1)
                        eng.tensor_mul(rb[:, 192 + tt * 8:192 + (tt + 1) * 8],
                                       rb[:, 64 + tt * 8:64 + (tt + 1) * 8],
                                       rb[:, sl8]).then_inc(dve.h, 1)
                        sems["dve"].wait(eng, C["dve"] + 2)
                        eng.reduce_max(c(5), rb[:, 192 + tt * 8:192 + (tt + 1) * 8],
                                       axis=AX.X).then_inc(dve.h, 1)
                        sems["dve"].wait(eng, C["dve"] + 3)
                        eng.tensor_scalar(rb[:, 64 + tt * 8:64 + (tt + 1) * 8], rb[:, sl8],
                                          c(5), None, ALU.is_ge).then_inc(dve.h, 1)
                        eng.tensor_scalar(rb[:, 192 + tt * 8:192 + (tt + 1) * 8], rb[:, sl8],
                                          c(3), None, ALU.mult).then_inc(dve.h, 1)
                        sems["dve"].wait(eng, C["dve"] + 5)
                        eng.tensor_mul(cwsb[:, sl8],
                                       rb[:, 192 + tt * 8:192 + (tt + 1) * 8],
                                       rb[:, 64 + tt * 8:64 + (tt + 1) * 8]).then_inc(dve.h, 1)
                    INC("dve", 6)
                    MK(("m_rcw", hf, tt), "dve")

                for e in range(MOE_EXPERTS):
                    shared = (e == 0)
                    sls = SH if shared else RT
                    nch = 22 if shared else 11
                    if shared:
                        w1a, w3a, w2a = W["sw1"], W["sw3"], W["sw2"]
                    else:
                        w1a, w3a, w2a = W["rw1"][e - 1], W["rw3"][e - 1], W["rw2"][e - 1]

                    for si, (f0, fw) in enumerate(sls):
                        nsub = fw // 128
                        for ph, wa in ((0, w1a), (1, w3a)):
                            wav = wa.rearrange("(c p) f -> p c f", p=128)
                            for g in range(4):
                                wti = C["wti"] = C.get("wti", 0) + 1
                                slot = (wti - 1) % 4
                                sn = f"wt{slot}"
                                if R == "gpsimd":
                                    if (("m_wfree", wti - 4) in marks) if R != "sim" else (wti > 4):
                                        WT("pe", ("m_wfree", wti - 4))
                                    wdst = bass.AP(wt, slot * 2048,
                                                   [[4 * 2048, 128], [512, 4], [1, fw]])
                                    eng.dma_start(
                                        out=wdst,
                                        in_=wav[:, g * 4:(g + 1) * 4, f0:f0 + fw]
                                    ).then_inc(sems[sn].h, 16)
                                INC(sn, 16)
                                MK(("m_wld", wti), sn)
                                if R == "tensor":
                                    WT(sn, ("m_wld", wti))
                                    if ph == 0 and g == 0:
                                        if e == 0 and si == 0:
                                            WT("act", ("m_rev", hf, 7))
                                            WT("dve", ("m_rcw", hf, 7))
                                        elif si == 0:
                                            WT("dve", ("m_yev", hf, e - 1, 3))
                                        else:
                                            WT("act", ("m_tev", hf, e, si - 1, 7))
                                    if ph == 1 and g == 0:
                                        WT("act", ("m_aev", hf, e, si, 7))
                                    for j in range(4):
                                        kc = g * 4 + j
                                        for tt in range(8):
                                            ins = eng.matmul(
                                                ps[tt][:, 0:fw],
                                                xnh[:, kc * 1024 + tt * 128:kc * 1024 + (tt + 1) * 128],
                                                wt[:, slot * 2048 + j * 512:slot * 2048 + j * 512 + fw],
                                                start=(kc == 0), stop=(kc == 15))
                                            if tt == 7 and j == 3:
                                                ins.then_inc(pe.h, 1)
                                INC("pe", 1)
                                MK(("m_wfree", wti), "pe")
                                if g == 3:
                                    MK(("m_gstop", hf, e, si, ph), "pe")
                            if ph == 0:
                                for tt in range(8):
                                    if R == "scalar":
                                        WT("pe", ("m_gstop", hf, e, si, 0))
                                        if si > 0 or e > 0:
                                            WT("dve", ("m_hev", hf, e, si - 1, 7) if si > 0
                                               else ("m_hev", hf, e - 1,
                                                     (len(SH) if e == 1 else len(RT)) - 1, 7))
                                        elif hf > 0:
                                            lsi = (len(SH) if MOE_EXPERTS == 1 else len(RT)) - 1
                                            WT("dve", ("m_hev", hf - 1, MOE_EXPERTS - 1, lsi, 7))
                                        eng.activation(s_sb[:, tt * 512:tt * 512 + fw],
                                                       ps[tt][:, 0:fw],
                                                       AF.Silu).then_inc(act.h, 1)
                                    INC("act", 1)
                                    MK(("m_aev", hf, e, si, tt), "act")
                        c0 = f0 // 128
                        for tt in range(8):
                            if R == "vector":
                                WT("pe", ("m_gstop", hf, e, si, 1))
                                if (hf, e, si, tt) != (0, 0, 0, tt):
                                    pass
                                if tt >= 2:
                                    WT("act", ("m_tev", hf, e, si, tt - 2))
                                if e == 0:
                                    eng.tensor_tensor(htmp[:, (tt % 2) * 512:(tt % 2) * 512 + fw],
                                                      ps[tt][:, 0:fw],
                                                      s_sb[:, tt * 512:tt * 512 + fw],
                                                      ALU.mult).then_inc(dve.h, 1)
                                else:
                                    eng.tensor_tensor(htA[:, (tt % 2) * 512:(tt % 2) * 512 + fw],
                                                      ps[tt][:, 0:fw],
                                                      s_sb[:, tt * 512:tt * 512 + fw],
                                                      ALU.mult)
                                    eng.tensor_scalar(
                                        htmp[:, (tt % 2) * 512:(tt % 2) * 512 + fw],
                                        htA[:, (tt % 2) * 512:(tt % 2) * 512 + fw],
                                        cwsb[:, tt * 8 + e - 1:tt * 8 + e], None,
                                        ALU.mult).then_inc(dve.h, 1)
                            INC("dve", 1)
                            MK(("m_hev", hf, e, si, tt), "dve")
                            if R == "tensor":
                                WT("dve", ("m_hev", hf, e, si, tt))
                                for j in range(nsub):
                                    ins = eng.transpose(
                                        ps[tt][:, j * 128:(j + 1) * 128],
                                        htmp[:, (tt % 2) * 512 + j * 128:(tt % 2) * 512 + (j + 1) * 128],
                                        ident32[:, :])
                                    if j == nsub - 1:
                                        ins.then_inc(pe.h, 1)
                            INC("pe", 1)
                            MK(("m_tpe", hf, e, si, tt), "pe")
                            if R == "scalar":
                                WT("pe", ("m_tpe", hf, e, si, tt))
                                dst_ap = bass.AP(h_fm, c0 * 1024 + tt * 128,
                                                 [[22 * 1024, 128], [1024, nsub], [1, 128]])
                                eng.activation(
                                    dst_ap,
                                    ps[tt][:, 0:nsub * 128].rearrange("p (s t) -> p s t", t=128),
                                    AF.Copy).then_inc(act.h, 1)
                            INC("act", 1)
                            MK(("m_tev", hf, e, si, tt), "act")

                    w2v = w2a.rearrange("(c p) d -> p c d", p=128)
                    ngr = (nch + 3) // 4
                    for ds in range(4):
                        for g in range(ngr):
                            gs = min(4, nch - 4 * g)
                            wti = C["wti"] = C.get("wti", 0) + 1
                            slot = (wti - 1) % 4
                            sn = f"wt{slot}"
                            if R == "gpsimd":
                                WT("pe", ("m_wfree", wti - 4))
                                wdst = bass.AP(wt, slot * 2048,
                                               [[4 * 2048, 128], [512, gs], [1, 512]])
                                eng.dma_start(out=wdst,
                                              in_=w2v[:, g * 4:g * 4 + gs,
                                                      ds * 512:(ds + 1) * 512]).then_inc(sems[sn].h, 16)
                            INC(sn, 16)
                            MK(("m_wld", wti), sn)
                            if R == "tensor":
                                WT(sn, ("m_wld", wti))
                                if g == 0:
                                    WT("act", ("m_tev", hf, e, len(sls) - 1, 7))
                                    if ds > 0:
                                        WT("dve", ("m_yev", hf, e, ds - 1, 7))
                                for j in range(gs):
                                    ic = g * 4 + j
                                    for tt in range(8):
                                        ins = eng.matmul(
                                            ps[tt][:, 0:512],
                                            h_fm[:, ic * 1024 + tt * 128:ic * 1024 + (tt + 1) * 128],
                                            wt[:, slot * 2048 + j * 512:slot * 2048 + (j + 1) * 512],
                                            start=(ic == 0), stop=(ic == nch - 1))
                                        if tt == 7 and j == gs - 1:
                                            ins.then_inc(pe.h, 1)
                            INC("pe", 1)
                            MK(("m_wfree", wti), "pe")
                            if g == ngr - 1:
                                MK(("m_ygstop", hf, e, ds), "pe")
                        for tt in range(8):
                            if R == "vector":
                                WT("pe", ("m_ygstop", hf, e, ds))
                                if e == 0 and ds == 0 and tt == 0:
                                    WT("qs", ("m_ld", hf))
                                eng.tensor_tensor(
                                    oac[:, tt * 2048 + ds * 512:tt * 2048 + (ds + 1) * 512],
                                    ps[tt][:, 0:512],
                                    oac[:, tt * 2048 + ds * 512:tt * 2048 + (ds + 1) * 512],
                                    ALU.add).then_inc(dve.h, 1)
                            INC("dve", 1)
                            if tt in (3, 7):
                                MK(("m_yev", hf, e, ds, tt), "dve")
                        MK(("m_yev", hf, e, ds), "dve")
                MK(("m_peend", hf), "pe")

                for tt in range(8):
                    if R == "scalar":
                        WT("dve", ("m_yev", hf, MOE_EXPERTS - 1, 3))
                        WT("qs", ("m_ld", hf))
                        eng.dma_start(out=dst[hf * 1024 + tt * 128:hf * 1024 + (tt + 1) * 128, :],
                                      in_=oac[:, tt * 2048:(tt + 1) * 2048]).then_inc(qc.h, 16)
                    INC("qc", 16)
                if MOE_DBG:
                    if R == "gpsimd":
                        WT("dve", ("m_rcw", hf, 7))
                        eng.dma_start(out=pubT[0:128, 1900 + hf * 64:1900 + (hf + 1) * 64],
                                      in_=cwsb[:, :]).then_inc(qg.h, 16)
                        eng.dma_start(out=pubT[128:256, 1900 + hf * 64:1900 + (hf + 1) * 64],
                                      in_=lg[:, :]).then_inc(qg.h, 16)
                        eng.dma_start(out=pubT[256:384, 1900 + hf * 64:1900 + (hf + 1) * 64],
                                      in_=rb[:, 0:64]).then_inc(qg.h, 16)
                    INC("qg", 32)
                MK(("m_st", hf), "qc")

            if MOE_DBG2:
                if R == "gpsimd":
                    WT("dve", ("m_yev", MOE_HALVES - 1, MOE_EXPERTS - 1, 3))
                    for i in range(11):
                        eng.dma_start(out=pubT[i * 128:(i + 1) * 128, :],
                                      in_=h_fm[:, i * 2048:(i + 1) * 2048]).then_inc(qg.h, 16)
                    for i in range(2):
                        eng.dma_start(out=pubT[(11 + i) * 128:(12 + i) * 128, :],
                                      in_=s_sb[:, i * 2048:(i + 1) * 2048]).then_inc(qg.h, 16)
                    for i in range(2):
                        eng.dma_start(out=pubT[(13 + i) * 128:(14 + i) * 128, :],
                                      in_=xnh[:, i * 2048:(i + 1) * 2048]).then_inc(qg.h, 16)
                INC("qg", 16 * 15)
            if R == "sim":
                for k in sems:
                    marks[("m_end", k)] = C[k]

        body("sim")
        with nc.Block() as blk:
            @blk.sync
            def _(sy):
                body("sync", sy)

            @blk.gpsimd
            def _(g):
                body("gpsimd", g)

            @blk.tensor
            def _(te):
                body("tensor", te)

            @blk.scalar
            def _(sc):
                body("scalar", sc)

            @blk.vector
            def _(v):
                body("vector", v)

        for k in sems:
            sems[k].n = marks[("m_end", k)]


def _tern_prepass(nc, W, sems, ones_c, ones_r, pcol, prow):
    import contextlib
    qc = sems["qc"]
    pe, act, dve = sems["pe"], sems["act"], sems["dve"]

    pieces = []
    for kc in range(16):
        for half in range(2):
            pieces.append(("wfcT", kc * 128, half * 2560, 2560))
    for kc in range(40):
        pieces.append(("wpjT", kc * 128, 0, 2048))
    NP_ = len(pieces)
    NEL = float(D * FT)

    with contextlib.ExitStack() as st:
        tb = st.enter_context(nc.sbuf_tensor("T_tb", [128, 2 * 2560], F32))
        stf = st.enter_context(nc.sbuf_tensor("T_stf", [128, 80], F32))
        p1 = st.enter_context(nc.psum_tensor("T_p1", [1, 1], F32))
        pb = st.enter_context(nc.psum_tensor("T_pb", [128, 1], F32))
        qc0, dve0, act0, pe0 = qc.n, dve.n, act.n, pe.n
        tw0, tw1 = sems["wt0"].n, sems["wt1"].n

        with nc.Block() as blk:
            @blk.scalar
            def _(sc):
                qc.wait(sc, qc0)
                for i, (nm, r0, c0, cw) in enumerate(pieces):
                    if i >= 2:
                        dve.wait(sc, dve0 + i - 1)
                    ws = sems["wt" + str(i % 2)]
                    sc.dma_start(out=tb[:, (i % 2) * 2560:(i % 2) * 2560 + cw],
                                 in_=W[nm][r0:r0 + 128, c0:c0 + cw]).then_inc(ws.h, 16)
                    ws.n += 16
                pe.wait(sc, pe0 + 1)
                sc.activation(prow[0:1, 130:131], p1[:, :], AF.Copy,
                              scale=1.0 / NEL).then_inc(act.h, 1)
                dve.wait(sc, dve0 + NP_ + 3)
                sc.mul(prow[0:1, 132:133], prow[0:1, 131:132], 0.7).then_inc(act.h, 1)
                pe.wait(sc, pe0 + 2)
                sc.activation(prow[0:1, 140:141], p1[:, :], AF.Copy,
                              scale=1.0 / NEL).then_inc(act.h, 1)
                dve.wait(sc, dve0 + NP_ + 4)
                sc.mul(prow[0:1, 142:143], prow[0:1, 141:142], 0.7).then_inc(act.h, 1)
                sc.activation(prow[0:1, 143:144], prow[0:1, 131:132],
                              AF.Square).then_inc(act.h, 1)
                act.n += 5
                for j, (col, dst0) in enumerate([(132, 0), (142, 2), (144, 4)]):
                    pe.wait(sc, pe0 + 3 + j)
                    sc.activation(pcol[:, dst0:dst0 + 1], pb[:, :], AF.Copy).then_inc(act.h, 1)
                    act.n += 1
                    if dst0 < 4:
                        sc.activation(pcol[:, dst0 + 1:dst0 + 2], pb[:, :], AF.Copy,
                                      scale=-1.0).then_inc(act.h, 1)
                        act.n += 1

            @blk.vector
            def _(v):
                qc.wait(v, qc0)
                for i in range(NP_):
                    cw = pieces[i][3]
                    ws = sems["wt" + str(i % 2)]
                    ws.wait(v, (tw0 if i % 2 == 0 else tw1) + 16 * (i // 2 + 1))
                    v.tensor_reduce(stf[:, i:i + 1],
                                    tb[:, (i % 2) * 2560:(i % 2) * 2560 + cw],
                                    AX.X, ALU.add,
                                    apply_absolute_value=True).then_inc(dve.h, 1)
                    dve.n += 1
                v.tensor_reduce(stf[:, 72:73], stf[:, 0:32], AX.X, ALU.add).then_inc(dve.h, 1)
                v.tensor_reduce(stf[:, 73:74], stf[:, 32:72], AX.X, ALU.add).then_inc(dve.h, 1)
                dve.n += 2
                act.wait(v, act0 + 1)
                v.tensor_scalar(prow[0:1, 131:132], prow[0:1, 130:131], 1e-5, None,
                                ALU.max).then_inc(dve.h, 1)
                act.wait(v, act0 + 3)
                v.tensor_scalar(prow[0:1, 141:142], prow[0:1, 140:141], 1e-5, None,
                                ALU.max).then_inc(dve.h, 1)
                act.wait(v, act0 + 5)
                v.tensor_mul(prow[0:1, 144:145], prow[0:1, 143:144],
                             prow[0:1, 141:142]).then_inc(dve.h, 1)
                dve.n += 3

            @blk.tensor
            def _(te):
                act.wait(te, act0)
                dve.wait(te, dve0)
                dve.wait(te, dve0 + NP_ + 2)
                te.matmul(p1[:, :], ones_c[:, :], stf[:, 72:73], start=True,
                          stop=True).then_inc(pe.h, 1)
                act.wait(te, act0 + 1)
                te.matmul(p1[:, :], ones_c[:, :], stf[:, 73:74], start=True,
                          stop=True).then_inc(pe.h, 1)
                pe.n += 2
                waits = [(act0 + 2, dve0 + NP_ + 3), (act0 + 5, dve0 + NP_ + 4),
                         (act0 + 5, dve0 + NP_ + 5)]
                for j, (col, (aw, dw)) in enumerate(zip((132, 142, 144), waits)):
                    act.wait(te, aw)
                    dve.wait(te, dw)
                    if j > 0:
                        act.wait(te, act0 + 5 + 2 * j)
                    te.matmul(pb[:, :], ones_r[:, :], prow[0:1, col:col + 1],
                              start=True, stop=True).then_inc(pe.h, 1)
                    pe.n += 1


def _tern_quant(nc, W, sems, pcol, qfcT, qpjT):
    import contextlib
    qc = sems["qc"]
    pool, dve = sems["pool"], sems["dve"]

    pieces = []
    for kc in range(16):
        for half in range(2):
            pieces.append((W["wfcT"], qfcT, kc * 128, half * 2560, 2560, 0))
    for kc in range(40):
        pieces.append((W["wpjT"], qpjT, kc * 128, 0, 2048, 2))
    NP_ = len(pieces)

    with contextlib.ExitStack() as st:
        tb = st.enter_context(nc.sbuf_tensor("Q_tb", [128, 2 * 2560], F32))
        gl = st.enter_context(nc.sbuf_tensor("Q_gl", [128, 4 * 2560], BF16))
        qt_ = st.enter_context(nc.sbuf_tensor("Q_qt", [128, 2 * 2560], BF16))
        qc0, pool0, dve0 = qc.n, pool.n, dve.n
        qw2, qw3 = sems["wt2"].n, sems["wt3"].n

        with nc.Block() as blk:
            @blk.scalar
            def _(sc):
                qc.wait(sc, qc0)
                for i, (srcT, dstT, r0, c0, cw, tc) in enumerate(pieces):
                    if i >= 2:
                        pool.wait(sc, pool0 + (i - 1))
                    ws = sems["wt" + str(2 + i % 2)]
                    sc.dma_start(out=tb[:, (i % 2) * 2560:(i % 2) * 2560 + cw],
                                 in_=srcT[r0:r0 + 128, c0:c0 + cw]).then_inc(ws.h, 16)
                    ws.n += 16
                    dve.wait(sc, dve0 + (i + 1))
                    sc.dma_start(out=dstT[r0:r0 + 128, c0:c0 + cw],
                                 in_=qt_[:, (i % 2) * 2560:(i % 2) * 2560 + cw]
                                 ).then_inc(qc.h, 16)
                    qc.n += 16

            @blk.gpsimd
            def _(g):
                qc.wait(g, qc0)
                for i, (srcT, dstT, r0, c0, cw, tc) in enumerate(pieces):
                    ws = sems["wt" + str(2 + i % 2)]
                    ws.wait(g, (qw2 if i % 2 == 0 else qw3) + 16 * (i // 2 + 1))
                    if i >= 2:
                        dve.wait(g, dve0 + (i - 1))
                    sl = slice((i % 2) * 2560, (i % 2) * 2560 + cw)
                    g.tensor_scalar(gl[:, (i % 2) * 2560:(i % 2) * 2560 + cw],
                                    tb[:, sl], pcol[:, tc:tc + 1], None, ALU.is_gt)
                    g.tensor_scalar(gl[:, 5120 + (i % 2) * 2560:5120 + (i % 2) * 2560 + cw],
                                    tb[:, sl], pcol[:, tc + 1:tc + 2], None,
                                    ALU.is_lt).then_inc(pool.h, 1)
                    pool.n += 1

            @blk.vector
            def _(v):
                for i, (srcT, dstT, r0, c0, cw, tc) in enumerate(pieces):
                    pool.wait(v, pool0 + (i + 1))
                    if i >= 1:
                        qc.wait(v, qc0 + 16 * i)
                    v.tensor_sub(qt_[:, (i % 2) * 2560:(i % 2) * 2560 + cw],
                                 gl[:, (i % 2) * 2560:(i % 2) * 2560 + cw],
                                 gl[:, 5120 + (i % 2) * 2560:5120 + (i % 2) * 2560 + cw]
                                 ).then_inc(dve.h, 1)
                    dve.n += 1


def _phase_attn(nc, lay, gatT, pubaT, gataT, W, sems, ident16, ones_c16,
                ones_r, pcol, prope_s, cc_idx):
    import contextlib
    qs, qc, qg, cc = sems["qs"], sems["qc"], sems["qg"], sems["cc"]
    pe, act, dve = sems["pe"], sems["act"], sems["dve"]
    base = {k: s.n for k, s in sems.items()}
    marks = {}
    L1 = (lay == 1)
    PR = PUB1 if L1 else PUB0
    S0 = 1.0 / np.sqrt(128.0)

    with contextlib.ExitStack() as st:
        wsl = st.enter_context(nc.sbuf_tensor(f"P{lay}_wsl", [128, 48 * 128], BF16))
        gwt = st.enter_context(nc.sbuf_tensor(f"P{lay}_gwt", [128, 128], BF16))
        xnb = st.enter_context(nc.sbuf_tensor(f"P{lay}_xnb", [128, 16 * 512], BF16))
        kh_a = st.enter_context(nc.sbuf_tensor(f"P{lay}_kha", [128, S], BF16))
        qh_a = st.enter_context(nc.sbuf_tensor(f"P{lay}_qha", [128, S], BF16))
        vT_h = st.enter_context(nc.sbuf_tensor(f"P{lay}_vth", [128, S], BF16))
        vtmp = st.enter_context(nc.sbuf_tensor(f"P{lay}_vtmp", [128, 512], BF16))
        probsT = st.enter_context(nc.sbuf_tensor(f"P{lay}_pT", [128, 2 * 512], BF16))
        ptmp = st.enter_context(nc.sbuf_tensor(f"P{lay}_ptmp", [128, 512], BF16))
        rs_sb = st.enter_context(nc.sbuf_tensor(f"P{lay}_rs", [1, 512], F32))
        rb_sb = st.enter_context(nc.sbuf_tensor(f"P{lay}_rb", [128, 512], F32))
        fac = st.enter_context(nc.sbuf_tensor(f"P{lay}_fac", [128, 512], F32))
        av_sb = st.enter_context(nc.sbuf_tensor(f"P{lay}_av", [128, 2 * 512], BF16))
        if L1:
            kh_b = st.enter_context(nc.sbuf_tensor("P1_khb", [128, S], BF16))
            qh_b = st.enter_context(nc.sbuf_tensor("P1_qhb", [128, S], BF16))
            ckvb = st.enter_context(nc.sbuf_tensor("P1_ckvb", [128, 4 * 512], BF16))
            cqb = st.enter_context(nc.sbuf_tensor("P1_cqb", [128, 8 * 512], BF16))
            tabb = st.enter_context(nc.sbuf_tensor("P1_tab", [128, 4 * 512], F32))
            kraw = st.enter_context(nc.sbuf_tensor("P1_kraw", [128, 512], BF16))
            tmp1 = st.enter_context(nc.sbuf_tensor("P1_t1", [128, 512], F32))
            tmp2 = st.enter_context(nc.sbuf_tensor("P1_t2", [128, 512], F32))
        else:
            gate_sb = st.enter_context(nc.sbuf_tensor("P0_gate", [128, 512], F32))
        P = [st.enter_context(nc.psum_tensor(f"P{lay}_P{i}", [128, 512], F32))
             for i in range(5)]
        PT = st.enter_context(nc.psum_tensor(f"P{lay}_PT", [128, 512], BF16))
        PS = st.enter_context(nc.psum_tensor(f"P{lay}_PS", [1, 512], F32))
        MB = st.enter_context(nc.sbuf_tensor(f"P{lay}_mb", [128, 4 * 512], BF16))

        if L1:
            WTAB = [("wukT", 4, 0, "ckvb"), ("wuvT", 4, 4, "ckvb"),
                    ("wuqT", 8, 8, "cqb"), ("wkrT", 16, 16, "xnb")]
        else:
            WTAB = [("wqT", 16, 0, "xnb"), ("wkT", 16, 16, "xnb"),
                    ("wvT", 16, 32, "xnb")]

        def body(R, eng=None):
            C = dict(base)

            def INC(sname, k, ins=None):
                C[sname] += k
                if ins is not None:
                    ins.then_inc(sems[sname].h, k)
                return C[sname]

            def MK(key, sname):
                if R == "sim":
                    marks[key] = C[sname]

            def WT(sname, key_or_val):
                if R == "sim":
                    return
                v = marks[key_or_val] if isinstance(key_or_val, tuple) else key_or_val
                sems[sname].wait(eng, v)

            if R != "sim":
                sems["qc"].wait(eng, base["qc"])
            if R == "sync":
                eng.dma_start(out=MB[:, :], in_=W["masks"]).then_inc(qs.h, 16)
            INC("qs", 16)
            MK(("p_mb",), "qs")

            for h in range(HPC):
                if h > 0:
                    WT("pe", ("p_hd_pe", h - 1))
                for mi, (nm, nkc, tb0, _r) in enumerate(WTAB):
                    for kc in range(nkc):
                        if R == "gpsimd":
                            eng.dma_start(
                                out=wsl[:, (tb0 + kc) * 128:(tb0 + kc + 1) * 128],
                                in_=W[nm][kc * 128:(kc + 1) * 128, h * 128:(h + 1) * 128]
                            ).then_inc(qg.h, 16)
                        INC("qg", 16)
                if lay == 0 and h == 0:
                    if R == "gpsimd":
                        eng.dma_start(out=gwt[:, :], in_=W["gwT"]).then_inc(qg.h, 16)
                    INC("qg", 16)
                MK(("p_wsl", h), "qg")

                for ts in range(16):
                    r, lc = ts // 4, (ts % 4) * 512
                    if R == "sync":
                        if h == 0 and ts == 0:
                            WT("cc", cc_idx)
                        if not (h == 0 and ts == 0):
                            WT("pe", ("p_pj", h, ts - 1) if ts > 0 else ("p_pj", h - 1, 15))
                            if L1:
                                WT("dve", ("p_rpq", h, ts - 1) if ts > 0 else ("p_rpq", h - 1, 15))
                        for kc in range(16):
                            eng.dma_start(out=xnb[:, kc * 512:(kc + 1) * 512],
                                          in_=gatT[r * PR + kc * 128:r * PR + (kc + 1) * 128,
                                                   lc:lc + 512]).then_inc(qs.h, 16)
                        if L1:
                            for i in range(4):
                                eng.dma_start(out=ckvb[:, i * 512:(i + 1) * 512],
                                              in_=gatT[r * PR + D + i * 128:r * PR + D + (i + 1) * 128,
                                                       lc:lc + 512]).then_inc(qs.h, 16)
                            for i in range(8):
                                eng.dma_start(out=cqb[:, i * 512:(i + 1) * 512],
                                              in_=gatT[r * PR + D + DC + i * 128:
                                                       r * PR + D + DC + (i + 1) * 128,
                                                       lc:lc + 512]).then_inc(qs.h, 16)
                            for ti, tn in enumerate(("cosq", "sinq", "cosk", "sink")):
                                eng.dma_start(out=tabb[:, ti * 512:(ti + 1) * 512],
                                              in_=W[tn][:, ts * 512:(ts + 1) * 512]
                                              ).then_inc(qs.h, 16)
                    INC("qs", 16 * (16 + (16 if L1 else 0)))
                    MK(("p_ld", h, ts), "qs")

                    if R == "tensor":
                        WT("qs", ("p_ld", h, ts))
                        if h == 0 and ts == 0:
                            WT("qg", ("p_wsl", 0))
                            WT("act", base["act"])
                            WT("dve", base["dve"])
                        elif ts == 0:
                            WT("qg", ("p_wsl", h))
                            WT("act", ("p_hd_act", h - 1))
                            WT("dve", ("p_hd_dve", h - 1))
                        if not (h == 0 and ts == 0):
                            WT("act", ("p_pjev", h, ts - 1) if ts > 0 else ("p_pjev", h - 1, 15))
                        if L1:
                            plan = [(0, 0, 4, "ckvb"), (2, 4, 4, "ckvb"),
                                    (3, 8, 8, "cqb"), (1, 16, 16, "xnb")]
                        else:
                            plan = [(0, 0, 16, "xnb"), (1, 16, 16, "xnb"), (2, 32, 16, "xnb")]
                        rhs_map = {"xnb": xnb, "ckvb": (ckvb if L1 else None),
                                   "cqb": (cqb if L1 else None)}
                        for pi, tb0, nkc, rn in plan:
                            rbuf = rhs_map[rn]
                            for kc in range(nkc):
                                ins = eng.matmul(P[pi][:, :],
                                                 wsl[:, (tb0 + kc) * 128:(tb0 + kc + 1) * 128],
                                                 rbuf[:, kc * 512:(kc + 1) * 512],
                                                 start=(kc == 0), stop=(kc == nkc - 1))
                            ins.then_inc(pe.h, 1)
                        INC("pe", len(plan))
                    else:
                        INC("pe", 4 if L1 else 3)
                    MK(("p_pj", h, ts), "pe")

                    if R == "scalar":
                        WT("pe", ("p_pj", h, ts))
                        if L1 and not (h == 0 and ts == 0):
                            WT("dve", ("p_rpk", h, ts - 1) if ts > 0 else ("p_rpk", h - 1, 15))
                        if L1:
                            eng.activation(kh_a[:, ts * 512:(ts + 1) * 512], P[0][:, :],
                                           AF.Identity, bias=pcol[:, 18 + h:19 + h],
                                           scale=1.0 / 16.0).then_inc(act.h, 1)
                            eng.activation(vtmp[:, :], P[2][:, :], AF.Identity,
                                           bias=pcol[:, 22 + h:23 + h]).then_inc(act.h, 1)
                            eng.activation(qh_a[:, ts * 512:(ts + 1) * 512], P[3][:, :],
                                           AF.Identity, bias=pcol[:, 26 + h:27 + h]
                                           ).then_inc(act.h, 1)
                            eng.activation(kraw[:, :], P[1][:, :], AF.Identity,
                                           bias=pcol[:, 30 + h:31 + h]).then_inc(act.h, 1)
                        else:
                            eng.activation(qh_a[:, ts * 512:(ts + 1) * 512], P[0][:, :],
                                           AF.Copy).then_inc(act.h, 1)
                            eng.activation(kh_a[:, ts * 512:(ts + 1) * 512], P[1][:, :],
                                           AF.Copy, scale=S0).then_inc(act.h, 1)
                            eng.activation(vtmp[:, :], P[2][:, :], AF.Copy).then_inc(act.h, 1)
                    INC("act", 4 if L1 else 3)
                    MK(("p_pjev", h, ts), "act")

                    if R == "tensor":
                        WT("act", ("p_pjev", h, ts))
                        if not (h == 0 and ts == 0):
                            WT("act", ("p_vtev", h, ts - 1) if ts > 0 else ("p_vtev", h - 1, 15))
                        for j in range(4):
                            ins = eng.transpose(PT[:, j * 128:(j + 1) * 128],
                                                vtmp[:, j * 128:(j + 1) * 128],
                                                ident16[:, :])
                        ins.then_inc(pe.h, 1)
                    INC("pe", 1)
                    MK(("p_vt", h, ts), "pe")
                    if R == "scalar":
                        WT("pe", ("p_vt", h, ts))
                        eng.activation(vT_h[:, ts * 512:(ts + 1) * 512], PT[:, :],
                                       AF.Copy).then_inc(act.h, 1)
                    INC("act", 1)
                    MK(("p_vtev", h, ts), "act")

                    if L1:
                        if R == "tensor":
                            WT("act", ("p_pjev", h, ts))
                            if not (h == 0 and ts == 0):
                                WT("dve", ("p_rpq", h, ts - 1) if ts > 0 else ("p_rpq", h - 1, 15))
                            eng.matmul(P[4][:, :], prope_s[:, :], kraw[:, :],
                                       start=True, stop=True).then_inc(pe.h, 1)
                        INC("pe", 1)
                        MK(("p_rotk", h, ts), "pe")
                        if R == "vector":
                            WT("act", ("p_pjev", h, ts))
                            WT("qs", ("p_ld", h, ts))
                            eng.tensor_mul(tmp1[:, :], kraw[:, :], tabb[:, 2 * 512:3 * 512])
                            WT("pe", ("p_rotk", h, ts))
                            eng.tensor_mul(tmp2[:, :], P[4][:, :], tabb[:, 3 * 512:4 * 512])
                            eng.tensor_add(kh_b[:, ts * 512:(ts + 1) * 512], tmp1[:, :],
                                           tmp2[:, :]).then_inc(dve.h, 1)
                        INC("dve", 1)
                        MK(("p_rpk", h, ts), "dve")
                        if R == "tensor":
                            WT("dve", ("p_rpk", h, ts))
                            eng.matmul(P[4][:, :], prope_s[:, :],
                                       qh_a[:, ts * 512:(ts + 1) * 512],
                                       start=True, stop=True).then_inc(pe.h, 1)
                        INC("pe", 1)
                        MK(("p_rotq", h, ts), "pe")
                        if R == "vector":
                            eng.tensor_mul(tmp1[:, :], qh_a[:, ts * 512:(ts + 1) * 512],
                                           tabb[:, 0:512])
                            WT("pe", ("p_rotq", h, ts))
                            eng.tensor_mul(tmp2[:, :], P[4][:, :], tabb[:, 512:1024])
                            eng.tensor_add(qh_b[:, ts * 512:(ts + 1) * 512], tmp1[:, :],
                                           tmp2[:, :]).then_inc(dve.h, 1)
                        INC("dve", 1)
                        MK(("p_rpq", h, ts), "dve")

                kidx = 0
                for qt in range(16):
                    nkt = 4 * (qt + 1)

                    def emit_scores(kt):
                        ki = kidx + kt
                        if R == "tensor":
                            if ki >= 2:
                                WT("act", ("a_ex", h, ki - 2))
                            if qt == 0 and kt == 0:
                                WT("act", ("p_vtev", h, 15))
                                if L1:
                                    WT("dve", ("p_rpq", h, 15))
                            if L1:
                                eng.matmul(P[ki % 2][:, :],
                                           kh_a[:, kt * 128:(kt + 1) * 128],
                                           qh_a[:, qt * 512:(qt + 1) * 512],
                                           start=True, stop=False)
                                eng.matmul(P[ki % 2][:, :],
                                           kh_b[:, kt * 128:(kt + 1) * 128],
                                           qh_b[:, qt * 512:(qt + 1) * 512],
                                           start=False, stop=True).then_inc(pe.h, 1)
                            else:
                                eng.matmul(P[ki % 2][:, :],
                                           kh_a[:, kt * 128:(kt + 1) * 128],
                                           qh_a[:, qt * 512:(qt + 1) * 512],
                                           start=True, stop=True).then_inc(pe.h, 1)
                        INC("pe", 1)
                        MK(("a_sc", h, ki), "pe")

                    def emit_exp(kt):
                        ki = kidx + kt
                        diag = kt >= 4 * qt
                        j = kt - 4 * qt
                        if R == "scalar":
                            WT("pe", ("a_sc", h, ki))
                            if ki >= 2:
                                WT("pe", ("a_pv", h, ki - 2))
                            if diag:
                                prev = marks.get(("a_dprev", h, ki))
                                if prev is not None:
                                    WT("dve", prev)
                                eng.activation(ptmp[:, :], P[ki % 2][:, :],
                                               AF.Exp).then_inc(act.h, 1)
                            else:
                                eng.activation(probsT[:, (ki % 2) * 512:(ki % 2 + 1) * 512],
                                               P[ki % 2][:, :], AF.Exp).then_inc(act.h, 1)
                        INC("act", 1)
                        MK(("a_ex", h, ki), "act")
                        if diag:
                            if R == "vector":
                                WT("act", ("a_ex", h, ki))
                                eng.tensor_mul(probsT[:, (ki % 2) * 512:(ki % 2 + 1) * 512],
                                               ptmp[:, :],
                                               MB[:, j * 512:(j + 1) * 512]).then_inc(dve.h, 1)
                            INC("dve", 1)
                            MK(("a_mk", h, ki), "dve")
                            if R == "sim":
                                marks[("a_dprev", h, ki + 1 if kt < nkt - 1 else ki + 4 * qt + 5)] = C["dve"]

                    def emit_av(kt):
                        ki = kidx + kt
                        diag = kt >= 4 * qt
                        if R == "tensor":
                            if diag:
                                WT("dve", ("a_mk", h, ki))
                            else:
                                WT("act", ("a_ex", h, ki))
                            if kt == 0:
                                if (h, qt) != (0, 0):
                                    WT("dve", ("a_avm", h, qt - 1) if qt > 0 else ("a_avm", h - 1, 15))
                                    WT("dve", ("a_rc", h, qt - 1) if qt > 0 else ("a_rc", h - 1, 15))
                            eng.matmul(P[2][:, :], vT_h[:, kt * 128:(kt + 1) * 128],
                                       probsT[:, (ki % 2) * 512:(ki % 2 + 1) * 512],
                                       start=(kt == 0), stop=(kt == nkt - 1))
                            ins = eng.matmul(PS[:, :], ones_c16[:, :],
                                             probsT[:, (ki % 2) * 512:(ki % 2 + 1) * 512],
                                             start=(kt == 0), stop=(kt == nkt - 1))
                            ins.then_inc(pe.h, 1)
                        INC("pe", 1)
                        MK(("a_pv", h, ki), "pe")

                    for kt in range(nkt):
                        emit_scores(kt)
                        emit_exp(kt)
                        if kt > 0:
                            emit_av(kt - 1)
                    emit_av(nkt - 1)
                    kidx += nkt
                    if R == "vector":
                        WT("pe", ("a_pv", h, kidx - 1))
                        eng.reciprocal(rs_sb[:, :], PS[:, :]).then_inc(dve.h, 1)
                    INC("dve", 1)
                    MK(("a_rc", h, qt), "dve")
                    if R == "tensor":
                        WT("dve", ("a_rc", h, qt))
                        if (h, qt) != (0, 0):
                            WT("act", ("a_rbev", h, qt - 1) if qt > 0 else ("a_rbev", h - 1, 15))
                        eng.matmul(P[3][:, :], ones_r[:, :], rs_sb[:, :],
                                   start=True, stop=True).then_inc(pe.h, 1)
                        if lay == 0:
                            if (h, qt) != (0, 0):
                                WT("act", ("a_ggev", h, qt - 1) if qt > 0 else ("a_ggev", h - 1, 15))
                            eng.matmul(P[4][:, :], gwt[:, :],
                                       qh_a[:, qt * 512:(qt + 1) * 512],
                                       start=True, stop=True).then_inc(pe.h, 1)
                    INC("pe", 2 if lay == 0 else 1)
                    MK(("a_rb", h, qt), "pe")
                    if R == "scalar":
                        WT("pe", ("a_rb", h, qt))
                        eng.activation(rb_sb[:, :], P[3][:, :], AF.Copy).then_inc(act.h, 1)
                    INC("act", 1)
                    MK(("a_rbev", h, qt), "act")
                    if lay == 0:
                        if R == "scalar":
                            eng.activation(gate_sb[:, :], P[4][:, :], AF.Sigmoid,
                                           bias=pcol[:, 5:6]).then_inc(act.h, 1)
                        INC("act", 1)
                        MK(("a_ggev", h, qt), "act")
                    if R == "vector":
                        WT("act", ("a_rbev", h, qt))
                        if lay == 0:
                            WT("act", ("a_ggev", h, qt))
                            eng.tensor_mul(fac[:, :], gate_sb[:, :], rb_sb[:, :])
                            srcf = fac
                        else:
                            srcf = rb_sb
                        if not (h == 0 and qt < 2):
                            WT("qc", ("a_st", h, qt - 2) if qt >= 2 else ("a_st", h - 1, 14 + qt))
                        eng.tensor_mul(av_sb[:, (qt % 2) * 512:(qt % 2 + 1) * 512],
                                       P[2][:, :], srcf[:, :]).then_inc(dve.h, 1)
                    INC("dve", 1)
                    MK(("a_avm", h, qt), "dve")
                    if R == "scalar":
                        WT("dve", ("a_avm", h, qt))
                        eng.dma_start(out=pubaT[h * 128:(h + 1) * 128,
                                                qt * 512:(qt + 1) * 512],
                                      in_=av_sb[:, (qt % 2) * 512:(qt % 2 + 1) * 512]
                                      ).then_inc(qc.h, 16)
                    INC("qc", 16)
                    MK(("a_st", h, qt), "qc")
                MK(("p_hd_pe", h), "pe")
                MK(("p_hd_act", h), "act")
                MK(("p_hd_dve", h), "dve")

            if R == "gpsimd":
                WT("qc", ("a_st", HPC - 1, 15))
                if ATTN_CC:
                    eng.collective_compute(
                        "AllGather", ALU.bypass, replica_groups=RGROUPS,
                        ins=[pubaT.opt()], outs=[gataT.opt()]).then_inc(cc.h, 1)
                else:
                    eng.engine_nop().then_inc(cc.h, 1)
            INC("cc", 1)
            if R == "sim":
                for k in sems:
                    marks[("p_end", k)] = C[k]

        body("sim")
        with nc.Block() as blk:
            @blk.sync
            def _(sy):
                body("sync", sy)

            @blk.gpsimd
            def _(g):
                body("gpsimd", g)

            @blk.tensor
            def _(te):
                body("tensor", te)

            @blk.scalar
            def _(sc):
                body("scalar", sc)

            @blk.vector
            def _(v):
                body("vector", v)

        for k in sems:
            sems[k].n = marks[("p_end", k)]


def _phase_oproj(nc, gataT, srcT, dstT, woTap, sems, cc_idx, tag=0):
    import contextlib
    qs, qc, qg = sems["qs"], sems["qc"], sems["qg"]
    pe, act, dve = sems["pe"], sems["act"], sems["dve"]
    base = {k: s.n for k, s in sems.items()}
    marks = {}

    with contextlib.ExitStack() as st:
        woT_sb = st.enter_context(nc.sbuf_tensor(f"O_wo{tag}", [128, 16 * 2048], BF16))
        avb = st.enter_context(nc.sbuf_tensor(f"O_avb{tag}", [128, 16 * 1024], BF16))
        x1b = st.enter_context(nc.sbuf_tensor(f"O_x1b{tag}", [128, 2 * 2048], F32))
        outb = st.enter_context(nc.sbuf_tensor(f"O_outb{tag}", [128, 2 * 2048], F32))
        po = [st.enter_context(nc.psum_tensor(f"O_po{tag}_{i}", [128, 512], F32))
              for i in range(4)]

        def body(R, eng=None):
            C = dict(base)

            def INC(sname, k, ins=None):
                C[sname] += k
                if ins is not None:
                    ins.then_inc(sems[sname].h, k)
                return C[sname]

            def MK(key, sname):
                if R == "sim":
                    marks[key] = C[sname]

            def WT(sname, key_or_val):
                if R == "sim":
                    return
                v = marks[key_or_val] if isinstance(key_or_val, tuple) else key_or_val
                sems[sname].wait(eng, v)

            if R != "sim":
                sems["qc"].wait(eng, base["qc"])
            if R == "gpsimd":
                for kc in range(16):
                    eng.dma_start(out=woT_sb[:, kc * 2048:(kc + 1) * 2048],
                                  in_=woTap[kc * 128:(kc + 1) * 128, :]).then_inc(qg.h, 16)
            INC("qg", 16 * 16)
            MK(("o_wo",), "qg")

            if R == "gpsimd":
                own0 = (nc.partition_id() % 4) * 2048
                offs = [own0, own0 + 1024]
                gv = gataT.rearrange("(c p) t -> p c t", p=128)
                av = avb[:, :].rearrange("p (c t) -> p c t", t=1024)

            for tg in range(2):
                if R == "gpsimd":
                    if tg == 0:
                        WT("cc", cc_idx)
                    else:
                        WT("pe", ("o_pj", 7, 3))
                    for grp in range(2):
                        eng.dma_start(out=av[:, grp * 8:(grp + 1) * 8, :],
                                      in_=gv[:, grp * 8:(grp + 1) * 8,
                                             bass.ds(offs[tg], 1024)]
                                      ).then_inc(qg.h, 16)
                INC("qg", 16 * 2)
                MK(("o_avb", tg), "qg")
                for tt8 in range(8):
                    tt = tg * 8 + tt8
                    sxn = "wt" + str(tt % 2)
                    if R == "sync":
                        if tt == 0:
                            WT("qc", base["qc"])
                        if tt >= 2:
                            WT("dve", ("o_ev", tt - 2, 3))
                        eng.dma_start(out=x1b[:, (tt % 2) * 2048:(tt % 2 + 1) * 2048],
                                      in_=srcT[tt * 128:(tt + 1) * 128, :]).then_inc(sems[sxn].h, 16)
                    INC(sxn, 16)
                    MK(("o_x1", tt), sxn)
                    for ds in range(4):
                        if R == "tensor":
                            if tt == 0 and ds == 0 and tg == 0:
                                WT("qg", ("o_wo",))
                                WT("act", base["act"])
                                WT("dve", base["dve"])
                            WT("qg", ("o_avb", tg))
                            if tt8 > 0 or tg > 0:
                                pt = tt - 1
                                WT("dve", ("o_ev", pt, ds))
                            for kc in range(16):
                                ins = eng.matmul(po[ds][:, :],
                                                 avb[:, kc * 1024 + tt8 * 128:kc * 1024 + (tt8 + 1) * 128],
                                                 woT_sb[:, kc * 2048 + ds * 512:kc * 2048 + (ds + 1) * 512],
                                                 start=(kc == 0), stop=(kc == 15))
                            ins.then_inc(pe.h, 1)
                        INC("pe", 1)
                        MK(("o_pj", tt, ds), "pe")
                        if R == "vector":
                            WT("pe", ("o_pj", tt, ds))
                            if ds == 0:
                                WT("wt" + str(tt % 2), ("o_x1", tt))
                                if tt >= 1:
                                    WT("qc", ("o_st", tt - 1))
                            eng.tensor_add(outb[:, (tt % 2) * 2048 + ds * 512:
                                                (tt % 2) * 2048 + (ds + 1) * 512],
                                           po[ds][:, :],
                                           x1b[:, (tt % 2) * 2048 + ds * 512:
                                               (tt % 2) * 2048 + (ds + 1) * 512]
                                           ).then_inc(dve.h, 1)
                        INC("dve", 1)
                        MK(("o_ev", tt, ds), "dve")
                    if R == "scalar":
                        WT("dve", ("o_ev", tt, 3))
                        eng.dma_start(out=dstT[tt * 128:(tt + 1) * 128, :],
                                      in_=outb[:, (tt % 2) * 2048:(tt % 2 + 1) * 2048]
                                      ).then_inc(qc.h, 16)
                    INC("qc", 16)
                    MK(("o_st", tt), "qc")

            if R == "sim":
                for k in sems:
                    marks[("o_end", k)] = C[k]

        body("sim")
        with nc.Block() as blk:
            @blk.sync
            def _(sy):
                body("sync", sy)

            @blk.gpsimd
            def _(g):
                body("gpsimd", g)

            @blk.tensor
            def _(te):
                body("tensor", te)

            @blk.scalar
            def _(sc):
                body("scalar", sc)

            @blk.vector
            def _(v):
                body("vector", v)

        for k in sems:
            sems[k].n = marks[("o_end", k)]


def _phase_ffn1(nc, src, dst, pubT, qfcTap, qpjTap, W, sems, pcol, bo_bc):
    import contextlib
    qs, qc, qg = sems["qs"], sems["qc"], sems["qg"]
    pe, act, dve = sems["pe"], sems["act"], sems["dve"]
    base = {k: s.n for k, s in sems.items()}
    marks = {}

    with contextlib.ExitStack() as st:
        xnh = st.enter_context(nc.sbuf_tensor("F_xnh", [128, 16 * 1024], BF16))
        oac = st.enter_context(nc.sbuf_tensor("F_oac", [128, 8 * 2048], F32))
        rtmp = st.enter_context(nc.sbuf_tensor("F_rt", [128, 2 * 512], F32))
        htmp = st.enter_context(nc.sbuf_tensor("F_ht", [128, 2 * 512], F32))
        ytmp = st.enter_context(nc.sbuf_tensor("F_yt", [128, 2 * 512], F32))
        wt = st.enter_context(nc.sbuf_tensor("F_wt", [128, 4 * 512], BF16))
        h_fm = st.enter_context(nc.sbuf_tensor("F_hfm", [128, 20 * 1024], BF16))
        ident32 = st.enter_context(nc.sbuf_tensor("F_id32", [128, 128], F32))
        ps = [st.enter_context(nc.psum_tensor(f"F_ps{i}", [128, 512], F32))
              for i in range(8)]

        def body(R, eng=None):
            C = dict(base)

            def INC(sname, k, ins=None):
                C[sname] += k
                if ins is not None:
                    ins.then_inc(sems[sname].h, k)
                return C[sname]

            def MK(key, sname):
                if R == "sim":
                    marks[key] = C[sname]

            def WT(sname, key_or_val):
                if R == "sim":
                    return
                v = marks[key_or_val] if isinstance(key_or_val, tuple) else key_or_val
                sems[sname].wait(eng, v)

            if R != "sim":
                sems["qc"].wait(eng, base["qc"])
            if R == "sync":
                eng.dma_start(out=ident32[:, :], in_=W["ident32"]).then_inc(qs.h, 16)
            INC("qs", 16)

            for hf in range(2):
                if hf == 0:
                    if R == "sync":
                        WT("qc", base["qc"])
                else:
                    if R != "sim":
                        WT("qc", ("f_st", 0))
                    if R == "sync":
                        WT("pe", ("f_pe_end", 0))
                for tt in range(8):
                    if R == "sync":
                        eng.dma_start(out=oac[:, tt * 2048:(tt + 1) * 2048],
                                      in_=src[hf * 1024 + tt * 128:hf * 1024 + (tt + 1) * 128, :]
                                      ).then_inc(qs.h, 16)
                    INC("qs", 16)
                MK(("f_oacld", hf), "qs")
                for tt in range(8):
                    if R == "vector":
                        WT("qs", ("f_ld", hf))
                        eng.tensor_add(oac[:, tt * 2048:(tt + 1) * 2048],
                                       oac[:, tt * 2048:(tt + 1) * 2048],
                                       bo_bc[:, :]).then_inc(dve.h, 1)
                    INC("dve", 1)
                MK(("f_bo", hf), "dve")
                for kc in range(16):
                    if R == "sync":
                        eng.dma_start(out=xnh[:, kc * 1024:(kc + 1) * 1024],
                                      in_=pubT[kc * 128:(kc + 1) * 128,
                                               hf * 1024:(hf + 1) * 1024]).then_inc(qs.h, 16)
                    INC("qs", 16)
                MK(("f_ld", hf), "qs")

                for fh in range(2):
                    for si in range(5):
                        for kc in range(16):
                            wti = C["wti"] = C.get("wti", 0) + 1
                            slot = (wti - 1) % 4
                            sn = f"wt{slot}"
                            if R == "sync":
                                if wti > 4:
                                    WT("pe", ("f_wfree", wti - 4))
                                eng.dma_start(out=wt[:, slot * 512:(slot + 1) * 512],
                                              in_=qfcTap[kc * 128:(kc + 1) * 128,
                                                         fh * 2560 + si * 512:
                                                         fh * 2560 + (si + 1) * 512]
                                              ).then_inc(sems[sn].h, 16)
                            INC(sn, 16)
                            MK(("f_wld", wti), sn)
                            if R == "tensor":
                                WT(sn, ("f_wld", wti))
                                if kc == 0:
                                    WT("qs", ("f_ld", hf))
                                    if hf == 0 and fh == 0 and si == 0:
                                        WT("act", base["act"])
                                        WT("dve", base["dve"])
                                    elif si == 0 and fh == 0:
                                        WT("dve", ("f_yev", hf - 1, 1, 3))
                                    elif si == 0:
                                        WT("dve", ("f_yev", hf, fh - 1, 3))
                                    else:
                                        WT("act", ("f_hev", hf, fh, si - 1, 7))
                                for tt in range(8):
                                    ins = eng.matmul(
                                        ps[tt][:, :],
                                        xnh[:, kc * 1024 + tt * 128:kc * 1024 + (tt + 1) * 128],
                                        wt[:, slot * 512:(slot + 1) * 512],
                                        start=(kc == 0), stop=(kc == 15))
                                    if tt == 7:
                                        ins.then_inc(pe.h, 1)
                            INC("pe", 1)
                            MK(("f_wfree", wti), "pe")
                            if kc == 15:
                                MK(("f_gstop", hf, fh, si), "pe")
                        for tt in range(8):
                            if R == "scalar":
                                WT("pe", ("f_gstop", hf, fh, si))
                                if tt >= 2 or si > 0 or fh > 0 or hf > 0:
                                    pk = ("f_tp", hf, fh, si, tt - 2)
                                    if tt < 2:
                                        pk = None
                                    if pk is not None and pk in marks:
                                        WT("pe", pk)
                                eng.activation(rtmp[:, (tt % 2) * 512:(tt % 2 + 1) * 512],
                                               ps[tt][:, :], AF.Relu).then_inc(act.h, 1)
                                eng.activation(htmp[:, (tt % 2) * 512:(tt % 2 + 1) * 512],
                                               rtmp[:, (tt % 2) * 512:(tt % 2 + 1) * 512],
                                               AF.Square).then_inc(act.h, 1)
                            INC("act", 2)
                            MK(("f_sq", hf, fh, si, tt), "act")
                            if R == "tensor":
                                WT("act", ("f_sq", hf, fh, si, tt))
                                for j in range(4):
                                    ins = eng.transpose(
                                        ps[tt][:, j * 128:(j + 1) * 128],
                                        htmp[:, (tt % 2) * 512 + j * 128:(tt % 2) * 512 + (j + 1) * 128],
                                        ident32[:, :])
                                ins.then_inc(pe.h, 1)
                            INC("pe", 1)
                            MK(("f_tp", hf, fh, si, tt), "pe")
                            if R == "scalar":
                                WT("pe", ("f_tp", hf, fh, si, tt))
                                if hf > 0 or fh > 0:
                                    WT("pe", ("f_ygstop", hf if fh > 0 else hf - 1,
                                              0 if fh > 0 else 1, 3))
                                dst_ap = bass.AP(h_fm, si * 4 * 1024 + tt * 128,
                                                 [[20 * 1024, 128], [1024, 4], [1, 128]])
                                eng.activation(dst_ap,
                                               ps[tt][:, :].rearrange("p (s t) -> p s t", t=128),
                                               AF.Copy).then_inc(act.h, 1)
                            INC("act", 1)
                            MK(("f_hev", hf, fh, si, tt), "act")

                    for ds in range(4):
                        for ic in range(20):
                            wti = C["wti"] = C.get("wti", 0) + 1
                            slot = (wti - 1) % 4
                            sn = f"wt{slot}"
                            if R == "sync":
                                WT("pe", ("f_wfree", wti - 4))
                                eng.dma_start(out=wt[:, slot * 512:(slot + 1) * 512],
                                              in_=qpjTap[fh * 2560 + ic * 128:
                                                         fh * 2560 + (ic + 1) * 128,
                                                         ds * 512:(ds + 1) * 512]
                                              ).then_inc(sems[sn].h, 16)
                            INC(sn, 16)
                            MK(("f_wld", wti), sn)
                            if R == "tensor":
                                WT(sn, ("f_wld", wti))
                                if ic == 0:
                                    WT("act", ("f_hev", hf, fh, 4, 7))
                                    if ds > 0:
                                        WT("dve", ("f_yev", hf, fh, ds - 1))
                                for tt in range(8):
                                    ins = eng.matmul(
                                        ps[tt][:, :],
                                        h_fm[:, ic * 1024 + tt * 128:ic * 1024 + (tt + 1) * 128],
                                        wt[:, slot * 512:(slot + 1) * 512],
                                        start=(ic == 0), stop=(ic == 19))
                                    if tt == 7:
                                        ins.then_inc(pe.h, 1)
                            INC("pe", 1)
                            MK(("f_wfree", wti), "pe")
                            if ic == 19:
                                MK(("f_ygstop", hf, fh, ds), "pe")
                        for tt in range(8):
                            if R == "vector":
                                WT("pe", ("f_ygstop", hf, fh, ds))
                                eng.tensor_scalar(ytmp[:, (tt % 2) * 512:(tt % 2 + 1) * 512],
                                                  ps[tt][:, :], pcol[:, 4:5], None,
                                                  ALU.mult)
                                eng.tensor_add(
                                    oac[:, tt * 2048 + ds * 512:tt * 2048 + (ds + 1) * 512],
                                    ytmp[:, (tt % 2) * 512:(tt % 2 + 1) * 512],
                                    oac[:, tt * 2048 + ds * 512:tt * 2048 + (ds + 1) * 512]
                                    ).then_inc(dve.h, 1)
                            INC("dve", 1)
                            if tt in (3, 7):
                                MK(("f_yev", hf, fh, ds, tt), "dve")
                        MK(("f_yev", hf, fh, ds), "dve")
                MK(("f_pe_end", hf), "pe")
                if R == "vector":
                    eng.memset(ytmp[:, 0:4], 0.0).then_inc(dve.h, 1)
                    eng.memset(ytmp[:, 4:8], 0.0).then_inc(dve.h, 1)
                INC("dve", 2)
                MK(("f_flush", hf), "dve")

                for tt in range(8):
                    if R == "scalar":
                        WT("dve", ("f_flush", hf))
                        eng.dma_start(out=dst[hf * 1024 + tt * 128:hf * 1024 + (tt + 1) * 128, :],
                                      in_=oac[:, tt * 2048:(tt + 1) * 2048]).then_inc(qc.h, 16)
                    INC("qc", 16)
                MK(("f_st", hf), "qc")

            if R == "sim":
                for k in sems:
                    marks[("f_end", k)] = C[k]

        body("sim")
        with nc.Block() as blk:
            @blk.sync
            def _(sy):
                body("sync", sy)

            @blk.gpsimd
            def _(g):
                body("gpsimd", g)

            @blk.tensor
            def _(te):
                body("tensor", te)

            @blk.scalar
            def _(sc):
                body("scalar", sc)

            @blk.vector
            def _(v):
                body("vector", v)

        for k in sems:
            sems[k].n = marks[("f_end", k)]


_NC_CACHE = {}


def _get_program(layers, taps=()):
    key = (tuple(layers), tuple(taps))
    if key not in _NC_CACHE:
        _NC_CACHE[key] = build(layers, taps)
    return _NC_CACHE[key]


def _assemble(outs):
    x = np.empty((B, S, D), np.float32)
    for c in range(NCORES):
        b, p = c // 4, c % 4
        x[b, p * TOK:(p + 1) * TOK] = outs[c]
    return x


FUSED = True


def kernel(**inputs):
    ins = {k: np.asarray(v) for k, v in inputs.items()}
    if FUSED:
        maps = host_prep(ins, layers=(0, 1))
        nc = _get_program((0, 1))
        res = run_bass_kernel_spmd(nc, maps, core_ids=list(range(NCORES)))
        return _assemble([res.results[c]["out"] for c in range(NCORES)])
    maps0 = host_prep(ins, layers=(0,))
    res0 = run_bass_kernel_spmd(_get_program((0,)), maps0,
                                core_ids=list(range(NCORES)))
    x1 = _assemble([res0.results[c]["out"] for c in range(NCORES)])
    ins2 = dict(ins)
    ins2["x1"] = x1
    maps1 = host_prep(ins2, layers=(1,))
    res1 = run_bass_kernel_spmd(_get_program((1,)), maps1,
                                core_ids=list(range(NCORES)))
    return _assemble([res1.results[c]["out"] for c in range(NCORES)])


# version 05
# speedup vs baseline: 1.5233x; 1.1993x over previous
import numpy as np
import ml_dtypes
import concourse.bass as bass
import concourse.mybir as mybir
from concourse.bass_utils import run_bass_kernel_spmd

F32 = mybir.dt.float32
BF16 = mybir.dt.bfloat16
AF = mybir.ActivationFunctionType
ALU = mybir.AluOpType
AX = mybir.AxisListType
BF = ml_dtypes.bfloat16

B, S, D = 2, 8192, 2048
NCORES = 8
TOK = 2048
NTT = TOK // 128
HPC = 4
DH = 128
NKC = D // 128
NQT = S // 512
FSH, FRT = 2816, 1408
DC, DQC = 512, 1024
FT = 5120
EPS = 1e-5
MOE_EXPERTS = 9
MOE_HALVES = 2
ATTN_CC = True
MOE_DBG = False
MOE_DBG2 = False
RGROUPS = [[0, 1, 2, 3], [4, 5, 6, 7]]
PUB0, PUB1 = D, D + DC + DQC


class Sem:
    __slots__ = ("h", "n", "maxw", "name")

    def __init__(self, h, name):
        self.h = h
        self.n = 0
        self.maxw = 0
        self.name = name

    def inc(self, ins, k):
        ins.then_inc(self.h, k)
        self.n += k
        return self.n

    def wait(self, eng, v):
        if v > self.maxw:
            self.maxw = v
        eng.wait_ge(self.h, v)


def _slices(total, step):
    return [(o, min(step, total - o)) for o in range(0, total, step)]


def _host_consts():
    c = {}
    c["ident16"] = np.eye(128, dtype=BF)
    c["ident32"] = np.eye(128, dtype=np.float32)
    P = np.zeros((128, 128), np.float32)
    for i in range(64):
        P[2 * i, 2 * i + 1] = -1.0
        P[2 * i + 1, 2 * i] = 1.0
    c["prope"] = P.T.astype(BF)
    m = np.zeros((128, 4 * 512), np.float32)
    qq = np.arange(512)
    for j in range(4):
        for kk in range(128):
            m[kk, j * 512:(j + 1) * 512] = (128 * j + kk <= qq)
    c["masks"] = m.astype(BF)
    idx = np.arange(64, dtype=np.float64)
    inv = (10000.0 ** (-2.0 * idx / 128.0))
    pos = np.arange(S, dtype=np.float64)
    theta = pos[:, None] * inv[None, :]
    cos = np.cos(theta).astype(np.float32)
    sin = np.sin(theta).astype(np.float32)
    c["cosq"] = np.ascontiguousarray(np.repeat(cos.T, 2, axis=0))
    c["sinq"] = np.ascontiguousarray(np.repeat(sin.T, 2, axis=0))
    c["cosk"] = np.ascontiguousarray(c["cosq"] / np.float32(16.0))
    c["sink"] = np.ascontiguousarray(c["sinq"] / np.float32(16.0))
    return c


def host_prep(ins, layers=(0, 1)):
    cst = _host_consts()
    shared = {}
    if 0 in layers:
        shared.update(
            sw1=ins["L0_ffn_sw1"], sw2=ins["L0_ffn_sw2"], sw3=ins["L0_ffn_sw3"],
            rw1=ins["L0_ffn_rw1"], rw2=ins["L0_ffn_rw2"], rw3=ins["L0_ffn_rw3"],
            gwf=ins["L0_ffn_gate_w"],
            gwT=np.ascontiguousarray(ins["L0_attn_gate_w"].T),
            gbcol=np.ascontiguousarray(ins["L0_attn_gate_b"].reshape(128, 1)),
            woT=np.ascontiguousarray(ins["L0_attn_o_w"].T),
        )
    if 1 in layers:
        shared.update(
            wdkvT=np.ascontiguousarray(ins["L1_attn_W_DKV_w"].T),
            bdkv=np.ascontiguousarray(ins["L1_attn_W_DKV_b"].reshape(DC, 1)),
            wdqT=np.ascontiguousarray(ins["L1_attn_W_DQ_w"].T),
            bdq=np.ascontiguousarray(ins["L1_attn_W_DQ_b"].reshape(DQC, 1)),
            woT1=np.ascontiguousarray(ins["L1_attn_W_O_w"].T),
            borow=np.ascontiguousarray(ins["L1_attn_W_O_b"].reshape(1, D)),
            wfcT=np.ascontiguousarray(ins["L1_ffn_w_fc"].T),
            wpjT=np.ascontiguousarray(ins["L1_ffn_w_proj"].T),
            cosq=cst["cosq"], sinq=cst["sinq"], cosk=cst["cosk"], sink=cst["sink"],
            prope=cst["prope"],
        )
    shared.update(ident16=cst["ident16"], ident32=cst["ident32"], masks=cst["masks"])

    if 0 in layers:
        qT = np.ascontiguousarray(ins["L0_attn_q_w"].T)
        kT = np.ascontiguousarray(ins["L0_attn_k_w"].T)
        vT = np.ascontiguousarray(ins["L0_attn_v_w"].T)
    if 1 in layers:
        ukT = np.ascontiguousarray(ins["L1_attn_W_UK_w"].T)
        uvT = np.ascontiguousarray(ins["L1_attn_W_UV_w"].T)
        uqT = np.ascontiguousarray(ins["L1_attn_W_UQ_w"].T)
        krT = np.ascontiguousarray(ins["L1_attn_W_KR_w"].T)

    xin_key = "L0_attn_x" if 0 in layers else "x1"
    xfull = np.asarray(ins[xin_key])
    if xfull.ndim == 3:
        xfull = xfull.reshape(B, S, D)

    maps = []
    for c in range(NCORES):
        b, p = c // 4, c % 4
        hs = slice(p * 512, (p + 1) * 512)
        m = dict(shared)
        m["xin"] = np.ascontiguousarray(xfull[b, p * TOK:(p + 1) * TOK, :], dtype=np.float32)
        if 0 in layers:
            m["wqT"] = np.ascontiguousarray(qT[:, hs])
            m["wkT"] = np.ascontiguousarray(kT[:, hs])
            m["wvT"] = np.ascontiguousarray(vT[:, hs])
        if 1 in layers:
            m["wukT"] = np.ascontiguousarray(ukT[:, hs])
            m["buk"] = np.ascontiguousarray(ins["L1_attn_W_UK_b"][hs].reshape(512, 1))
            m["wuvT"] = np.ascontiguousarray(uvT[:, hs])
            m["buv"] = np.ascontiguousarray(ins["L1_attn_W_UV_b"][hs].reshape(512, 1))
            m["wuqT"] = np.ascontiguousarray(uqT[:, hs])
            m["buq"] = np.ascontiguousarray(ins["L1_attn_W_UQ_b"][hs].reshape(512, 1))
            m["wkrT"] = np.ascontiguousarray(krT[:, hs])
            m["bkr"] = np.ascontiguousarray(ins["L1_attn_W_KR_b"][hs].reshape(512, 1))
        maps.append(m)
    return maps


def build(layers=(0, 1), taps=(), upto=None):
    import contextlib
    _pidx = [0]

    def _go():
        _pidx[0] += 1
        return upto is None or _pidx[0] <= upto

    nc = bass.Bass("TRN2", target_bir_lowering=False)
    HASL0, HASL1 = (0 in layers), (1 in layers)

    xin = nc.dram_tensor("xin", [TOK, D], F32, kind="ExternalInput").ap()
    out = nc.dram_tensor("out", [TOK, D], F32, kind="ExternalOutput").ap()
    if HASL0 and HASL1:
        r1 = nc.dram_tensor("r1", [TOK, D], F32).ap()
    else:
        r1 = out if HASL0 else None
    l_in = {0: xin, 1: (r1 if HASL0 else xin)}
    l_out = {0: r1, 1: out}

    W = {}

    def din(name, shape, dt=F32):
        W[name] = nc.dram_tensor(name, list(shape), dt, kind="ExternalInput").ap()
        return W[name]

    din("ident16", (128, 128), BF16)
    din("ident32", (128, 128), F32)
    din("masks", (128, 4 * 512), BF16)
    if HASL0:
        din("sw1", (D, FSH)); din("sw2", (FSH, D)); din("sw3", (D, FSH))
        din("rw1", (8, D, FRT)); din("rw2", (8, FRT, D)); din("rw3", (8, D, FRT))
        din("gwf", (D, 8))
        din("gwT", (128, 128)); din("gbcol", (128, 1))
        din("wqT", (D, 512)); din("wkT", (D, 512)); din("wvT", (D, 512))
        din("woT", (D, D))
    if HASL1:
        din("wdkvT", (D, DC)); din("bdkv", (DC, 1))
        din("wdqT", (D, DQC)); din("bdq", (DQC, 1))
        din("wukT", (DC, 512)); din("buk", (512, 1))
        din("wuvT", (DC, 512)); din("buv", (512, 1))
        din("wuqT", (DQC, 512)); din("buq", (512, 1))
        din("wkrT", (D, 512)); din("bkr", (512, 1))
        din("woT1", (D, D)); din("borow", (1, D))
        din("wfcT", (D, FT)); din("wpjT", (FT, D))
        din("prope", (128, 128), BF16)
        din("cosq", (128, S)); din("sinq", (128, S))
        din("cosk", (128, S)); din("sink", (128, S))

    pub, gat, puba, gata = {}, {}, {}, {}
    for lay in layers:
        pr = PUB0 if lay == 0 else PUB1
        pub[lay] = nc.dram_tensor(f"pub{lay}", [pr, TOK], BF16).ap()
        gat[lay] = nc.dram_tensor(f"gat{lay}", [4 * pr, TOK], BF16).ap()
        puba[lay] = nc.dram_tensor(f"puba{lay}", [HPC * DH, S], BF16).ap()
        gata[lay] = nc.dram_tensor(f"gata{lay}", [4 * HPC * DH, S], BF16).ap()
    if HASL1:
        qfcT = nc.dram_tensor("qfcT", [D, FT], BF16).ap()
        qpjT = nc.dram_tensor("qpjT", [FT, D], BF16).ap()

    tap_out = {}
    for t in taps:
        srcs = {"pub0": (pub.get(0), BF16), "pub1": (pub.get(1), BF16),
                "puba0": (puba.get(0), BF16), "puba1": (puba.get(1), BF16),
                "gata0": (gata.get(0), BF16),
                "r1": (r1 if (HASL0 and HASL1) else None, F32)}
        src, dt = srcs[t]
        tap_out[t] = (nc.dram_tensor("tap_" + t, list(src.tensor.shape), dt,
                                     kind="ExternalOutput").ap(), src)

    ctx = contextlib.ExitStack()
    with ctx:
        sems = {}
        for name in ("qs", "qc", "qg", "cc", "pe", "act", "dve", "pool",
                     "wt0", "wt1", "wt2", "wt3"):
            sems[name] = Sem(ctx.enter_context(nc.semaphore(name)), name)
        qs, qc, qg, cc = sems["qs"], sems["qc"], sems["qg"], sems["cc"]
        pe, act, dve, pool = sems["pe"], sems["act"], sems["dve"], sems["pool"]

        ident16 = ctx.enter_context(nc.sbuf_tensor("ident16s", [128, 128], BF16))
        ident32 = ctx.enter_context(nc.sbuf_tensor("ident32s", [128, 128], F32))
        masks = ctx.enter_context(nc.sbuf_tensor("masks_s", [128, 4 * 512], BF16))
        ones_c = ctx.enter_context(nc.sbuf_tensor("ones_c", [128, 1], F32))
        ones_c16 = ctx.enter_context(nc.sbuf_tensor("ones_c16", [128, 1], BF16))
        ones_r = ctx.enter_context(nc.sbuf_tensor("ones_r", [1, 128], F32))
        pcol = ctx.enter_context(nc.sbuf_tensor("pcol", [128, 40], F32))
        prow = ctx.enter_context(nc.sbuf_tensor("prow", [1, 256], F32))
        bo_bc = ctx.enter_context(nc.sbuf_tensor("bo_bc", [128, D], F32)) if HASL1 else None
        prope_s = ctx.enter_context(nc.sbuf_tensor("prope_s", [128, 128], BF16)) if HASL1 else None
        borow_s = ctx.enter_context(nc.sbuf_tensor("borow_s", [1, D], F32)) if HASL1 else None

        with nc.Block() as blk:
            @blk.sync
            def _(sy):
                sy.dma_start(out=ident16[:, :], in_=W["ident16"]).then_inc(qs.h, 16)
                sy.dma_start(out=ident32[:, :], in_=W["ident32"]).then_inc(qs.h, 16)
                sy.dma_start(out=masks[:, :], in_=W["masks"]).then_inc(qs.h, 16)
                qs.n += 48
                if HASL0:
                    sy.dma_start(out=pcol[:, 5:6], in_=W["gbcol"]).then_inc(qs.h, 16)
                    qs.n += 16
                if HASL1:
                    for i in range(4):
                        sy.dma_start(out=pcol[:, 6 + i:7 + i],
                                     in_=W["bdkv"][i * 128:(i + 1) * 128, :]).then_inc(qs.h, 16)
                    for i in range(8):
                        sy.dma_start(out=pcol[:, 10 + i:11 + i],
                                     in_=W["bdq"][i * 128:(i + 1) * 128, :]).then_inc(qs.h, 16)
                    for nm, base in (("buk", 18), ("buv", 22), ("buq", 26), ("bkr", 30)):
                        for i in range(4):
                            sy.dma_start(out=pcol[:, base + i:base + i + 1],
                                         in_=W[nm][i * 128:(i + 1) * 128, :]).then_inc(qs.h, 16)
                    sy.dma_start(out=prope_s[:, :], in_=W["prope"]).then_inc(qs.h, 16)
                    sy.dma_start(out=borow_s[:, :], in_=W["borow"]).then_inc(qs.h, 16)
                    qs.n += 16 * (4 + 8 + 16 + 2)

            @blk.vector
            def _(v):
                v.memset(ones_c[:, :], 1.0)
                v.memset(ones_c16[:, :], 1.0)
                v.memset(pcol[:, 39:40], EPS)
                v.memset(ones_r[:, :], 1.0).then_inc(dve.h, 1)
                dve.n += 1

            @blk.scalar
            def _(sc):
                qs.wait(sc, qs.n)
                dve.wait(sc, dve.n)
                if HASL1:
                    for i in range(4):
                        sc.mul(pcol[:, 18 + i:19 + i], pcol[:, 18 + i:19 + i], 1.0 / 16.0)

        if HASL1:
            with nc.psum_tensor("pbo", [128, 512], F32) as pbo, nc.Block() as blk:
                @blk.tensor
                def _(te):
                    qs.wait(te, qs.n)
                    dve.wait(te, dve.n)
                    for i in range(4):
                        act.wait(te, act.n + i)
                        te.matmul(pbo[:, :], ones_r[:, 0:128], borow_s[0:1, i * 512:(i + 1) * 512],
                                  start=True, stop=True).then_inc(pe.h, 1)
                        pe.n += 1

                @blk.scalar
                def _(sc):
                    for i in range(4):
                        pe.wait(sc, pe.n - 3 + i)
                        sc.activation(bo_bc[:, i * 512:(i + 1) * 512], pbo[:, :],
                                      AF.Copy).then_inc(act.h, 1)
                        act.n += 1

        if HASL0:
            if _go():
                _phase_ln_pub(nc, 0, l_in[0], pub[0], gat[0], W, sems,
                              ident16, ones_c, ones_r, pcol, bo_bc)
            if _go():
                _phase_moe(nc, l_in[0], l_out[0], pub[0], W, sems, pcol)
            if HASL1 and _go():
                _tern_prepass(nc, W, sems, ones_c, ones_r, pcol, prow)
            if HASL1 and _go():
                _tern_quant(nc, W, sems, pcol, qfcT, qpjT)
            if _go():
                _phase_attn(nc, 0, gat[0], puba[0], gata[0], W, sems,
                            ident16, ones_c16, ones_r, pcol, prope_s, sems["cc"].n)
            if _go():
                _phase_oproj(nc, gata[0], l_out[0], l_out[0], W["woT"], sems,
                             sems["cc"].n, tag=0)
        if HASL1:
            if not HASL0:
                if _go():
                    _tern_prepass(nc, W, sems, ones_c, ones_r, pcol, prow)
                if _go():
                    _tern_quant(nc, W, sems, pcol, qfcT, qpjT)
            if _go():
                _phase_ln_pub(nc, 1, l_in[1], pub[1], gat[1], W, sems,
                              ident16, ones_c, ones_r, pcol, bo_bc)
            if _go():
                _phase_ffn1(nc, l_in[1], l_out[1], pub[1], qfcT, qpjT, W, sems,
                            pcol, bo_bc)
            if _go():
                _phase_attn(nc, 1, gat[1], puba[1], gata[1], W, sems,
                            ident16, ones_c16, ones_r, pcol, prope_s, sems["cc"].n)
            if _go():
                _phase_oproj(nc, gata[1], l_out[1], out, W["woT1"], sems,
                             sems["cc"].n, tag=1)

        with nc.Block() as blk:
            @blk.gpsimd
            def _(g):
                qc.wait(g, qc.n)
                qs.wait(g, qs.n)
                for t, (dst, src) in tap_out.items():
                    g.dma_start(out=dst, in_=src).then_inc(qg.h, 16)
                    qg.n += 16
                qg.wait(g, qg.n)

            @blk.sync
            def _(sy):
                qc.wait(sy, qc.n)

        for s in sems.values():
            assert s.maxw <= s.n, f"sem {s.name}: wait {s.maxw} > total {s.n}"
    return nc


def _phase_ln_pub(nc, lay, src, pubT, gatT, W, sems, ident16, ones_c, ones_r,
                  pcol, bo_bc):
    qs, qc, qg, cc = sems["qs"], sems["qc"], sems["qg"], sems["cc"]
    pe, act, dve, pool = sems["pe"], sems["act"], sems["dve"], sems["pool"]
    L1 = (lay == 1)
    APT = 21 if L1 else 9
    PPT = 16 if L1 else 4
    NPUB = 2 if L1 else 1
    qs0, qc0, qg0, pe0, act0, dve0 = qs.n, qc.n, qg.n, pe.n, act.n, dve.n
    w0_, w1_ = sems["wt0"].n, sems["wt1"].n
    D_ = D

    import contextlib
    with contextlib.ExitStack() as st:
        xbuf = st.enter_context(nc.sbuf_tensor(f"A{lay}_xbuf", [128, 2 * D_], F32))
        xsq = st.enter_context(nc.sbuf_tensor(f"A{lay}_xsq", [128, 2 * D_], F32))
        xnt = st.enter_context(nc.sbuf_tensor(f"A{lay}_xnt", [128, 2 * D_], BF16))
        stage = st.enter_context(nc.sbuf_tensor(f"A{lay}_stage", [128, 2 * D_], BF16))
        stf = st.enter_context(nc.sbuf_tensor(f"A{lay}_stf", [128, 114], F32))
        trpb = [st.enter_context(nc.psum_tensor(f"A{lay}_trp{i}", [128, 512], BF16))
                for i in range(2)]
        if L1:
            ckst = st.enter_context(nc.sbuf_tensor("A1_ckst", [128, 2 * 1536], BF16))
            wdkv_sb = st.enter_context(nc.sbuf_tensor("A1_wdkv", [128, 16 * 512], BF16))
            wdq_sb = st.enter_context(nc.sbuf_tensor("A1_wdq", [128, 16 * 1024], BF16))
            pck = [st.enter_context(nc.psum_tensor(f"A1_pck{i}", [128, 128], F32))
                   for i in range(4)]

        with nc.Block() as blk:
            @blk.sync
            def _(sy):
                qc.wait(sy, qc0)
                for t in range(NTT):
                    if t >= 2:
                        dve.wait(sy, dve0 + 5 * (t - 1))
                    wsem = sems["wt" + str(t % 2)]
                    sy.dma_start(out=xbuf[:, (t % 2) * D_:(t % 2 + 1) * D_],
                                 in_=src[t * 128:(t + 1) * 128, :]).then_inc(wsem.h, 16)
                    wsem.n += 16

            @blk.gpsimd
            def _(g):
                qc.wait(g, qc0)
                if L1:
                    for kc in range(16):
                        g.dma_start(out=wdkv_sb[:, kc * 512:(kc + 1) * 512],
                                    in_=W["wdkvT"][kc * 128:(kc + 1) * 128, :]).then_inc(qg.h, 16)
                        qg.n += 16
                    for kc in range(16):
                        g.dma_start(out=wdq_sb[:, kc * 1024:(kc + 1) * 1024],
                                    in_=W["wdqT"][kc * 128:(kc + 1) * 128, :]).then_inc(qg.h, 16)
                        qg.n += 16
                qc.wait(g, qc0 + 16 * NPUB * NTT)
                g.collective_compute(
                    "AllGather", ALU.bypass, replica_groups=RGROUPS,
                    ins=[pubT.opt()], outs=[gatT.opt()]).then_inc(cc.h, 1)
                cc.n += 1

            @blk.vector
            def _(v):
                qc.wait(v, qc0)
                for t in range(NTT):
                    sl = slice((t % 2) * D_, (t % 2 + 1) * D_)
                    wsem = sems["wt" + str(t % 2)]
                    wsem.wait(v, (w0_ if t % 2 == 0 else w1_) + 16 * (t // 2 + 1))
                    v.reduce_sum(stf[:, t:t + 1], xbuf[:, sl], axis=AX.X).then_inc(dve.h, 1)
                    act.wait(v, act0 + APT * t + 1)
                    v.reduce_sum(stf[:, 16 + t:17 + t], xsq[:, sl], axis=AX.X).then_inc(dve.h, 1)
                    act.wait(v, act0 + APT * t + 4)
                    v.tensor_scalar(stf[:, 64 + t:65 + t], stf[:, 112 + t % 2:113 + t % 2],
                                    stf[:, 48 + t:49 + t], None,
                                    ALU.subtract).then_inc(dve.h, 1)
                    act.wait(v, act0 + APT * t + 5)
                    v.reciprocal(stf[:, 96 + t:97 + t], stf[:, 80 + t:81 + t]).then_inc(dve.h, 1)
                    v.tensor_scalar(xnt[:, sl], xbuf[:, sl], stf[:, 32 + t:33 + t],
                                    stf[:, 96 + t:97 + t], ALU.subtract,
                                    ALU.mult).then_inc(dve.h, 1)
                    dve.n += 5

            @blk.tensor
            def _(te):
                qc.wait(te, qc0)
                act.wait(te, act0)
                if L1:
                    qg.wait(te, qg0 + 16 * 32)
                for t in range(NTT):
                    sl16 = slice((t % 2) * D_, (t % 2 + 1) * D_)
                    dve.wait(te, dve0 + 5 * t + 5)
                    for g in range(4):
                        G = 4 * t + g
                        if G >= 2:
                            t2, g2 = (G - 2) // 4, (G - 2) % 4
                            act.wait(te, act0 + APT * t2 + 6 + g2)
                        for j in range(4):
                            ins = te.transpose(trpb[g % 2][:, j * 128:(j + 1) * 128],
                                               xnt[:, (t % 2) * D_ + (4 * g + j) * 128:
                                                   (t % 2) * D_ + (4 * g + j + 1) * 128],
                                               ident16[:, :])
                            if j == 3:
                                ins.then_inc(pe.h, 1)
                                pe.n += 1
                    if L1:
                        act.wait(te, act0 + APT * t + 9)
                        for r, (wsb, wstep, mcs) in enumerate(
                                [(wdkv_sb, 512, range(4)),
                                 (wdq_sb, 1024, range(4)),
                                 (wdq_sb, 1024, range(4, 8))]):
                            for i, mc in enumerate(mcs):
                                if r > 0:
                                    act.wait(te, act0 + APT * t + 9 + 4 * (r - 1) + i + 1)
                                for kc in range(16):
                                    ins = te.matmul(
                                        pck[i][:, :],
                                        wsb[:, kc * wstep + mc * 128:kc * wstep + (mc + 1) * 128],
                                        stage[:, (t % 2) * D_ + kc * 128:(t % 2) * D_ + (kc + 1) * 128],
                                        start=(kc == 0), stop=(kc == 15))
                                ins.then_inc(pe.h, 1)
                                pe.n += 1

            @blk.scalar
            def _(sc):
                qc.wait(sc, qc0)
                for t in range(NTT):
                    sl = slice((t % 2) * D_, (t % 2 + 1) * D_)
                    wsem = sems["wt" + str(t % 2)]
                    wsem.wait(sc, (w0_ if t % 2 == 0 else w1_) + 16 * (t // 2 + 1))
                    sc.activation(xsq[:, sl], xbuf[:, sl], AF.Square).then_inc(act.h, 1)
                    dve.wait(sc, dve0 + 5 * t + 1)
                    sc.activation(stf[:, 32 + t:33 + t], stf[:, t:t + 1], AF.Copy,
                                  scale=1.0 / D_).then_inc(act.h, 1)
                    sc.activation(stf[:, 48 + t:49 + t], stf[:, 32 + t:33 + t],
                                  AF.Square).then_inc(act.h, 1)
                    dve.wait(sc, dve0 + 5 * t + 2)
                    sc.activation(stf[:, 112 + t % 2:113 + t % 2], stf[:, 16 + t:17 + t],
                                  AF.Copy, scale=1.0 / D_).then_inc(act.h, 1)
                    dve.wait(sc, dve0 + 5 * t + 3)
                    sc.activation(stf[:, 80 + t:81 + t], stf[:, 64 + t:65 + t], AF.Sqrt,
                                  bias=pcol[:, 39:40]).then_inc(act.h, 1)
                    act.n += 5
                    if t >= 1:
                        qc.wait(sc, qc0 + 16 * NPUB * t)
                    for g in range(4):
                        pe.wait(sc, pe0 + PPT * t + g + 1)
                        sc.activation(stage[:, (t % 2) * D_ + g * 512:(t % 2) * D_ + (g + 1) * 512],
                                      trpb[g % 2][:, :], AF.Copy).then_inc(act.h, 1)
                        act.n += 1
                    if L1:
                        for mc in range(4):
                            pe.wait(sc, pe0 + PPT * t + 4 + mc + 1)
                            sc.activation(ckst[:, (t % 2) * 1536 + mc * 128:(t % 2) * 1536 + (mc + 1) * 128],
                                          pck[mc][:, :], AF.Identity,
                                          bias=pcol[:, 6 + mc:7 + mc]).then_inc(act.h, 1)
                            act.n += 1
                        for mc in range(8):
                            pe.wait(sc, pe0 + PPT * t + 8 + mc + 1)
                            sc.activation(ckst[:, (t % 2) * 1536 + 512 + mc * 128:
                                              (t % 2) * 1536 + 512 + (mc + 1) * 128],
                                          pck[mc % 4][:, :], AF.Identity,
                                          bias=pcol[:, 10 + mc:11 + mc]).then_inc(act.h, 1)
                            act.n += 1
                    sc.dma_start(
                        out=pubT[0:D_, t * 128:(t + 1) * 128].rearrange("(c p) t -> p c t", p=128),
                        in_=stage[:, sl].rearrange("p (c t) -> p c t", t=128)).then_inc(qc.h, 16)
                    qc.n += 16
                    if L1:
                        sc.dma_start(
                            out=pubT[D_:D_ + 1536, t * 128:(t + 1) * 128].rearrange("(c p) t -> p c t", p=128),
                            in_=ckst[:, (t % 2) * 1536:(t % 2 + 1) * 1536].rearrange(
                                "p (c t) -> p c t", t=128)).then_inc(qc.h, 16)
                        qc.n += 16



def _phase_moe(nc, src, dst, pubT, W, sems, pcol, bo_bc=None):
    import contextlib
    qs, qc, qg = sems["qs"], sems["qc"], sems["qg"]
    pe, act, dve = sems["pe"], sems["act"], sems["dve"]
    base = {k: s.n for k, s in sems.items()}
    marks = {}
    SH = _slices(FSH, 512)
    RT = _slices(FRT, 512)

    with contextlib.ExitStack() as st:
        xnh = st.enter_context(nc.sbuf_tensor("M_xnh", [128, 16 * 1024], BF16))
        oac = st.enter_context(nc.sbuf_tensor("M_oac", [128, 8 * 2048], F32))
        s_sb = st.enter_context(nc.sbuf_tensor("M_s", [128, 8 * 512], BF16))
        htmp = st.enter_context(nc.sbuf_tensor("M_ht", [128, 2 * 512], F32))
        htA = st.enter_context(nc.sbuf_tensor("M_htA", [128, 2 * 512], F32))
        wt = st.enter_context(nc.sbuf_tensor("M_wt", [128, 4 * 2048], BF16))
        h_fm = st.enter_context(nc.sbuf_tensor("M_hfm", [128, 22 * 1024], BF16))
        cwsb = st.enter_context(nc.sbuf_tensor("M_cw", [128, 64], F32))
        rb = st.enter_context(nc.sbuf_tensor("M_rb", [128, 4 * 64], F32))
        rst = st.enter_context(nc.sbuf_tensor("M_rst", [128, 6 * 8], F32))
        lg = st.enter_context(nc.sbuf_tensor("M_lg", [128, 64], F32))
        gwf_sb = st.enter_context(nc.sbuf_tensor("M_gwf", [128, 16 * 8], BF16))
        ident32 = st.enter_context(nc.sbuf_tensor("M_id32", [128, 128], F32))
        ps = [st.enter_context(nc.psum_tensor(f"M_ps{i}", [128, 512], F32))
              for i in range(8)]

        def body(R, eng=None):
            C = dict(base)

            def INC(sname, k, ins=None):
                C[sname] += k
                if ins is not None:
                    ins.then_inc(sems[sname].h, k)
                return C[sname]

            def MK(key, sname):
                if R == "sim":
                    marks[key] = C[sname]

            def WT(sname, key_or_val):
                if R == "sim":
                    return
                v = marks[key_or_val] if isinstance(key_or_val, tuple) else key_or_val
                sems[sname].wait(eng, v)

            if R != "sim":
                sems["qc"].wait(eng, base["qc"])
            if R == "gpsimd":
                eng.dma_start(out=gwf_sb[:, :],
                              in_=W["gwf"].rearrange("(c p) e -> p c e", p=128)
                              ).then_inc(qg.h, 16)
            if R == "sync":
                eng.dma_start(out=ident32[:, :], in_=W["ident32"]).then_inc(qs.h, 16)
            INC("qg", 16)
            INC("qs", 16)
            MK(("m_pre", "qg"), "qg")
            MK(("m_pre", "qs"), "qs")

            for hf in range(MOE_HALVES):
                if hf == 1:
                    WT("qc", ("m_st", 0))
                    WT("pe", ("m_peend", 0))
                elif R == "sync":
                    WT("qc", base["qc"])
                for tt in range(8):
                    if R == "sync":
                        eng.dma_start(out=oac[:, tt * 2048:(tt + 1) * 2048],
                                      in_=src[hf * 1024 + tt * 128:hf * 1024 + (tt + 1) * 128, :]
                                      ).then_inc(qs.h, 16)
                    INC("qs", 16)
                for kc in range(16):
                    if R == "sync":
                        eng.dma_start(out=xnh[:, kc * 1024:(kc + 1) * 1024],
                                      in_=pubT[kc * 128:(kc + 1) * 128,
                                               hf * 1024:(hf + 1) * 1024]).then_inc(qs.h, 16)
                    INC("qs", 16)
                MK(("m_ld", hf), "qs")

                if R == "tensor":
                    WT("qs", ("m_ld", hf))
                    WT("qg", ("m_pre", "qg"))
                    if hf == 0:
                        WT("act", base["act"])
                        WT("dve", base["dve"])
                    if hf == 1:
                        WT("dve", ("m_yev", hf - 1, MOE_EXPERTS - 1, 3))
                    for kc in range(16):
                        for tt in range(8):
                            ins = eng.matmul(ps[tt][:, 0:8],
                                             xnh[:, kc * 1024 + tt * 128:kc * 1024 + (tt + 1) * 128],
                                             gwf_sb[:, kc * 8:(kc + 1) * 8],
                                             start=(kc == 0), stop=(kc == 15))
                            if kc == 15:
                                ins.then_inc(pe.h, 1)
                for tt in range(8):
                    INC("pe", 1)
                    MK(("m_rpe", hf, tt), "pe")
                for tt in range(8):
                    sl8 = slice(tt * 8, (tt + 1) * 8)
                    c = lambda j: rst[:, tt * 6 + j:tt * 6 + j + 1]
                    if R == "scalar":
                        WT("pe", ("m_rpe", hf, tt))
                        eng.activation(lg[:, sl8], ps[tt][:, 0:8], AF.Copy).then_inc(act.h, 1)
                    INC("act", 1)
                    MK(("m_rev", hf, tt), "act")
                    if R == "vector":
                        WT("act", ("m_rev", hf, tt))
                        eng.reduce_max(c(0), lg[:, sl8], axis=AX.X).then_inc(dve.h, 1)
                        sems["dve"].wait(eng, C["dve"] + 1)
                        eng.tensor_scalar(c(1), c(0), -1.0, None, ALU.mult).then_inc(dve.h, 1)
                    INC("dve", 2)
                    MK(("m_rneg", hf, tt), "dve")
                    if R == "scalar":
                        WT("dve", ("m_rneg", hf, tt))
                        eng.activation(rb[:, sl8], lg[:, sl8], AF.Exp,
                                       bias=c(1)).then_inc(act.h, 1)
                    INC("act", 1)
                    MK(("m_rexp", hf, tt), "act")
                    if R == "vector":
                        WT("act", ("m_rexp", hf, tt))
                        eng.reduce_sum(c(2), rb[:, sl8], axis=AX.X).then_inc(dve.h, 1)
                        sems["dve"].wait(eng, C["dve"] + 1)
                        eng.reciprocal(c(3), c(2)).then_inc(dve.h, 1)
                        eng.reduce_max(c(4), rb[:, sl8], axis=AX.X).then_inc(dve.h, 1)
                        sems["dve"].wait(eng, C["dve"] + 3)
                        eng.tensor_scalar(rb[:, 64 + tt * 8:64 + (tt + 1) * 8], rb[:, sl8],
                                          c(4), None, ALU.subtract).then_inc(dve.h, 1)
                    INC("dve", 4)
                    MK(("m_rd", hf, tt), "dve")
                    if R == "scalar":
                        WT("dve", ("m_rd", hf, tt))
                        eng.activation(rb[:, 128 + tt * 8:128 + (tt + 1) * 8],
                                       rb[:, 64 + tt * 8:64 + (tt + 1) * 8],
                                       AF.Sign).then_inc(act.h, 1)
                    INC("act", 1)
                    MK(("m_rsg", hf, tt), "act")
                    if R == "vector":
                        WT("act", ("m_rsg", hf, tt))
                        eng.tensor_scalar(rb[:, 64 + tt * 8:64 + (tt + 1) * 8],
                                          rb[:, 128 + tt * 8:128 + (tt + 1) * 8], -1.0,
                                          None, ALU.mult).then_inc(dve.h, 1)
                        sems["dve"].wait(eng, C["dve"] + 1)
                        eng.tensor_mul(rb[:, 192 + tt * 8:192 + (tt + 1) * 8],
                                       rb[:, 64 + tt * 8:64 + (tt + 1) * 8],
                                       rb[:, sl8]).then_inc(dve.h, 1)
                        sems["dve"].wait(eng, C["dve"] + 2)
                        eng.reduce_max(c(5), rb[:, 192 + tt * 8:192 + (tt + 1) * 8],
                                       axis=AX.X).then_inc(dve.h, 1)
                        sems["dve"].wait(eng, C["dve"] + 3)
                        eng.tensor_scalar(rb[:, 64 + tt * 8:64 + (tt + 1) * 8], rb[:, sl8],
                                          c(5), None, ALU.is_ge).then_inc(dve.h, 1)
                        eng.tensor_scalar(rb[:, 192 + tt * 8:192 + (tt + 1) * 8], rb[:, sl8],
                                          c(3), None, ALU.mult).then_inc(dve.h, 1)
                        sems["dve"].wait(eng, C["dve"] + 5)
                        eng.tensor_mul(cwsb[:, sl8],
                                       rb[:, 192 + tt * 8:192 + (tt + 1) * 8],
                                       rb[:, 64 + tt * 8:64 + (tt + 1) * 8]).then_inc(dve.h, 1)
                    INC("dve", 6)
                    MK(("m_rcw", hf, tt), "dve")

                for e in range(MOE_EXPERTS):
                    shared = (e == 0)
                    sls = SH if shared else RT
                    nch = 22 if shared else 11
                    if shared:
                        w1a, w3a, w2a = W["sw1"], W["sw3"], W["sw2"]
                    else:
                        w1a, w3a, w2a = W["rw1"][e - 1], W["rw3"][e - 1], W["rw2"][e - 1]

                    for si, (f0, fw) in enumerate(sls):
                        nsub = fw // 128
                        for ph, wa in ((0, w1a), (1, w3a)):
                            wav = wa.rearrange("(c p) f -> p c f", p=128)
                            for g in range(4):
                                wti = C["wti"] = C.get("wti", 0) + 1
                                slot = (wti - 1) % 4
                                sn = f"wt{slot}"
                                if R == "gpsimd":
                                    if (("m_wfree", wti - 4) in marks) if R != "sim" else (wti > 4):
                                        WT("pe", ("m_wfree", wti - 4))
                                    wdst = bass.AP(wt, slot * 2048,
                                                   [[4 * 2048, 128], [512, 4], [1, fw]])
                                    eng.dma_start(
                                        out=wdst,
                                        in_=wav[:, g * 4:(g + 1) * 4, f0:f0 + fw]
                                    ).then_inc(sems[sn].h, 16)
                                INC(sn, 16)
                                MK(("m_wld", wti), sn)
                                if R == "tensor":
                                    WT(sn, ("m_wld", wti))
                                    if ph == 0 and g == 0:
                                        if e == 0 and si == 0:
                                            WT("act", ("m_rev", hf, 7))
                                            WT("dve", ("m_rcw", hf, 7))
                                        elif si == 0:
                                            WT("dve", ("m_yev", hf, e - 1, 3))
                                        else:
                                            WT("act", ("m_tev", hf, e, si - 1, 7))
                                    if ph == 1 and g == 0:
                                        WT("act", ("m_aev", hf, e, si, 7))
                                    for j in range(4):
                                        kc = g * 4 + j
                                        for tt in range(8):
                                            ins = eng.matmul(
                                                ps[tt][:, 0:fw],
                                                xnh[:, kc * 1024 + tt * 128:kc * 1024 + (tt + 1) * 128],
                                                wt[:, slot * 2048 + j * 512:slot * 2048 + j * 512 + fw],
                                                start=(kc == 0), stop=(kc == 15))
                                            if tt == 7 and j == 3:
                                                ins.then_inc(pe.h, 1)
                                INC("pe", 1)
                                MK(("m_wfree", wti), "pe")
                                if g == 3:
                                    MK(("m_gstop", hf, e, si, ph), "pe")
                            if ph == 0:
                                for tt in range(8):
                                    if R == "scalar":
                                        WT("pe", ("m_gstop", hf, e, si, 0))
                                        if si > 0 or e > 0:
                                            WT("dve", ("m_hev", hf, e, si - 1, 7) if si > 0
                                               else ("m_hev", hf, e - 1,
                                                     (len(SH) if e == 1 else len(RT)) - 1, 7))
                                        elif hf > 0:
                                            lsi = (len(SH) if MOE_EXPERTS == 1 else len(RT)) - 1
                                            WT("dve", ("m_hev", hf - 1, MOE_EXPERTS - 1, lsi, 7))
                                        eng.activation(s_sb[:, tt * 512:tt * 512 + fw],
                                                       ps[tt][:, 0:fw],
                                                       AF.Silu).then_inc(act.h, 1)
                                    INC("act", 1)
                                    MK(("m_aev", hf, e, si, tt), "act")
                        c0 = f0 // 128
                        for tt in range(8):
                            if R == "vector":
                                WT("pe", ("m_gstop", hf, e, si, 1))
                                if (hf, e, si, tt) != (0, 0, 0, tt):
                                    pass
                                if tt >= 2:
                                    WT("act", ("m_tev", hf, e, si, tt - 2))
                                if e == 0:
                                    eng.tensor_tensor(htmp[:, (tt % 2) * 512:(tt % 2) * 512 + fw],
                                                      ps[tt][:, 0:fw],
                                                      s_sb[:, tt * 512:tt * 512 + fw],
                                                      ALU.mult).then_inc(dve.h, 1)
                                else:
                                    eng.tensor_tensor(htA[:, (tt % 2) * 512:(tt % 2) * 512 + fw],
                                                      ps[tt][:, 0:fw],
                                                      s_sb[:, tt * 512:tt * 512 + fw],
                                                      ALU.mult)
                                    eng.tensor_scalar(
                                        htmp[:, (tt % 2) * 512:(tt % 2) * 512 + fw],
                                        htA[:, (tt % 2) * 512:(tt % 2) * 512 + fw],
                                        cwsb[:, tt * 8 + e - 1:tt * 8 + e], None,
                                        ALU.mult).then_inc(dve.h, 1)
                            INC("dve", 1)
                            MK(("m_hev", hf, e, si, tt), "dve")
                            if R == "tensor":
                                WT("dve", ("m_hev", hf, e, si, tt))
                                for j in range(nsub):
                                    ins = eng.transpose(
                                        ps[tt][:, j * 128:(j + 1) * 128],
                                        htmp[:, (tt % 2) * 512 + j * 128:(tt % 2) * 512 + (j + 1) * 128],
                                        ident32[:, :])
                                    if j == nsub - 1:
                                        ins.then_inc(pe.h, 1)
                            INC("pe", 1)
                            MK(("m_tpe", hf, e, si, tt), "pe")
                            if R == "scalar":
                                WT("pe", ("m_tpe", hf, e, si, tt))
                                dst_ap = bass.AP(h_fm, c0 * 1024 + tt * 128,
                                                 [[22 * 1024, 128], [1024, nsub], [1, 128]])
                                eng.activation(
                                    dst_ap,
                                    ps[tt][:, 0:nsub * 128].rearrange("p (s t) -> p s t", t=128),
                                    AF.Copy).then_inc(act.h, 1)
                            INC("act", 1)
                            MK(("m_tev", hf, e, si, tt), "act")

                    w2v = w2a.rearrange("(c p) d -> p c d", p=128)
                    ngr = (nch + 3) // 4
                    for ds in range(4):
                        for g in range(ngr):
                            gs = min(4, nch - 4 * g)
                            wti = C["wti"] = C.get("wti", 0) + 1
                            slot = (wti - 1) % 4
                            sn = f"wt{slot}"
                            if R == "gpsimd":
                                WT("pe", ("m_wfree", wti - 4))
                                wdst = bass.AP(wt, slot * 2048,
                                               [[4 * 2048, 128], [512, gs], [1, 512]])
                                eng.dma_start(out=wdst,
                                              in_=w2v[:, g * 4:g * 4 + gs,
                                                      ds * 512:(ds + 1) * 512]).then_inc(sems[sn].h, 16)
                            INC(sn, 16)
                            MK(("m_wld", wti), sn)
                            if R == "tensor":
                                WT(sn, ("m_wld", wti))
                                if g == 0:
                                    WT("act", ("m_tev", hf, e, len(sls) - 1, 7))
                                    if ds > 0:
                                        WT("dve", ("m_yev", hf, e, ds - 1, 7))
                                for j in range(gs):
                                    ic = g * 4 + j
                                    for tt in range(8):
                                        ins = eng.matmul(
                                            ps[tt][:, 0:512],
                                            h_fm[:, ic * 1024 + tt * 128:ic * 1024 + (tt + 1) * 128],
                                            wt[:, slot * 2048 + j * 512:slot * 2048 + (j + 1) * 512],
                                            start=(ic == 0), stop=(ic == nch - 1))
                                        if tt == 7 and j == gs - 1:
                                            ins.then_inc(pe.h, 1)
                            INC("pe", 1)
                            MK(("m_wfree", wti), "pe")
                            if g == ngr - 1:
                                MK(("m_ygstop", hf, e, ds), "pe")
                        for tt in range(8):
                            if R == "vector":
                                WT("pe", ("m_ygstop", hf, e, ds))
                                if e == 0 and ds == 0 and tt == 0:
                                    WT("qs", ("m_ld", hf))
                                eng.tensor_tensor(
                                    oac[:, tt * 2048 + ds * 512:tt * 2048 + (ds + 1) * 512],
                                    ps[tt][:, 0:512],
                                    oac[:, tt * 2048 + ds * 512:tt * 2048 + (ds + 1) * 512],
                                    ALU.add).then_inc(dve.h, 1)
                            INC("dve", 1)
                            if tt in (3, 7):
                                MK(("m_yev", hf, e, ds, tt), "dve")
                        MK(("m_yev", hf, e, ds), "dve")
                MK(("m_peend", hf), "pe")

                for tt in range(8):
                    if R == "scalar":
                        WT("dve", ("m_yev", hf, MOE_EXPERTS - 1, 3))
                        WT("qs", ("m_ld", hf))
                        eng.dma_start(out=dst[hf * 1024 + tt * 128:hf * 1024 + (tt + 1) * 128, :],
                                      in_=oac[:, tt * 2048:(tt + 1) * 2048]).then_inc(qc.h, 16)
                    INC("qc", 16)
                if MOE_DBG:
                    if R == "gpsimd":
                        WT("dve", ("m_rcw", hf, 7))
                        eng.dma_start(out=pubT[0:128, 1900 + hf * 64:1900 + (hf + 1) * 64],
                                      in_=cwsb[:, :]).then_inc(qg.h, 16)
                        eng.dma_start(out=pubT[128:256, 1900 + hf * 64:1900 + (hf + 1) * 64],
                                      in_=lg[:, :]).then_inc(qg.h, 16)
                        eng.dma_start(out=pubT[256:384, 1900 + hf * 64:1900 + (hf + 1) * 64],
                                      in_=rb[:, 0:64]).then_inc(qg.h, 16)
                    INC("qg", 32)
                MK(("m_st", hf), "qc")

            if MOE_DBG2:
                if R == "gpsimd":
                    WT("dve", ("m_yev", MOE_HALVES - 1, MOE_EXPERTS - 1, 3))
                    for i in range(11):
                        eng.dma_start(out=pubT[i * 128:(i + 1) * 128, :],
                                      in_=h_fm[:, i * 2048:(i + 1) * 2048]).then_inc(qg.h, 16)
                    for i in range(2):
                        eng.dma_start(out=pubT[(11 + i) * 128:(12 + i) * 128, :],
                                      in_=s_sb[:, i * 2048:(i + 1) * 2048]).then_inc(qg.h, 16)
                    for i in range(2):
                        eng.dma_start(out=pubT[(13 + i) * 128:(14 + i) * 128, :],
                                      in_=xnh[:, i * 2048:(i + 1) * 2048]).then_inc(qg.h, 16)
                INC("qg", 16 * 15)
            if R == "sim":
                for k in sems:
                    marks[("m_end", k)] = C[k]

        body("sim")
        with nc.Block() as blk:
            @blk.sync
            def _(sy):
                body("sync", sy)

            @blk.gpsimd
            def _(g):
                body("gpsimd", g)

            @blk.tensor
            def _(te):
                body("tensor", te)

            @blk.scalar
            def _(sc):
                body("scalar", sc)

            @blk.vector
            def _(v):
                body("vector", v)

        for k in sems:
            sems[k].n = marks[("m_end", k)]


def _tern_prepass(nc, W, sems, ones_c, ones_r, pcol, prow):
    import contextlib
    qc = sems["qc"]
    pe, act, dve = sems["pe"], sems["act"], sems["dve"]

    pieces = []
    for kc in range(16):
        for half in range(2):
            pieces.append(("wfcT", kc * 128, half * 2560, 2560))
    for kc in range(40):
        pieces.append(("wpjT", kc * 128, 0, 2048))
    NP_ = len(pieces)
    NEL = float(D * FT)

    with contextlib.ExitStack() as st:
        tb = st.enter_context(nc.sbuf_tensor("T_tb", [128, 2 * 2560], F32))
        stf = st.enter_context(nc.sbuf_tensor("T_stf", [128, 80], F32))
        p1 = st.enter_context(nc.psum_tensor("T_p1", [1, 1], F32))
        pb = st.enter_context(nc.psum_tensor("T_pb", [128, 1], F32))
        qc0, dve0, act0, pe0 = qc.n, dve.n, act.n, pe.n
        tw0, tw1 = sems["wt0"].n, sems["wt1"].n

        with nc.Block() as blk:
            @blk.scalar
            def _(sc):
                qc.wait(sc, qc0)
                for i, (nm, r0, c0, cw) in enumerate(pieces):
                    if i >= 2:
                        dve.wait(sc, dve0 + i - 1)
                    ws = sems["wt" + str(i % 2)]
                    sc.dma_start(out=tb[:, (i % 2) * 2560:(i % 2) * 2560 + cw],
                                 in_=W[nm][r0:r0 + 128, c0:c0 + cw]).then_inc(ws.h, 16)
                    ws.n += 16
                pe.wait(sc, pe0 + 1)
                sc.activation(prow[0:1, 130:131], p1[:, :], AF.Copy,
                              scale=1.0 / NEL).then_inc(act.h, 1)
                dve.wait(sc, dve0 + NP_ + 3)
                sc.mul(prow[0:1, 132:133], prow[0:1, 131:132], 0.7).then_inc(act.h, 1)
                pe.wait(sc, pe0 + 2)
                sc.activation(prow[0:1, 140:141], p1[:, :], AF.Copy,
                              scale=1.0 / NEL).then_inc(act.h, 1)
                dve.wait(sc, dve0 + NP_ + 4)
                sc.mul(prow[0:1, 142:143], prow[0:1, 141:142], 0.7).then_inc(act.h, 1)
                sc.activation(prow[0:1, 143:144], prow[0:1, 131:132],
                              AF.Square).then_inc(act.h, 1)
                act.n += 5
                for j, (col, dst0) in enumerate([(132, 0), (142, 2), (144, 4)]):
                    pe.wait(sc, pe0 + 3 + j)
                    sc.activation(pcol[:, dst0:dst0 + 1], pb[:, :], AF.Copy).then_inc(act.h, 1)
                    act.n += 1
                    if dst0 < 4:
                        sc.activation(pcol[:, dst0 + 1:dst0 + 2], pb[:, :], AF.Copy,
                                      scale=-1.0).then_inc(act.h, 1)
                        act.n += 1

            @blk.vector
            def _(v):
                qc.wait(v, qc0)
                for i in range(NP_):
                    cw = pieces[i][3]
                    ws = sems["wt" + str(i % 2)]
                    ws.wait(v, (tw0 if i % 2 == 0 else tw1) + 16 * (i // 2 + 1))
                    v.tensor_reduce(stf[:, i:i + 1],
                                    tb[:, (i % 2) * 2560:(i % 2) * 2560 + cw],
                                    AX.X, ALU.add,
                                    apply_absolute_value=True).then_inc(dve.h, 1)
                    dve.n += 1
                v.tensor_reduce(stf[:, 72:73], stf[:, 0:32], AX.X, ALU.add).then_inc(dve.h, 1)
                v.tensor_reduce(stf[:, 73:74], stf[:, 32:72], AX.X, ALU.add).then_inc(dve.h, 1)
                dve.n += 2
                act.wait(v, act0 + 1)
                v.tensor_scalar(prow[0:1, 131:132], prow[0:1, 130:131], 1e-5, None,
                                ALU.max).then_inc(dve.h, 1)
                act.wait(v, act0 + 3)
                v.tensor_scalar(prow[0:1, 141:142], prow[0:1, 140:141], 1e-5, None,
                                ALU.max).then_inc(dve.h, 1)
                act.wait(v, act0 + 5)
                v.tensor_mul(prow[0:1, 144:145], prow[0:1, 143:144],
                             prow[0:1, 141:142]).then_inc(dve.h, 1)
                dve.n += 3

            @blk.tensor
            def _(te):
                act.wait(te, act0)
                dve.wait(te, dve0)
                dve.wait(te, dve0 + NP_ + 2)
                te.matmul(p1[:, :], ones_c[:, :], stf[:, 72:73], start=True,
                          stop=True).then_inc(pe.h, 1)
                act.wait(te, act0 + 1)
                te.matmul(p1[:, :], ones_c[:, :], stf[:, 73:74], start=True,
                          stop=True).then_inc(pe.h, 1)
                pe.n += 2
                waits = [(act0 + 2, dve0 + NP_ + 3), (act0 + 5, dve0 + NP_ + 4),
                         (act0 + 5, dve0 + NP_ + 5)]
                for j, (col, (aw, dw)) in enumerate(zip((132, 142, 144), waits)):
                    act.wait(te, aw)
                    dve.wait(te, dw)
                    if j > 0:
                        act.wait(te, act0 + 5 + 2 * j)
                    te.matmul(pb[:, :], ones_r[:, :], prow[0:1, col:col + 1],
                              start=True, stop=True).then_inc(pe.h, 1)
                    pe.n += 1


def _tern_quant(nc, W, sems, pcol, qfcT, qpjT):
    import contextlib
    qc = sems["qc"]
    pool, dve = sems["pool"], sems["dve"]

    pieces = []
    for kc in range(16):
        for half in range(2):
            pieces.append((W["wfcT"], qfcT, kc * 128, half * 2560, 2560, 0))
    for kc in range(40):
        pieces.append((W["wpjT"], qpjT, kc * 128, 0, 2048, 2))
    NP_ = len(pieces)

    with contextlib.ExitStack() as st:
        tb = st.enter_context(nc.sbuf_tensor("Q_tb", [128, 2 * 2560], F32))
        gl = st.enter_context(nc.sbuf_tensor("Q_gl", [128, 4 * 2560], BF16))
        qt_ = st.enter_context(nc.sbuf_tensor("Q_qt", [128, 2 * 2560], BF16))
        qc0, pool0, dve0 = qc.n, pool.n, dve.n
        qw2, qw3 = sems["wt2"].n, sems["wt3"].n

        with nc.Block() as blk:
            @blk.scalar
            def _(sc):
                qc.wait(sc, qc0)
                for i, (srcT, dstT, r0, c0, cw, tc) in enumerate(pieces):
                    if i >= 2:
                        dve.wait(sc, dve0 + 2 * (i - 1) - 1)
                    ws = sems["wt" + str(2 + i % 2)]
                    sc.dma_start(out=tb[:, (i % 2) * 2560:(i % 2) * 2560 + cw],
                                 in_=srcT[r0:r0 + 128, c0:c0 + cw]).then_inc(ws.h, 16)
                    ws.n += 16
                    dve.wait(sc, dve0 + 2 * (i + 1))
                    sc.dma_start(out=dstT[r0:r0 + 128, c0:c0 + cw],
                                 in_=qt_[:, (i % 2) * 2560:(i % 2) * 2560 + cw]
                                 ).then_inc(qc.h, 16)
                    qc.n += 16

            @blk.vector
            def _(v):
                qc.wait(v, qc0)
                for i, (srcT, dstT, r0, c0, cw, tc) in enumerate(pieces):
                    ws = sems["wt" + str(2 + i % 2)]
                    ws.wait(v, (qw2 if i % 2 == 0 else qw3) + 16 * (i // 2 + 1))
                    if i >= 1:
                        qc.wait(v, qc0 + 16 * i)
                    sl = slice((i % 2) * 2560, (i % 2) * 2560 + cw)
                    v.tensor_scalar(gl[:, (i % 2) * 2560:(i % 2) * 2560 + cw],
                                    tb[:, sl], pcol[:, tc:tc + 1], None, ALU.is_gt)
                    v.tensor_scalar(gl[:, 5120 + (i % 2) * 2560:5120 + (i % 2) * 2560 + cw],
                                    tb[:, sl], pcol[:, tc + 1:tc + 2], None,
                                    ALU.is_lt).then_inc(dve.h, 1)
                    dve.n += 1
                    v.tensor_sub(qt_[:, (i % 2) * 2560:(i % 2) * 2560 + cw],
                                 gl[:, (i % 2) * 2560:(i % 2) * 2560 + cw],
                                 gl[:, 5120 + (i % 2) * 2560:5120 + (i % 2) * 2560 + cw]
                                 ).then_inc(dve.h, 1)
                    dve.n += 1


def _phase_attn(nc, lay, gatT, pubaT, gataT, W, sems, ident16, ones_c16,
                ones_r, pcol, prope_s, cc_idx):
    import contextlib
    qs, qc, qg, cc = sems["qs"], sems["qc"], sems["qg"], sems["cc"]
    pe, act, dve = sems["pe"], sems["act"], sems["dve"]
    base = {k: s.n for k, s in sems.items()}
    marks = {}
    L1 = (lay == 1)
    PR = PUB1 if L1 else PUB0
    S0 = 1.0 / np.sqrt(128.0)

    with contextlib.ExitStack() as st:
        wsl = st.enter_context(nc.sbuf_tensor(f"P{lay}_wsl", [128, 48 * 128], BF16))
        gwt = st.enter_context(nc.sbuf_tensor(f"P{lay}_gwt", [128, 128], BF16))
        xnb = st.enter_context(nc.sbuf_tensor(f"P{lay}_xnb", [128, 16 * 512], BF16))
        kh_a = st.enter_context(nc.sbuf_tensor(f"P{lay}_kha", [128, S], BF16))
        qh_a = st.enter_context(nc.sbuf_tensor(f"P{lay}_qha", [128, S], BF16))
        vT_h = st.enter_context(nc.sbuf_tensor(f"P{lay}_vth", [128, S], BF16))
        vtmp = st.enter_context(nc.sbuf_tensor(f"P{lay}_vtmp", [128, 512], BF16))
        probsT = st.enter_context(nc.sbuf_tensor(f"P{lay}_pT", [128, 2 * 512], BF16))
        ptmp = st.enter_context(nc.sbuf_tensor(f"P{lay}_ptmp", [128, 512], BF16))
        rs_sb = st.enter_context(nc.sbuf_tensor(f"P{lay}_rs", [1, 512], F32))
        rb_sb = st.enter_context(nc.sbuf_tensor(f"P{lay}_rb", [128, 512], F32))
        fac = st.enter_context(nc.sbuf_tensor(f"P{lay}_fac", [128, 512], F32))
        av_sb = st.enter_context(nc.sbuf_tensor(f"P{lay}_av", [128, 2 * 512], BF16))
        if L1:
            kh_b = st.enter_context(nc.sbuf_tensor("P1_khb", [128, S], BF16))
            qh_b = st.enter_context(nc.sbuf_tensor("P1_qhb", [128, S], BF16))
            ckvb = st.enter_context(nc.sbuf_tensor("P1_ckvb", [128, 4 * 512], BF16))
            cqb = st.enter_context(nc.sbuf_tensor("P1_cqb", [128, 8 * 512], BF16))
            tabb = st.enter_context(nc.sbuf_tensor("P1_tab", [128, 4 * 512], F32))
            kraw = st.enter_context(nc.sbuf_tensor("P1_kraw", [128, 512], BF16))
            tmp1 = st.enter_context(nc.sbuf_tensor("P1_t1", [128, 512], F32))
            tmp2 = st.enter_context(nc.sbuf_tensor("P1_t2", [128, 512], F32))
        else:
            gate_sb = st.enter_context(nc.sbuf_tensor("P0_gate", [128, 512], F32))
        P = [st.enter_context(nc.psum_tensor(f"P{lay}_P{i}", [128, 512], F32))
             for i in range(5)]
        PT = st.enter_context(nc.psum_tensor(f"P{lay}_PT", [128, 512], BF16))
        PS = st.enter_context(nc.psum_tensor(f"P{lay}_PS", [1, 512], F32))
        MB = st.enter_context(nc.sbuf_tensor(f"P{lay}_mb", [128, 4 * 512], BF16))

        if L1:
            WTAB = [("wukT", 4, 0, "ckvb"), ("wuvT", 4, 4, "ckvb"),
                    ("wuqT", 8, 8, "cqb"), ("wkrT", 16, 16, "xnb")]
        else:
            WTAB = [("wqT", 16, 0, "xnb"), ("wkT", 16, 16, "xnb"),
                    ("wvT", 16, 32, "xnb")]

        def body(R, eng=None):
            C = dict(base)

            def INC(sname, k, ins=None):
                C[sname] += k
                if ins is not None:
                    ins.then_inc(sems[sname].h, k)
                return C[sname]

            def MK(key, sname):
                if R == "sim":
                    marks[key] = C[sname]

            def WT(sname, key_or_val):
                if R == "sim":
                    return
                v = marks[key_or_val] if isinstance(key_or_val, tuple) else key_or_val
                sems[sname].wait(eng, v)

            if R != "sim":
                sems["qc"].wait(eng, base["qc"])
            if R == "sync":
                eng.dma_start(out=MB[:, :], in_=W["masks"]).then_inc(qs.h, 16)
            INC("qs", 16)
            MK(("p_mb",), "qs")

            for h in range(HPC):
                if h > 0:
                    WT("pe", ("p_hd_pe", h - 1))
                for mi, (nm, nkc, tb0, _r) in enumerate(WTAB):
                    for kc in range(nkc):
                        if R == "gpsimd":
                            eng.dma_start(
                                out=wsl[:, (tb0 + kc) * 128:(tb0 + kc + 1) * 128],
                                in_=W[nm][kc * 128:(kc + 1) * 128, h * 128:(h + 1) * 128]
                            ).then_inc(qg.h, 16)
                        INC("qg", 16)
                if lay == 0 and h == 0:
                    if R == "gpsimd":
                        eng.dma_start(out=gwt[:, :], in_=W["gwT"]).then_inc(qg.h, 16)
                    INC("qg", 16)
                MK(("p_wsl", h), "qg")

                for ts in range(16):
                    r, lc = ts // 4, (ts % 4) * 512
                    if R == "sync":
                        if h == 0 and ts == 0:
                            WT("cc", cc_idx)
                        if not (h == 0 and ts == 0):
                            WT("pe", ("p_pj", h, ts - 1) if ts > 0 else ("p_pj", h - 1, 15))
                            if L1:
                                WT("dve", ("p_rpq", h, ts - 1) if ts > 0 else ("p_rpq", h - 1, 15))
                        for kc in range(16):
                            eng.dma_start(out=xnb[:, kc * 512:(kc + 1) * 512],
                                          in_=gatT[r * PR + kc * 128:r * PR + (kc + 1) * 128,
                                                   lc:lc + 512]).then_inc(qs.h, 16)
                        if L1:
                            for i in range(4):
                                eng.dma_start(out=ckvb[:, i * 512:(i + 1) * 512],
                                              in_=gatT[r * PR + D + i * 128:r * PR + D + (i + 1) * 128,
                                                       lc:lc + 512]).then_inc(qs.h, 16)
                            for i in range(8):
                                eng.dma_start(out=cqb[:, i * 512:(i + 1) * 512],
                                              in_=gatT[r * PR + D + DC + i * 128:
                                                       r * PR + D + DC + (i + 1) * 128,
                                                       lc:lc + 512]).then_inc(qs.h, 16)
                            for ti, tn in enumerate(("cosq", "sinq", "cosk", "sink")):
                                eng.dma_start(out=tabb[:, ti * 512:(ti + 1) * 512],
                                              in_=W[tn][:, ts * 512:(ts + 1) * 512]
                                              ).then_inc(qs.h, 16)
                    INC("qs", 16 * (16 + (16 if L1 else 0)))
                    MK(("p_ld", h, ts), "qs")

                    if R == "tensor":
                        WT("qs", ("p_ld", h, ts))
                        if h == 0 and ts == 0:
                            WT("qg", ("p_wsl", 0))
                            WT("act", base["act"])
                            WT("dve", base["dve"])
                        elif ts == 0:
                            WT("qg", ("p_wsl", h))
                            WT("act", ("p_hd_act", h - 1))
                            WT("dve", ("p_hd_dve", h - 1))
                        if not (h == 0 and ts == 0):
                            WT("act", ("p_pjev", h, ts - 1) if ts > 0 else ("p_pjev", h - 1, 15))
                        if L1:
                            plan = [(0, 0, 4, "ckvb"), (2, 4, 4, "ckvb"),
                                    (3, 8, 8, "cqb"), (1, 16, 16, "xnb")]
                        else:
                            plan = [(0, 0, 16, "xnb"), (1, 16, 16, "xnb"), (2, 32, 16, "xnb")]
                        rhs_map = {"xnb": xnb, "ckvb": (ckvb if L1 else None),
                                   "cqb": (cqb if L1 else None)}
                        for pi, tb0, nkc, rn in plan:
                            rbuf = rhs_map[rn]
                            for kc in range(nkc):
                                ins = eng.matmul(P[pi][:, :],
                                                 wsl[:, (tb0 + kc) * 128:(tb0 + kc + 1) * 128],
                                                 rbuf[:, kc * 512:(kc + 1) * 512],
                                                 start=(kc == 0), stop=(kc == nkc - 1))
                            ins.then_inc(pe.h, 1)
                        INC("pe", len(plan))
                    else:
                        INC("pe", 4 if L1 else 3)
                    MK(("p_pj", h, ts), "pe")

                    if R == "scalar":
                        WT("pe", ("p_pj", h, ts))
                        if L1 and not (h == 0 and ts == 0):
                            WT("dve", ("p_rpk", h, ts - 1) if ts > 0 else ("p_rpk", h - 1, 15))
                        if L1:
                            eng.activation(kh_a[:, ts * 512:(ts + 1) * 512], P[0][:, :],
                                           AF.Identity, bias=pcol[:, 18 + h:19 + h],
                                           scale=1.0 / 16.0).then_inc(act.h, 1)
                            eng.activation(vtmp[:, :], P[2][:, :], AF.Identity,
                                           bias=pcol[:, 22 + h:23 + h]).then_inc(act.h, 1)
                            eng.activation(qh_a[:, ts * 512:(ts + 1) * 512], P[3][:, :],
                                           AF.Identity, bias=pcol[:, 26 + h:27 + h]
                                           ).then_inc(act.h, 1)
                            eng.activation(kraw[:, :], P[1][:, :], AF.Identity,
                                           bias=pcol[:, 30 + h:31 + h]).then_inc(act.h, 1)
                        else:
                            eng.activation(qh_a[:, ts * 512:(ts + 1) * 512], P[0][:, :],
                                           AF.Copy).then_inc(act.h, 1)
                            eng.activation(kh_a[:, ts * 512:(ts + 1) * 512], P[1][:, :],
                                           AF.Copy, scale=S0).then_inc(act.h, 1)
                            eng.activation(vtmp[:, :], P[2][:, :], AF.Copy).then_inc(act.h, 1)
                    INC("act", 4 if L1 else 3)
                    MK(("p_pjev", h, ts), "act")

                    if R == "tensor":
                        WT("act", ("p_pjev", h, ts))
                        if not (h == 0 and ts == 0):
                            WT("act", ("p_vtev", h, ts - 1) if ts > 0 else ("p_vtev", h - 1, 15))
                        for j in range(4):
                            ins = eng.transpose(PT[:, j * 128:(j + 1) * 128],
                                                vtmp[:, j * 128:(j + 1) * 128],
                                                ident16[:, :])
                        ins.then_inc(pe.h, 1)
                    INC("pe", 1)
                    MK(("p_vt", h, ts), "pe")
                    if R == "scalar":
                        WT("pe", ("p_vt", h, ts))
                        eng.activation(vT_h[:, ts * 512:(ts + 1) * 512], PT[:, :],
                                       AF.Copy).then_inc(act.h, 1)
                    INC("act", 1)
                    MK(("p_vtev", h, ts), "act")

                    if L1:
                        if R == "tensor":
                            WT("act", ("p_pjev", h, ts))
                            if not (h == 0 and ts == 0):
                                WT("dve", ("p_rpq", h, ts - 1) if ts > 0 else ("p_rpq", h - 1, 15))
                            eng.matmul(P[4][:, :], prope_s[:, :], kraw[:, :],
                                       start=True, stop=True).then_inc(pe.h, 1)
                        INC("pe", 1)
                        MK(("p_rotk", h, ts), "pe")
                        if R == "vector":
                            WT("act", ("p_pjev", h, ts))
                            WT("qs", ("p_ld", h, ts))
                            eng.tensor_mul(tmp1[:, :], kraw[:, :], tabb[:, 2 * 512:3 * 512])
                            WT("pe", ("p_rotk", h, ts))
                            eng.tensor_mul(tmp2[:, :], P[4][:, :], tabb[:, 3 * 512:4 * 512])
                            eng.tensor_add(kh_b[:, ts * 512:(ts + 1) * 512], tmp1[:, :],
                                           tmp2[:, :]).then_inc(dve.h, 1)
                        INC("dve", 1)
                        MK(("p_rpk", h, ts), "dve")
                        if R == "tensor":
                            WT("dve", ("p_rpk", h, ts))
                            eng.matmul(P[4][:, :], prope_s[:, :],
                                       qh_a[:, ts * 512:(ts + 1) * 512],
                                       start=True, stop=True).then_inc(pe.h, 1)
                        INC("pe", 1)
                        MK(("p_rotq", h, ts), "pe")
                        if R == "vector":
                            eng.tensor_mul(tmp1[:, :], qh_a[:, ts * 512:(ts + 1) * 512],
                                           tabb[:, 0:512])
                            WT("pe", ("p_rotq", h, ts))
                            eng.tensor_mul(tmp2[:, :], P[4][:, :], tabb[:, 512:1024])
                            eng.tensor_add(qh_b[:, ts * 512:(ts + 1) * 512], tmp1[:, :],
                                           tmp2[:, :]).then_inc(dve.h, 1)
                        INC("dve", 1)
                        MK(("p_rpq", h, ts), "dve")

                kidx = 0
                for qt in range(16):
                    nkt = 4 * (qt + 1)

                    def emit_scores(kt):
                        ki = kidx + kt
                        if R == "tensor":
                            if ki >= 2:
                                WT("act", ("a_ex", h, ki - 2))
                            if qt == 0 and kt == 0:
                                WT("act", ("p_vtev", h, 15))
                                if L1:
                                    WT("dve", ("p_rpq", h, 15))
                            if L1:
                                eng.matmul(P[ki % 2][:, :],
                                           kh_a[:, kt * 128:(kt + 1) * 128],
                                           qh_a[:, qt * 512:(qt + 1) * 512],
                                           start=True, stop=False)
                                eng.matmul(P[ki % 2][:, :],
                                           kh_b[:, kt * 128:(kt + 1) * 128],
                                           qh_b[:, qt * 512:(qt + 1) * 512],
                                           start=False, stop=True).then_inc(pe.h, 1)
                            else:
                                eng.matmul(P[ki % 2][:, :],
                                           kh_a[:, kt * 128:(kt + 1) * 128],
                                           qh_a[:, qt * 512:(qt + 1) * 512],
                                           start=True, stop=True).then_inc(pe.h, 1)
                        INC("pe", 1)
                        MK(("a_sc", h, ki), "pe")

                    def emit_exp(kt):
                        ki = kidx + kt
                        diag = kt >= 4 * qt
                        j = kt - 4 * qt
                        if R == "scalar":
                            WT("pe", ("a_sc", h, ki))
                            if ki >= 2:
                                WT("pe", ("a_pv", h, ki - 2))
                            if diag:
                                prev = marks.get(("a_dprev", h, ki))
                                if prev is not None:
                                    WT("dve", prev)
                                eng.activation(ptmp[:, :], P[ki % 2][:, :],
                                               AF.Exp).then_inc(act.h, 1)
                            else:
                                eng.activation(probsT[:, (ki % 2) * 512:(ki % 2 + 1) * 512],
                                               P[ki % 2][:, :], AF.Exp).then_inc(act.h, 1)
                        INC("act", 1)
                        MK(("a_ex", h, ki), "act")
                        if diag:
                            if R == "vector":
                                WT("act", ("a_ex", h, ki))
                                eng.tensor_mul(probsT[:, (ki % 2) * 512:(ki % 2 + 1) * 512],
                                               ptmp[:, :],
                                               MB[:, j * 512:(j + 1) * 512]).then_inc(dve.h, 1)
                            INC("dve", 1)
                            MK(("a_mk", h, ki), "dve")
                            if R == "sim":
                                marks[("a_dprev", h, ki + 1 if kt < nkt - 1 else ki + 4 * qt + 5)] = C["dve"]

                    def emit_av(kt):
                        ki = kidx + kt
                        diag = kt >= 4 * qt
                        if R == "tensor":
                            if diag:
                                WT("dve", ("a_mk", h, ki))
                            else:
                                WT("act", ("a_ex", h, ki))
                            if kt == 0:
                                if (h, qt) != (0, 0):
                                    WT("dve", ("a_avm", h, qt - 1) if qt > 0 else ("a_avm", h - 1, 15))
                                    WT("dve", ("a_rc", h, qt - 1) if qt > 0 else ("a_rc", h - 1, 15))
                            eng.matmul(P[2][:, :], vT_h[:, kt * 128:(kt + 1) * 128],
                                       probsT[:, (ki % 2) * 512:(ki % 2 + 1) * 512],
                                       start=(kt == 0), stop=(kt == nkt - 1))
                            ins = eng.matmul(PS[:, :], ones_c16[:, :],
                                             probsT[:, (ki % 2) * 512:(ki % 2 + 1) * 512],
                                             start=(kt == 0), stop=(kt == nkt - 1))
                            ins.then_inc(pe.h, 1)
                        INC("pe", 1)
                        MK(("a_pv", h, ki), "pe")

                    for kt in range(nkt):
                        emit_scores(kt)
                        emit_exp(kt)
                        if kt > 0:
                            emit_av(kt - 1)
                    emit_av(nkt - 1)
                    kidx += nkt
                    if R == "vector":
                        WT("pe", ("a_pv", h, kidx - 1))
                        eng.reciprocal(rs_sb[:, :], PS[:, :]).then_inc(dve.h, 1)
                    INC("dve", 1)
                    MK(("a_rc", h, qt), "dve")
                    if R == "tensor":
                        WT("dve", ("a_rc", h, qt))
                        if (h, qt) != (0, 0):
                            WT("act", ("a_rbev", h, qt - 1) if qt > 0 else ("a_rbev", h - 1, 15))
                        eng.matmul(P[3][:, :], ones_r[:, :], rs_sb[:, :],
                                   start=True, stop=True).then_inc(pe.h, 1)
                        if lay == 0:
                            if (h, qt) != (0, 0):
                                WT("act", ("a_ggev", h, qt - 1) if qt > 0 else ("a_ggev", h - 1, 15))
                            eng.matmul(P[4][:, :], gwt[:, :],
                                       qh_a[:, qt * 512:(qt + 1) * 512],
                                       start=True, stop=True).then_inc(pe.h, 1)
                    INC("pe", 2 if lay == 0 else 1)
                    MK(("a_rb", h, qt), "pe")
                    if R == "scalar":
                        WT("pe", ("a_rb", h, qt))
                        eng.activation(rb_sb[:, :], P[3][:, :], AF.Copy).then_inc(act.h, 1)
                    INC("act", 1)
                    MK(("a_rbev", h, qt), "act")
                    if lay == 0:
                        if R == "scalar":
                            eng.activation(gate_sb[:, :], P[4][:, :], AF.Sigmoid,
                                           bias=pcol[:, 5:6]).then_inc(act.h, 1)
                        INC("act", 1)
                        MK(("a_ggev", h, qt), "act")
                    if R == "vector":
                        WT("act", ("a_rbev", h, qt))
                        if lay == 0:
                            WT("act", ("a_ggev", h, qt))
                            eng.tensor_mul(fac[:, :], gate_sb[:, :], rb_sb[:, :])
                            srcf = fac
                        else:
                            srcf = rb_sb
                        if not (h == 0 and qt < 2):
                            WT("qc", ("a_st", h, qt - 2) if qt >= 2 else ("a_st", h - 1, 14 + qt))
                        eng.tensor_mul(av_sb[:, (qt % 2) * 512:(qt % 2 + 1) * 512],
                                       P[2][:, :], srcf[:, :]).then_inc(dve.h, 1)
                    INC("dve", 1)
                    MK(("a_avm", h, qt), "dve")
                    if R == "scalar":
                        WT("dve", ("a_avm", h, qt))
                        eng.dma_start(out=pubaT[h * 128:(h + 1) * 128,
                                                qt * 512:(qt + 1) * 512],
                                      in_=av_sb[:, (qt % 2) * 512:(qt % 2 + 1) * 512]
                                      ).then_inc(qc.h, 16)
                    INC("qc", 16)
                    MK(("a_st", h, qt), "qc")
                MK(("p_hd_pe", h), "pe")
                MK(("p_hd_act", h), "act")
                MK(("p_hd_dve", h), "dve")

            if R == "gpsimd":
                WT("qc", ("a_st", HPC - 1, 15))
                if ATTN_CC:
                    eng.collective_compute(
                        "AllGather", ALU.bypass, replica_groups=RGROUPS,
                        ins=[pubaT.opt()], outs=[gataT.opt()]).then_inc(cc.h, 1)
                else:
                    eng.engine_nop().then_inc(cc.h, 1)
            INC("cc", 1)
            if R == "sim":
                for k in sems:
                    marks[("p_end", k)] = C[k]

        body("sim")
        with nc.Block() as blk:
            @blk.sync
            def _(sy):
                body("sync", sy)

            @blk.gpsimd
            def _(g):
                body("gpsimd", g)

            @blk.tensor
            def _(te):
                body("tensor", te)

            @blk.scalar
            def _(sc):
                body("scalar", sc)

            @blk.vector
            def _(v):
                body("vector", v)

        for k in sems:
            sems[k].n = marks[("p_end", k)]


def _phase_oproj(nc, gataT, srcT, dstT, woTap, sems, cc_idx, tag=0):
    import contextlib
    qs, qc, qg = sems["qs"], sems["qc"], sems["qg"]
    pe, act, dve = sems["pe"], sems["act"], sems["dve"]
    base = {k: s.n for k, s in sems.items()}
    marks = {}

    with contextlib.ExitStack() as st:
        woT_sb = st.enter_context(nc.sbuf_tensor(f"O_wo{tag}", [128, 16 * 2048], BF16))
        avb = st.enter_context(nc.sbuf_tensor(f"O_avb{tag}", [128, 16 * 1024], BF16))
        x1b = st.enter_context(nc.sbuf_tensor(f"O_x1b{tag}", [128, 2 * 2048], F32))
        outb = st.enter_context(nc.sbuf_tensor(f"O_outb{tag}", [128, 2 * 2048], F32))
        po = [st.enter_context(nc.psum_tensor(f"O_po{tag}_{i}", [128, 512], F32))
              for i in range(4)]

        def body(R, eng=None):
            C = dict(base)

            def INC(sname, k, ins=None):
                C[sname] += k
                if ins is not None:
                    ins.then_inc(sems[sname].h, k)
                return C[sname]

            def MK(key, sname):
                if R == "sim":
                    marks[key] = C[sname]

            def WT(sname, key_or_val):
                if R == "sim":
                    return
                v = marks[key_or_val] if isinstance(key_or_val, tuple) else key_or_val
                sems[sname].wait(eng, v)

            if R != "sim":
                sems["qc"].wait(eng, base["qc"])
            if R == "gpsimd":
                for kc in range(16):
                    eng.dma_start(out=woT_sb[:, kc * 2048:(kc + 1) * 2048],
                                  in_=woTap[kc * 128:(kc + 1) * 128, :]).then_inc(qg.h, 16)
            INC("qg", 16 * 16)
            MK(("o_wo",), "qg")

            if R == "gpsimd":
                own0 = (nc.partition_id() % 4) * 2048
                offs = [own0, own0 + 1024]
                gv = gataT.rearrange("(c p) t -> p c t", p=128)
                av = avb[:, :].rearrange("p (c t) -> p c t", t=1024)

            for tg in range(2):
                if R == "gpsimd":
                    if tg == 0:
                        WT("cc", cc_idx)
                    else:
                        WT("pe", ("o_pj", 7, 3))
                    for grp in range(2):
                        eng.dma_start(out=av[:, grp * 8:(grp + 1) * 8, :],
                                      in_=gv[:, grp * 8:(grp + 1) * 8,
                                             bass.ds(offs[tg], 1024)]
                                      ).then_inc(qg.h, 16)
                INC("qg", 16 * 2)
                MK(("o_avb", tg), "qg")
                for tt8 in range(8):
                    tt = tg * 8 + tt8
                    sxn = "wt" + str(tt % 2)
                    if R == "sync":
                        if tt == 0:
                            WT("qc", base["qc"])
                        if tt >= 2:
                            WT("dve", ("o_ev", tt - 2, 3))
                        eng.dma_start(out=x1b[:, (tt % 2) * 2048:(tt % 2 + 1) * 2048],
                                      in_=srcT[tt * 128:(tt + 1) * 128, :]).then_inc(sems[sxn].h, 16)
                    INC(sxn, 16)
                    MK(("o_x1", tt), sxn)
                    for ds in range(4):
                        if R == "tensor":
                            if tt == 0 and ds == 0 and tg == 0:
                                WT("qg", ("o_wo",))
                                WT("act", base["act"])
                                WT("dve", base["dve"])
                            WT("qg", ("o_avb", tg))
                            if tt8 > 0 or tg > 0:
                                pt = tt - 1
                                WT("dve", ("o_ev", pt, ds))
                            for kc in range(16):
                                ins = eng.matmul(po[ds][:, :],
                                                 avb[:, kc * 1024 + tt8 * 128:kc * 1024 + (tt8 + 1) * 128],
                                                 woT_sb[:, kc * 2048 + ds * 512:kc * 2048 + (ds + 1) * 512],
                                                 start=(kc == 0), stop=(kc == 15))
                            ins.then_inc(pe.h, 1)
                        INC("pe", 1)
                        MK(("o_pj", tt, ds), "pe")
                        if R == "vector":
                            WT("pe", ("o_pj", tt, ds))
                            if ds == 0:
                                WT("wt" + str(tt % 2), ("o_x1", tt))
                                if tt >= 1:
                                    WT("qc", ("o_st", tt - 1))
                            eng.tensor_add(outb[:, (tt % 2) * 2048 + ds * 512:
                                                (tt % 2) * 2048 + (ds + 1) * 512],
                                           po[ds][:, :],
                                           x1b[:, (tt % 2) * 2048 + ds * 512:
                                               (tt % 2) * 2048 + (ds + 1) * 512]
                                           ).then_inc(dve.h, 1)
                        INC("dve", 1)
                        MK(("o_ev", tt, ds), "dve")
                    if R == "scalar":
                        WT("dve", ("o_ev", tt, 3))
                        eng.dma_start(out=dstT[tt * 128:(tt + 1) * 128, :],
                                      in_=outb[:, (tt % 2) * 2048:(tt % 2 + 1) * 2048]
                                      ).then_inc(qc.h, 16)
                    INC("qc", 16)
                    MK(("o_st", tt), "qc")

            if R == "sim":
                for k in sems:
                    marks[("o_end", k)] = C[k]

        body("sim")
        with nc.Block() as blk:
            @blk.sync
            def _(sy):
                body("sync", sy)

            @blk.gpsimd
            def _(g):
                body("gpsimd", g)

            @blk.tensor
            def _(te):
                body("tensor", te)

            @blk.scalar
            def _(sc):
                body("scalar", sc)

            @blk.vector
            def _(v):
                body("vector", v)

        for k in sems:
            sems[k].n = marks[("o_end", k)]


def _phase_ffn1(nc, src, dst, pubT, qfcTap, qpjTap, W, sems, pcol, bo_bc):
    import contextlib
    qs, qc, qg = sems["qs"], sems["qc"], sems["qg"]
    pe, act, dve = sems["pe"], sems["act"], sems["dve"]
    base = {k: s.n for k, s in sems.items()}
    marks = {}

    with contextlib.ExitStack() as st:
        xnh = st.enter_context(nc.sbuf_tensor("F_xnh", [128, 16 * 1024], BF16))
        oac = st.enter_context(nc.sbuf_tensor("F_oac", [128, 8 * 2048], F32))
        rtmp = st.enter_context(nc.sbuf_tensor("F_rt", [128, 2 * 512], F32))
        htmp = st.enter_context(nc.sbuf_tensor("F_ht", [128, 2 * 512], F32))
        ytmp = st.enter_context(nc.sbuf_tensor("F_yt", [128, 2 * 512], F32))
        wt = st.enter_context(nc.sbuf_tensor("F_wt", [128, 4 * 512], BF16))
        h_fm = st.enter_context(nc.sbuf_tensor("F_hfm", [128, 20 * 1024], BF16))
        ident32 = st.enter_context(nc.sbuf_tensor("F_id32", [128, 128], F32))
        ps = [st.enter_context(nc.psum_tensor(f"F_ps{i}", [128, 512], F32))
              for i in range(8)]

        def body(R, eng=None):
            C = dict(base)

            def INC(sname, k, ins=None):
                C[sname] += k
                if ins is not None:
                    ins.then_inc(sems[sname].h, k)
                return C[sname]

            def MK(key, sname):
                if R == "sim":
                    marks[key] = C[sname]

            def WT(sname, key_or_val):
                if R == "sim":
                    return
                v = marks[key_or_val] if isinstance(key_or_val, tuple) else key_or_val
                sems[sname].wait(eng, v)

            if R != "sim":
                sems["qc"].wait(eng, base["qc"])
            if R == "sync":
                eng.dma_start(out=ident32[:, :], in_=W["ident32"]).then_inc(qs.h, 16)
            INC("qs", 16)

            for hf in range(2):
                if hf == 0:
                    if R == "sync":
                        WT("qc", base["qc"])
                else:
                    if R != "sim":
                        WT("qc", ("f_st", 0))
                    if R == "sync":
                        WT("pe", ("f_pe_end", 0))
                for tt in range(8):
                    if R == "sync":
                        eng.dma_start(out=oac[:, tt * 2048:(tt + 1) * 2048],
                                      in_=src[hf * 1024 + tt * 128:hf * 1024 + (tt + 1) * 128, :]
                                      ).then_inc(qs.h, 16)
                    INC("qs", 16)
                MK(("f_oacld", hf), "qs")
                for tt in range(8):
                    if R == "vector":
                        WT("qs", ("f_ld", hf))
                        eng.tensor_add(oac[:, tt * 2048:(tt + 1) * 2048],
                                       oac[:, tt * 2048:(tt + 1) * 2048],
                                       bo_bc[:, :]).then_inc(dve.h, 1)
                    INC("dve", 1)
                MK(("f_bo", hf), "dve")
                for kc in range(16):
                    if R == "sync":
                        eng.dma_start(out=xnh[:, kc * 1024:(kc + 1) * 1024],
                                      in_=pubT[kc * 128:(kc + 1) * 128,
                                               hf * 1024:(hf + 1) * 1024]).then_inc(qs.h, 16)
                    INC("qs", 16)
                MK(("f_ld", hf), "qs")

                for fh in range(2):
                    for si in range(5):
                        for kc in range(16):
                            wti = C["wti"] = C.get("wti", 0) + 1
                            slot = (wti - 1) % 4
                            sn = f"wt{slot}"
                            if R == "sync":
                                if wti > 4:
                                    WT("pe", ("f_wfree", wti - 4))
                                eng.dma_start(out=wt[:, slot * 512:(slot + 1) * 512],
                                              in_=qfcTap[kc * 128:(kc + 1) * 128,
                                                         fh * 2560 + si * 512:
                                                         fh * 2560 + (si + 1) * 512]
                                              ).then_inc(sems[sn].h, 16)
                            INC(sn, 16)
                            MK(("f_wld", wti), sn)
                            if R == "tensor":
                                WT(sn, ("f_wld", wti))
                                if kc == 0:
                                    WT("qs", ("f_ld", hf))
                                    if hf == 0 and fh == 0 and si == 0:
                                        WT("act", base["act"])
                                        WT("dve", base["dve"])
                                    elif si == 0 and fh == 0:
                                        WT("dve", ("f_yev", hf - 1, 1, 3))
                                    elif si == 0:
                                        WT("dve", ("f_yev", hf, fh - 1, 3))
                                    else:
                                        WT("act", ("f_hev", hf, fh, si - 1, 7))
                                for tt in range(8):
                                    ins = eng.matmul(
                                        ps[tt][:, :],
                                        xnh[:, kc * 1024 + tt * 128:kc * 1024 + (tt + 1) * 128],
                                        wt[:, slot * 512:(slot + 1) * 512],
                                        start=(kc == 0), stop=(kc == 15))
                                    if tt == 7:
                                        ins.then_inc(pe.h, 1)
                            INC("pe", 1)
                            MK(("f_wfree", wti), "pe")
                            if kc == 15:
                                MK(("f_gstop", hf, fh, si), "pe")
                        for tt in range(8):
                            if R == "scalar":
                                WT("pe", ("f_gstop", hf, fh, si))
                                if tt >= 2 or si > 0 or fh > 0 or hf > 0:
                                    pk = ("f_tp", hf, fh, si, tt - 2)
                                    if tt < 2:
                                        pk = None
                                    if pk is not None and pk in marks:
                                        WT("pe", pk)
                                eng.activation(rtmp[:, (tt % 2) * 512:(tt % 2 + 1) * 512],
                                               ps[tt][:, :], AF.Relu).then_inc(act.h, 1)
                                eng.activation(htmp[:, (tt % 2) * 512:(tt % 2 + 1) * 512],
                                               rtmp[:, (tt % 2) * 512:(tt % 2 + 1) * 512],
                                               AF.Square).then_inc(act.h, 1)
                            INC("act", 2)
                            MK(("f_sq", hf, fh, si, tt), "act")
                            if R == "tensor":
                                WT("act", ("f_sq", hf, fh, si, tt))
                                for j in range(4):
                                    ins = eng.transpose(
                                        ps[tt][:, j * 128:(j + 1) * 128],
                                        htmp[:, (tt % 2) * 512 + j * 128:(tt % 2) * 512 + (j + 1) * 128],
                                        ident32[:, :])
                                ins.then_inc(pe.h, 1)
                            INC("pe", 1)
                            MK(("f_tp", hf, fh, si, tt), "pe")
                            if R == "scalar":
                                WT("pe", ("f_tp", hf, fh, si, tt))
                                if hf > 0 or fh > 0:
                                    WT("pe", ("f_ygstop", hf if fh > 0 else hf - 1,
                                              0 if fh > 0 else 1, 3))
                                dst_ap = bass.AP(h_fm, si * 4 * 1024 + tt * 128,
                                                 [[20 * 1024, 128], [1024, 4], [1, 128]])
                                eng.activation(dst_ap,
                                               ps[tt][:, :].rearrange("p (s t) -> p s t", t=128),
                                               AF.Copy).then_inc(act.h, 1)
                            INC("act", 1)
                            MK(("f_hev", hf, fh, si, tt), "act")

                    for ds in range(4):
                        for ic in range(20):
                            wti = C["wti"] = C.get("wti", 0) + 1
                            slot = (wti - 1) % 4
                            sn = f"wt{slot}"
                            if R == "sync":
                                WT("pe", ("f_wfree", wti - 4))
                                eng.dma_start(out=wt[:, slot * 512:(slot + 1) * 512],
                                              in_=qpjTap[fh * 2560 + ic * 128:
                                                         fh * 2560 + (ic + 1) * 128,
                                                         ds * 512:(ds + 1) * 512]
                                              ).then_inc(sems[sn].h, 16)
                            INC(sn, 16)
                            MK(("f_wld", wti), sn)
                            if R == "tensor":
                                WT(sn, ("f_wld", wti))
                                if ic == 0:
                                    WT("act", ("f_hev", hf, fh, 4, 7))
                                    if ds > 0:
                                        WT("dve", ("f_yev", hf, fh, ds - 1))
                                for tt in range(8):
                                    ins = eng.matmul(
                                        ps[tt][:, :],
                                        h_fm[:, ic * 1024 + tt * 128:ic * 1024 + (tt + 1) * 128],
                                        wt[:, slot * 512:(slot + 1) * 512],
                                        start=(ic == 0), stop=(ic == 19))
                                    if tt == 7:
                                        ins.then_inc(pe.h, 1)
                            INC("pe", 1)
                            MK(("f_wfree", wti), "pe")
                            if ic == 19:
                                MK(("f_ygstop", hf, fh, ds), "pe")
                        for tt in range(8):
                            if R == "vector":
                                WT("pe", ("f_ygstop", hf, fh, ds))
                                eng.tensor_scalar(ytmp[:, (tt % 2) * 512:(tt % 2 + 1) * 512],
                                                  ps[tt][:, :], pcol[:, 4:5], None,
                                                  ALU.mult)
                                eng.tensor_add(
                                    oac[:, tt * 2048 + ds * 512:tt * 2048 + (ds + 1) * 512],
                                    ytmp[:, (tt % 2) * 512:(tt % 2 + 1) * 512],
                                    oac[:, tt * 2048 + ds * 512:tt * 2048 + (ds + 1) * 512]
                                    ).then_inc(dve.h, 1)
                            INC("dve", 1)
                            if tt in (3, 7):
                                MK(("f_yev", hf, fh, ds, tt), "dve")
                        MK(("f_yev", hf, fh, ds), "dve")
                MK(("f_pe_end", hf), "pe")
                if R == "vector":
                    eng.memset(ytmp[:, 0:4], 0.0).then_inc(dve.h, 1)
                    eng.memset(ytmp[:, 4:8], 0.0).then_inc(dve.h, 1)
                INC("dve", 2)
                MK(("f_flush", hf), "dve")

                for tt in range(8):
                    if R == "scalar":
                        WT("dve", ("f_flush", hf))
                        eng.dma_start(out=dst[hf * 1024 + tt * 128:hf * 1024 + (tt + 1) * 128, :],
                                      in_=oac[:, tt * 2048:(tt + 1) * 2048]).then_inc(qc.h, 16)
                    INC("qc", 16)
                MK(("f_st", hf), "qc")

            if R == "sim":
                for k in sems:
                    marks[("f_end", k)] = C[k]

        body("sim")
        with nc.Block() as blk:
            @blk.sync
            def _(sy):
                body("sync", sy)

            @blk.gpsimd
            def _(g):
                body("gpsimd", g)

            @blk.tensor
            def _(te):
                body("tensor", te)

            @blk.scalar
            def _(sc):
                body("scalar", sc)

            @blk.vector
            def _(v):
                body("vector", v)

        for k in sems:
            sems[k].n = marks[("f_end", k)]


_NC_CACHE = {}


def _get_program(layers, taps=()):
    key = (tuple(layers), tuple(taps))
    if key not in _NC_CACHE:
        _NC_CACHE[key] = build(layers, taps)
    return _NC_CACHE[key]


def _assemble(outs):
    x = np.empty((B, S, D), np.float32)
    for c in range(NCORES):
        b, p = c // 4, c % 4
        x[b, p * TOK:(p + 1) * TOK] = outs[c]
    return x


FUSED = True


def kernel(**inputs):
    ins = {k: np.asarray(v) for k, v in inputs.items()}
    if FUSED:
        maps = host_prep(ins, layers=(0, 1))
        nc = _get_program((0, 1))
        res = run_bass_kernel_spmd(nc, maps, core_ids=list(range(NCORES)))
        return _assemble([res.results[c]["out"] for c in range(NCORES)])
    maps0 = host_prep(ins, layers=(0,))
    res0 = run_bass_kernel_spmd(_get_program((0,)), maps0,
                                core_ids=list(range(NCORES)))
    x1 = _assemble([res0.results[c]["out"] for c in range(NCORES)])
    ins2 = dict(ins)
    ins2["x1"] = x1
    maps1 = host_prep(ins2, layers=(1,))
    res1 = run_bass_kernel_spmd(_get_program((1,)), maps1,
                                core_ids=list(range(NCORES)))
    return _assemble([res1.results[c]["out"] for c in range(NCORES)])


# version 06
# speedup vs baseline: 1.6607x; 1.0901x over previous
import numpy as np
import ml_dtypes
import concourse.bass as bass
import concourse.mybir as mybir
from concourse.bass_utils import run_bass_kernel_spmd

F32 = mybir.dt.float32
BF16 = mybir.dt.bfloat16
AF = mybir.ActivationFunctionType
ALU = mybir.AluOpType
AX = mybir.AxisListType
BF = ml_dtypes.bfloat16

B, S, D = 2, 8192, 2048
NCORES = 8
TOK = 2048
NTT = TOK // 128
HPC = 4
DH = 128
NKC = D // 128
NQT = S // 512
FSH, FRT = 2816, 1408
DC, DQC = 512, 1024
FT = 5120
EPS = 1e-5
MOE_EXPERTS = 9
MOE_HALVES = 2
ATTN_CC = True
MOE_DBG = False
MOE_DBG2 = False
RGROUPS = [[0, 1, 2, 3], [4, 5, 6, 7]]
PUB0, PUB1 = D, D + DC + DQC


class Sem:
    __slots__ = ("h", "n", "maxw", "name")

    def __init__(self, h, name):
        self.h = h
        self.n = 0
        self.maxw = 0
        self.name = name

    def inc(self, ins, k):
        ins.then_inc(self.h, k)
        self.n += k
        return self.n

    def wait(self, eng, v):
        if v > self.maxw:
            self.maxw = v
        eng.wait_ge(self.h, v)


def _slices(total, step):
    return [(o, min(step, total - o)) for o in range(0, total, step)]


def _host_consts():
    c = {}
    c["ident16"] = np.eye(128, dtype=BF)
    c["ident32"] = np.eye(128, dtype=np.float32)
    P = np.zeros((128, 128), np.float32)
    for i in range(64):
        P[2 * i, 2 * i + 1] = -1.0
        P[2 * i + 1, 2 * i] = 1.0
    c["prope"] = P.T.astype(BF)
    m = np.zeros((128, 4 * 512), np.float32)
    qq = np.arange(512)
    for j in range(4):
        for kk in range(128):
            m[kk, j * 512:(j + 1) * 512] = (128 * j + kk <= qq)
    c["masks"] = m.astype(BF)
    idx = np.arange(64, dtype=np.float64)
    inv = (10000.0 ** (-2.0 * idx / 128.0))
    pos = np.arange(S, dtype=np.float64)
    theta = pos[:, None] * inv[None, :]
    cos = np.cos(theta).astype(np.float32)
    sin = np.sin(theta).astype(np.float32)
    c["cosq"] = np.ascontiguousarray(np.repeat(cos.T, 2, axis=0))
    c["sinq"] = np.ascontiguousarray(np.repeat(sin.T, 2, axis=0))
    c["cosk"] = np.ascontiguousarray(c["cosq"] / np.float32(16.0))
    c["sink"] = np.ascontiguousarray(c["sinq"] / np.float32(16.0))
    return c


def host_prep(ins, layers=(0, 1)):
    cst = _host_consts()
    shared = {}
    if 0 in layers:
        shared.update(
            sw1=ins["L0_ffn_sw1"], sw2=ins["L0_ffn_sw2"], sw3=ins["L0_ffn_sw3"],
            rw1=ins["L0_ffn_rw1"], rw2=ins["L0_ffn_rw2"], rw3=ins["L0_ffn_rw3"],
            gwf=ins["L0_ffn_gate_w"],
            gwT=np.ascontiguousarray(ins["L0_attn_gate_w"].T),
            gbcol=np.ascontiguousarray(ins["L0_attn_gate_b"].reshape(128, 1)),
            woT=np.ascontiguousarray(ins["L0_attn_o_w"].T),
        )
    if 1 in layers:
        shared.update(
            wdkvT=np.ascontiguousarray(ins["L1_attn_W_DKV_w"].T),
            bdkv=np.ascontiguousarray(ins["L1_attn_W_DKV_b"].reshape(DC, 1)),
            wdqT=np.ascontiguousarray(ins["L1_attn_W_DQ_w"].T),
            bdq=np.ascontiguousarray(ins["L1_attn_W_DQ_b"].reshape(DQC, 1)),
            woT1=np.ascontiguousarray(ins["L1_attn_W_O_w"].T),
            borow=np.ascontiguousarray(ins["L1_attn_W_O_b"].reshape(1, D)),
            wfcT=np.ascontiguousarray(ins["L1_ffn_w_fc"].T),
            wpjT=np.ascontiguousarray(ins["L1_ffn_w_proj"].T),
            cosq=cst["cosq"].astype(BF), sinq=cst["sinq"].astype(BF),
            cosk=cst["cosk"].astype(BF), sink=cst["sink"].astype(BF),
            prope=cst["prope"],
        )
    shared.update(ident16=cst["ident16"], ident32=cst["ident32"], masks=cst["masks"])

    if 0 in layers:
        qT = np.ascontiguousarray(ins["L0_attn_q_w"].T)
        kT = np.ascontiguousarray(ins["L0_attn_k_w"].T)
        vT = np.ascontiguousarray(ins["L0_attn_v_w"].T)
    if 1 in layers:
        ukT = np.ascontiguousarray(ins["L1_attn_W_UK_w"].T)
        uvT = np.ascontiguousarray(ins["L1_attn_W_UV_w"].T)
        uqT = np.ascontiguousarray(ins["L1_attn_W_UQ_w"].T)
        krT = np.ascontiguousarray(ins["L1_attn_W_KR_w"].T)

    xin_key = "L0_attn_x" if 0 in layers else "x1"
    xfull = np.asarray(ins[xin_key])
    if xfull.ndim == 3:
        xfull = xfull.reshape(B, S, D)

    maps = []
    for c in range(NCORES):
        b, p = c // 4, c % 4
        hs = slice(p * 512, (p + 1) * 512)
        m = dict(shared)
        m["xin"] = np.ascontiguousarray(xfull[b, p * TOK:(p + 1) * TOK, :], dtype=np.float32)
        if 0 in layers:
            m["wqT"] = np.ascontiguousarray(qT[:, hs])
            m["wkT"] = np.ascontiguousarray(kT[:, hs])
            m["wvT"] = np.ascontiguousarray(vT[:, hs])
        if 1 in layers:
            m["wukT"] = np.ascontiguousarray(ukT[:, hs])
            m["buk"] = np.ascontiguousarray(ins["L1_attn_W_UK_b"][hs].reshape(512, 1))
            m["wuvT"] = np.ascontiguousarray(uvT[:, hs])
            m["buv"] = np.ascontiguousarray(ins["L1_attn_W_UV_b"][hs].reshape(512, 1))
            m["wuqT"] = np.ascontiguousarray(uqT[:, hs])
            m["buq"] = np.ascontiguousarray(ins["L1_attn_W_UQ_b"][hs].reshape(512, 1))
            m["wkrT"] = np.ascontiguousarray(krT[:, hs])
            m["bkr"] = np.ascontiguousarray(ins["L1_attn_W_KR_b"][hs].reshape(512, 1))
        maps.append(m)
    return maps


def build(layers=(0, 1), taps=(), upto=None):
    import contextlib
    _pidx = [0]

    def _go():
        _pidx[0] += 1
        return upto is None or _pidx[0] <= upto

    nc = bass.Bass("TRN2", target_bir_lowering=False)
    HASL0, HASL1 = (0 in layers), (1 in layers)

    xin = nc.dram_tensor("xin", [TOK, D], F32, kind="ExternalInput").ap()
    out = nc.dram_tensor("out", [TOK, D], F32, kind="ExternalOutput").ap()
    if HASL0 and HASL1:
        r1 = nc.dram_tensor("r1", [TOK, D], F32).ap()
    else:
        r1 = out if HASL0 else None
    l_in = {0: xin, 1: (r1 if HASL0 else xin)}
    l_out = {0: r1, 1: out}

    W = {}

    def din(name, shape, dt=F32):
        W[name] = nc.dram_tensor(name, list(shape), dt, kind="ExternalInput").ap()
        return W[name]

    din("ident16", (128, 128), BF16)
    din("ident32", (128, 128), F32)
    din("masks", (128, 4 * 512), BF16)
    if HASL0:
        din("sw1", (D, FSH)); din("sw2", (FSH, D)); din("sw3", (D, FSH))
        din("rw1", (8, D, FRT)); din("rw2", (8, FRT, D)); din("rw3", (8, D, FRT))
        din("gwf", (D, 8))
        din("gwT", (128, 128)); din("gbcol", (128, 1))
        din("wqT", (D, 512)); din("wkT", (D, 512)); din("wvT", (D, 512))
        din("woT", (D, D))
    if HASL1:
        din("wdkvT", (D, DC)); din("bdkv", (DC, 1))
        din("wdqT", (D, DQC)); din("bdq", (DQC, 1))
        din("wukT", (DC, 512)); din("buk", (512, 1))
        din("wuvT", (DC, 512)); din("buv", (512, 1))
        din("wuqT", (DQC, 512)); din("buq", (512, 1))
        din("wkrT", (D, 512)); din("bkr", (512, 1))
        din("woT1", (D, D)); din("borow", (1, D))
        din("wfcT", (D, FT)); din("wpjT", (FT, D))
        din("prope", (128, 128), BF16)
        din("cosq", (128, S), BF16); din("sinq", (128, S), BF16)
        din("cosk", (128, S), BF16); din("sink", (128, S), BF16)

    pub, gat, puba, gata = {}, {}, {}, {}
    for lay in layers:
        pr = PUB0 if lay == 0 else PUB1
        pub[lay] = nc.dram_tensor(f"pub{lay}", [pr, TOK], BF16).ap()
        gat[lay] = nc.dram_tensor(f"gat{lay}", [4 * pr, TOK], BF16).ap()
        puba[lay] = nc.dram_tensor(f"puba{lay}", [HPC * DH, S], BF16).ap()
        gata[lay] = nc.dram_tensor(f"gata{lay}", [4 * HPC * DH, S], BF16).ap()
    if HASL1:
        qfcT = nc.dram_tensor("qfcT", [D, FT], BF16).ap()
        qpjT = nc.dram_tensor("qpjT", [FT, D], BF16).ap()

    tap_out = {}
    for t in taps:
        srcs = {"pub0": (pub.get(0), BF16), "pub1": (pub.get(1), BF16),
                "puba0": (puba.get(0), BF16), "puba1": (puba.get(1), BF16),
                "gata0": (gata.get(0), BF16),
                "r1": (r1 if (HASL0 and HASL1) else None, F32)}
        src, dt = srcs[t]
        tap_out[t] = (nc.dram_tensor("tap_" + t, list(src.tensor.shape), dt,
                                     kind="ExternalOutput").ap(), src)

    ctx = contextlib.ExitStack()
    with ctx:
        sems = {}
        for name in ("qs", "qc", "qg", "cc", "pe", "act", "dve", "pool",
                     "wt0", "wt1", "wt2", "wt3"):
            sems[name] = Sem(ctx.enter_context(nc.semaphore(name)), name)
        qs, qc, qg, cc = sems["qs"], sems["qc"], sems["qg"], sems["cc"]
        pe, act, dve, pool = sems["pe"], sems["act"], sems["dve"], sems["pool"]

        ident16 = ctx.enter_context(nc.sbuf_tensor("ident16s", [128, 128], BF16))
        ident32 = ctx.enter_context(nc.sbuf_tensor("ident32s", [128, 128], F32))
        masks = ctx.enter_context(nc.sbuf_tensor("masks_s", [128, 4 * 512], BF16))
        ones_c = ctx.enter_context(nc.sbuf_tensor("ones_c", [128, 1], F32))
        ones_c16 = ctx.enter_context(nc.sbuf_tensor("ones_c16", [128, 1], BF16))
        ones_r = ctx.enter_context(nc.sbuf_tensor("ones_r", [1, 128], F32))
        pcol = ctx.enter_context(nc.sbuf_tensor("pcol", [128, 40], F32))
        prow = ctx.enter_context(nc.sbuf_tensor("prow", [1, 256], F32))
        bo_bc = ctx.enter_context(nc.sbuf_tensor("bo_bc", [128, D], F32)) if HASL1 else None
        prope_s = ctx.enter_context(nc.sbuf_tensor("prope_s", [128, 128], BF16)) if HASL1 else None
        borow_s = ctx.enter_context(nc.sbuf_tensor("borow_s", [1, D], F32)) if HASL1 else None

        with nc.Block() as blk:
            @blk.sync
            def _(sy):
                sy.dma_start(out=ident16[:, :], in_=W["ident16"]).then_inc(qs.h, 16)
                sy.dma_start(out=ident32[:, :], in_=W["ident32"]).then_inc(qs.h, 16)
                sy.dma_start(out=masks[:, :], in_=W["masks"]).then_inc(qs.h, 16)
                qs.n += 48
                if HASL0:
                    sy.dma_start(out=pcol[:, 5:6], in_=W["gbcol"]).then_inc(qs.h, 16)
                    qs.n += 16
                if HASL1:
                    for i in range(4):
                        sy.dma_start(out=pcol[:, 6 + i:7 + i],
                                     in_=W["bdkv"][i * 128:(i + 1) * 128, :]).then_inc(qs.h, 16)
                    for i in range(8):
                        sy.dma_start(out=pcol[:, 10 + i:11 + i],
                                     in_=W["bdq"][i * 128:(i + 1) * 128, :]).then_inc(qs.h, 16)
                    for nm, base in (("buk", 18), ("buv", 22), ("buq", 26), ("bkr", 30)):
                        for i in range(4):
                            sy.dma_start(out=pcol[:, base + i:base + i + 1],
                                         in_=W[nm][i * 128:(i + 1) * 128, :]).then_inc(qs.h, 16)
                    sy.dma_start(out=prope_s[:, :], in_=W["prope"]).then_inc(qs.h, 16)
                    sy.dma_start(out=borow_s[:, :], in_=W["borow"]).then_inc(qs.h, 16)
                    qs.n += 16 * (4 + 8 + 16 + 2)

            @blk.vector
            def _(v):
                v.memset(ones_c[:, :], 1.0)
                v.memset(ones_c16[:, :], 1.0)
                v.memset(pcol[:, 39:40], EPS)
                v.memset(ones_r[:, :], 1.0).then_inc(dve.h, 1)
                dve.n += 1

            @blk.scalar
            def _(sc):
                qs.wait(sc, qs.n)
                dve.wait(sc, dve.n)
                if HASL1:
                    for i in range(4):
                        sc.mul(pcol[:, 18 + i:19 + i], pcol[:, 18 + i:19 + i], 1.0 / 16.0)

        if HASL1:
            with nc.psum_tensor("pbo", [128, 512], F32) as pbo, nc.Block() as blk:
                @blk.tensor
                def _(te):
                    qs.wait(te, qs.n)
                    dve.wait(te, dve.n)
                    for i in range(4):
                        act.wait(te, act.n + i)
                        te.matmul(pbo[:, :], ones_r[:, 0:128], borow_s[0:1, i * 512:(i + 1) * 512],
                                  start=True, stop=True).then_inc(pe.h, 1)
                        pe.n += 1

                @blk.scalar
                def _(sc):
                    for i in range(4):
                        pe.wait(sc, pe.n - 3 + i)
                        sc.activation(bo_bc[:, i * 512:(i + 1) * 512], pbo[:, :],
                                      AF.Copy).then_inc(act.h, 1)
                        act.n += 1

        if HASL0:
            if _go():
                _phase_ln_pub(nc, 0, l_in[0], pub[0], gat[0], W, sems,
                              ident16, ones_c, ones_r, pcol, bo_bc)
            if _go():
                _phase_moe(nc, l_in[0], l_out[0], pub[0], W, sems, pcol)
            dve_moe_end = sems["dve"].n
            if HASL1 and _go():
                _tern_prepass(nc, W, sems, ones_c, ones_r, pcol, prow)
            if HASL1 and _go():
                _tern_quant(nc, W, sems, pcol, qfcT, qpjT)
            if _go():
                _phase_attn(nc, 0, gat[0], puba[0], gata[0], W, sems,
                            ident16, ones_c16, ones_r, pcol, prope_s, sems["cc"].n,
                            dve_guard=dve_moe_end)
            if _go():
                _phase_oproj(nc, gata[0], l_out[0], l_out[0], W["woT"], sems,
                             sems["cc"].n, tag=0)
        if HASL1:
            if not HASL0:
                if _go():
                    _tern_prepass(nc, W, sems, ones_c, ones_r, pcol, prow)
                if _go():
                    _tern_quant(nc, W, sems, pcol, qfcT, qpjT)
            if _go():
                _phase_ln_pub(nc, 1, l_in[1], pub[1], gat[1], W, sems,
                              ident16, ones_c, ones_r, pcol, bo_bc)
            if _go():
                _phase_ffn1(nc, l_in[1], l_out[1], pub[1], qfcT, qpjT, W, sems,
                            pcol, bo_bc)
            if _go():
                _phase_attn(nc, 1, gat[1], puba[1], gata[1], W, sems,
                            ident16, ones_c16, ones_r, pcol, prope_s, sems["cc"].n)
            if _go():
                _phase_oproj(nc, gata[1], l_out[1], out, W["woT1"], sems,
                             sems["cc"].n, tag=1)

        with nc.Block() as blk:
            @blk.gpsimd
            def _(g):
                qc.wait(g, qc.n)
                qs.wait(g, qs.n)
                for t, (dst, src) in tap_out.items():
                    g.dma_start(out=dst, in_=src).then_inc(qg.h, 16)
                    qg.n += 16
                qg.wait(g, qg.n)

            @blk.sync
            def _(sy):
                qc.wait(sy, qc.n)

        for s in sems.values():
            assert s.maxw <= s.n, f"sem {s.name}: wait {s.maxw} > total {s.n}"
    return nc


def _phase_ln_pub(nc, lay, src, pubT, gatT, W, sems, ident16, ones_c, ones_r,
                  pcol, bo_bc):
    qs, qc, qg, cc = sems["qs"], sems["qc"], sems["qg"], sems["cc"]
    pe, act, dve, pool = sems["pe"], sems["act"], sems["dve"], sems["pool"]
    L1 = (lay == 1)
    APT = 21 if L1 else 9
    PPT = 16 if L1 else 4
    NPUB = 2 if L1 else 1
    qs0, qc0, qg0, pe0, act0, dve0 = qs.n, qc.n, qg.n, pe.n, act.n, dve.n
    w0_, w1_ = sems["wt0"].n, sems["wt1"].n
    D_ = D

    import contextlib
    with contextlib.ExitStack() as st:
        xbuf = st.enter_context(nc.sbuf_tensor(f"A{lay}_xbuf", [128, 2 * D_], F32))
        xsq = st.enter_context(nc.sbuf_tensor(f"A{lay}_xsq", [128, 2 * D_], F32))
        xnt = st.enter_context(nc.sbuf_tensor(f"A{lay}_xnt", [128, 2 * D_], BF16))
        stage = st.enter_context(nc.sbuf_tensor(f"A{lay}_stage", [128, 2 * D_], BF16))
        stf = st.enter_context(nc.sbuf_tensor(f"A{lay}_stf", [128, 114], F32))
        trpb = [st.enter_context(nc.psum_tensor(f"A{lay}_trp{i}", [128, 512], BF16))
                for i in range(2)]
        if L1:
            ckst = st.enter_context(nc.sbuf_tensor("A1_ckst", [128, 2 * 1536], BF16))
            wdkv_sb = st.enter_context(nc.sbuf_tensor("A1_wdkv", [128, 16 * 512], BF16))
            wdq_sb = st.enter_context(nc.sbuf_tensor("A1_wdq", [128, 16 * 1024], BF16))
            pck = [st.enter_context(nc.psum_tensor(f"A1_pck{i}", [128, 128], F32))
                   for i in range(4)]

        with nc.Block() as blk:
            @blk.sync
            def _(sy):
                qc.wait(sy, qc0)
                for t in range(NTT):
                    if t >= 2:
                        dve.wait(sy, dve0 + 5 * (t - 1))
                    wsem = sems["wt" + str(t % 2)]
                    sy.dma_start(out=xbuf[:, (t % 2) * D_:(t % 2 + 1) * D_],
                                 in_=src[t * 128:(t + 1) * 128, :]).then_inc(wsem.h, 16)
                    wsem.n += 16

            @blk.gpsimd
            def _(g):
                qc.wait(g, qc0)
                if L1:
                    for kc in range(16):
                        g.dma_start(out=wdkv_sb[:, kc * 512:(kc + 1) * 512],
                                    in_=W["wdkvT"][kc * 128:(kc + 1) * 128, :]).then_inc(qg.h, 16)
                        qg.n += 16
                    for kc in range(16):
                        g.dma_start(out=wdq_sb[:, kc * 1024:(kc + 1) * 1024],
                                    in_=W["wdqT"][kc * 128:(kc + 1) * 128, :]).then_inc(qg.h, 16)
                        qg.n += 16
                qc.wait(g, qc0 + 16 * NPUB * NTT)
                g.collective_compute(
                    "AllGather", ALU.bypass, replica_groups=RGROUPS,
                    ins=[pubT.opt()], outs=[gatT.opt()]).then_inc(cc.h, 1)
                cc.n += 1

            @blk.vector
            def _(v):
                qc.wait(v, qc0)
                for t in range(NTT):
                    sl = slice((t % 2) * D_, (t % 2 + 1) * D_)
                    wsem = sems["wt" + str(t % 2)]
                    wsem.wait(v, (w0_ if t % 2 == 0 else w1_) + 16 * (t // 2 + 1))
                    v.reduce_sum(stf[:, t:t + 1], xbuf[:, sl], axis=AX.X).then_inc(dve.h, 1)
                    act.wait(v, act0 + APT * t + 1)
                    v.reduce_sum(stf[:, 16 + t:17 + t], xsq[:, sl], axis=AX.X).then_inc(dve.h, 1)
                    act.wait(v, act0 + APT * t + 4)
                    v.tensor_scalar(stf[:, 64 + t:65 + t], stf[:, 112 + t % 2:113 + t % 2],
                                    stf[:, 48 + t:49 + t], None,
                                    ALU.subtract).then_inc(dve.h, 1)
                    act.wait(v, act0 + APT * t + 5)
                    v.reciprocal(stf[:, 96 + t:97 + t], stf[:, 80 + t:81 + t]).then_inc(dve.h, 1)
                    v.tensor_scalar(xnt[:, sl], xbuf[:, sl], stf[:, 32 + t:33 + t],
                                    stf[:, 96 + t:97 + t], ALU.subtract,
                                    ALU.mult).then_inc(dve.h, 1)
                    dve.n += 5

            @blk.tensor
            def _(te):
                qc.wait(te, qc0)
                act.wait(te, act0)
                if L1:
                    qg.wait(te, qg0 + 16 * 32)
                for t in range(NTT):
                    sl16 = slice((t % 2) * D_, (t % 2 + 1) * D_)
                    dve.wait(te, dve0 + 5 * t + 5)
                    for g in range(4):
                        G = 4 * t + g
                        if G >= 2:
                            t2, g2 = (G - 2) // 4, (G - 2) % 4
                            act.wait(te, act0 + APT * t2 + 6 + g2)
                        for j in range(4):
                            ins = te.transpose(trpb[g % 2][:, j * 128:(j + 1) * 128],
                                               xnt[:, (t % 2) * D_ + (4 * g + j) * 128:
                                                   (t % 2) * D_ + (4 * g + j + 1) * 128],
                                               ident16[:, :])
                            if j == 3:
                                ins.then_inc(pe.h, 1)
                                pe.n += 1
                    if L1:
                        act.wait(te, act0 + APT * t + 9)
                        for r, (wsb, wstep, mcs) in enumerate(
                                [(wdkv_sb, 512, range(4)),
                                 (wdq_sb, 1024, range(4)),
                                 (wdq_sb, 1024, range(4, 8))]):
                            for i, mc in enumerate(mcs):
                                if r > 0:
                                    act.wait(te, act0 + APT * t + 9 + 4 * (r - 1) + i + 1)
                                for kc in range(16):
                                    ins = te.matmul(
                                        pck[i][:, :],
                                        wsb[:, kc * wstep + mc * 128:kc * wstep + (mc + 1) * 128],
                                        stage[:, (t % 2) * D_ + kc * 128:(t % 2) * D_ + (kc + 1) * 128],
                                        start=(kc == 0), stop=(kc == 15))
                                ins.then_inc(pe.h, 1)
                                pe.n += 1

            @blk.scalar
            def _(sc):
                qc.wait(sc, qc0)
                for t in range(NTT):
                    sl = slice((t % 2) * D_, (t % 2 + 1) * D_)
                    wsem = sems["wt" + str(t % 2)]
                    wsem.wait(sc, (w0_ if t % 2 == 0 else w1_) + 16 * (t // 2 + 1))
                    sc.activation(xsq[:, sl], xbuf[:, sl], AF.Square).then_inc(act.h, 1)
                    dve.wait(sc, dve0 + 5 * t + 1)
                    sc.activation(stf[:, 32 + t:33 + t], stf[:, t:t + 1], AF.Copy,
                                  scale=1.0 / D_).then_inc(act.h, 1)
                    sc.activation(stf[:, 48 + t:49 + t], stf[:, 32 + t:33 + t],
                                  AF.Square).then_inc(act.h, 1)
                    dve.wait(sc, dve0 + 5 * t + 2)
                    sc.activation(stf[:, 112 + t % 2:113 + t % 2], stf[:, 16 + t:17 + t],
                                  AF.Copy, scale=1.0 / D_).then_inc(act.h, 1)
                    dve.wait(sc, dve0 + 5 * t + 3)
                    sc.activation(stf[:, 80 + t:81 + t], stf[:, 64 + t:65 + t], AF.Sqrt,
                                  bias=pcol[:, 39:40]).then_inc(act.h, 1)
                    act.n += 5
                    if t >= 1:
                        qc.wait(sc, qc0 + 16 * NPUB * t)
                    for g in range(4):
                        pe.wait(sc, pe0 + PPT * t + g + 1)
                        sc.activation(stage[:, (t % 2) * D_ + g * 512:(t % 2) * D_ + (g + 1) * 512],
                                      trpb[g % 2][:, :], AF.Copy).then_inc(act.h, 1)
                        act.n += 1
                    if L1:
                        for mc in range(4):
                            pe.wait(sc, pe0 + PPT * t + 4 + mc + 1)
                            sc.activation(ckst[:, (t % 2) * 1536 + mc * 128:(t % 2) * 1536 + (mc + 1) * 128],
                                          pck[mc][:, :], AF.Identity,
                                          bias=pcol[:, 6 + mc:7 + mc]).then_inc(act.h, 1)
                            act.n += 1
                        for mc in range(8):
                            pe.wait(sc, pe0 + PPT * t + 8 + mc + 1)
                            sc.activation(ckst[:, (t % 2) * 1536 + 512 + mc * 128:
                                              (t % 2) * 1536 + 512 + (mc + 1) * 128],
                                          pck[mc % 4][:, :], AF.Identity,
                                          bias=pcol[:, 10 + mc:11 + mc]).then_inc(act.h, 1)
                            act.n += 1
                    sc.dma_start(
                        out=pubT[0:D_, t * 128:(t + 1) * 128].rearrange("(c p) t -> p c t", p=128),
                        in_=stage[:, sl].rearrange("p (c t) -> p c t", t=128)).then_inc(qc.h, 16)
                    qc.n += 16
                    if L1:
                        sc.dma_start(
                            out=pubT[D_:D_ + 1536, t * 128:(t + 1) * 128].rearrange("(c p) t -> p c t", p=128),
                            in_=ckst[:, (t % 2) * 1536:(t % 2 + 1) * 1536].rearrange(
                                "p (c t) -> p c t", t=128)).then_inc(qc.h, 16)
                        qc.n += 16



def _phase_moe(nc, src, dst, pubT, W, sems, pcol, bo_bc=None):
    import contextlib
    qs, qc, qg = sems["qs"], sems["qc"], sems["qg"]
    pe, act, dve = sems["pe"], sems["act"], sems["dve"]
    base = {k: s.n for k, s in sems.items()}
    marks = {}
    SH = _slices(FSH, 512)
    RT = _slices(FRT, 512)

    with contextlib.ExitStack() as st:
        xnh = st.enter_context(nc.sbuf_tensor("M_xnh", [128, 16 * 1024], BF16))
        oac = st.enter_context(nc.sbuf_tensor("M_oac", [128, 8 * 2048], F32))
        s_sb = st.enter_context(nc.sbuf_tensor("M_s", [128, 8 * 512], BF16))
        htmp = st.enter_context(nc.sbuf_tensor("M_ht", [128, 2 * 512], F32))
        htA = st.enter_context(nc.sbuf_tensor("M_htA", [128, 2 * 512], F32))
        wt = st.enter_context(nc.sbuf_tensor("M_wt", [128, 4 * 2048], BF16))
        h_fm = st.enter_context(nc.sbuf_tensor("M_hfm", [128, 22 * 1024], BF16))
        cwsb = st.enter_context(nc.sbuf_tensor("M_cw", [128, 64], F32))
        rb = st.enter_context(nc.sbuf_tensor("M_rb", [128, 4 * 64], F32))
        rst = st.enter_context(nc.sbuf_tensor("M_rst", [128, 6 * 8], F32))
        lg = st.enter_context(nc.sbuf_tensor("M_lg", [128, 64], F32))
        gwf_sb = st.enter_context(nc.sbuf_tensor("M_gwf", [128, 16 * 8], BF16))
        ident32 = st.enter_context(nc.sbuf_tensor("M_id32", [128, 128], F32))
        ps = [st.enter_context(nc.psum_tensor(f"M_ps{i}", [128, 512], F32))
              for i in range(8)]

        def body(R, eng=None):
            C = dict(base)

            def INC(sname, k, ins=None):
                C[sname] += k
                if ins is not None:
                    ins.then_inc(sems[sname].h, k)
                return C[sname]

            def MK(key, sname):
                if R == "sim":
                    marks[key] = C[sname]

            def WT(sname, key_or_val):
                if R == "sim":
                    return
                v = marks[key_or_val] if isinstance(key_or_val, tuple) else key_or_val
                sems[sname].wait(eng, v)

            if R != "sim":
                sems["qc"].wait(eng, base["qc"])
            if R == "gpsimd":
                eng.dma_start(out=gwf_sb[:, :],
                              in_=W["gwf"].rearrange("(c p) e -> p c e", p=128)
                              ).then_inc(qg.h, 16)
            if R == "sync":
                eng.dma_start(out=ident32[:, :], in_=W["ident32"]).then_inc(qs.h, 16)
            INC("qg", 16)
            INC("qs", 16)
            MK(("m_pre", "qg"), "qg")
            MK(("m_pre", "qs"), "qs")

            for hf in range(MOE_HALVES):
                if hf == 1:
                    WT("qc", ("m_st", 0))
                    WT("pe", ("m_peend", 0))
                elif R == "sync":
                    WT("qc", base["qc"])
                for tt in range(8):
                    if R == "sync":
                        eng.dma_start(out=oac[:, tt * 2048:(tt + 1) * 2048],
                                      in_=src[hf * 1024 + tt * 128:hf * 1024 + (tt + 1) * 128, :]
                                      ).then_inc(qs.h, 16)
                    INC("qs", 16)
                for kc in range(16):
                    if R == "sync":
                        eng.dma_start(out=xnh[:, kc * 1024:(kc + 1) * 1024],
                                      in_=pubT[kc * 128:(kc + 1) * 128,
                                               hf * 1024:(hf + 1) * 1024]).then_inc(qs.h, 16)
                    INC("qs", 16)
                MK(("m_ld", hf), "qs")

                if R == "tensor":
                    WT("qs", ("m_ld", hf))
                    WT("qg", ("m_pre", "qg"))
                    if hf == 0:
                        WT("act", base["act"])
                        WT("dve", base["dve"])
                    if hf == 1:
                        WT("dve", ("m_yev", hf - 1, MOE_EXPERTS - 1, 3))
                    for kc in range(16):
                        for tt in range(8):
                            ins = eng.matmul(ps[tt][:, 0:8],
                                             xnh[:, kc * 1024 + tt * 128:kc * 1024 + (tt + 1) * 128],
                                             gwf_sb[:, kc * 8:(kc + 1) * 8],
                                             start=(kc == 0), stop=(kc == 15))
                            if kc == 15:
                                ins.then_inc(pe.h, 1)
                for tt in range(8):
                    INC("pe", 1)
                    MK(("m_rpe", hf, tt), "pe")
                for tt in range(8):
                    sl8 = slice(tt * 8, (tt + 1) * 8)
                    c = lambda j: rst[:, tt * 6 + j:tt * 6 + j + 1]
                    if R == "scalar":
                        WT("pe", ("m_rpe", hf, tt))
                        eng.activation(lg[:, sl8], ps[tt][:, 0:8], AF.Copy).then_inc(act.h, 1)
                    INC("act", 1)
                    MK(("m_rev", hf, tt), "act")
                    if R == "vector":
                        WT("act", ("m_rev", hf, tt))
                        eng.reduce_max(c(0), lg[:, sl8], axis=AX.X).then_inc(dve.h, 1)
                        sems["dve"].wait(eng, C["dve"] + 1)
                        eng.tensor_scalar(c(1), c(0), -1.0, None, ALU.mult).then_inc(dve.h, 1)
                    INC("dve", 2)
                    MK(("m_rneg", hf, tt), "dve")
                    if R == "scalar":
                        WT("dve", ("m_rneg", hf, tt))
                        eng.activation(rb[:, sl8], lg[:, sl8], AF.Exp,
                                       bias=c(1)).then_inc(act.h, 1)
                    INC("act", 1)
                    MK(("m_rexp", hf, tt), "act")
                    if R == "vector":
                        WT("act", ("m_rexp", hf, tt))
                        eng.reduce_sum(c(2), rb[:, sl8], axis=AX.X).then_inc(dve.h, 1)
                        sems["dve"].wait(eng, C["dve"] + 1)
                        eng.reciprocal(c(3), c(2)).then_inc(dve.h, 1)
                        eng.reduce_max(c(4), rb[:, sl8], axis=AX.X).then_inc(dve.h, 1)
                        sems["dve"].wait(eng, C["dve"] + 3)
                        eng.tensor_scalar(rb[:, 64 + tt * 8:64 + (tt + 1) * 8], rb[:, sl8],
                                          c(4), None, ALU.subtract).then_inc(dve.h, 1)
                    INC("dve", 4)
                    MK(("m_rd", hf, tt), "dve")
                    if R == "scalar":
                        WT("dve", ("m_rd", hf, tt))
                        eng.activation(rb[:, 128 + tt * 8:128 + (tt + 1) * 8],
                                       rb[:, 64 + tt * 8:64 + (tt + 1) * 8],
                                       AF.Sign).then_inc(act.h, 1)
                    INC("act", 1)
                    MK(("m_rsg", hf, tt), "act")
                    if R == "vector":
                        WT("act", ("m_rsg", hf, tt))
                        eng.tensor_scalar(rb[:, 64 + tt * 8:64 + (tt + 1) * 8],
                                          rb[:, 128 + tt * 8:128 + (tt + 1) * 8], -1.0,
                                          None, ALU.mult).then_inc(dve.h, 1)
                        sems["dve"].wait(eng, C["dve"] + 1)
                        eng.tensor_mul(rb[:, 192 + tt * 8:192 + (tt + 1) * 8],
                                       rb[:, 64 + tt * 8:64 + (tt + 1) * 8],
                                       rb[:, sl8]).then_inc(dve.h, 1)
                        sems["dve"].wait(eng, C["dve"] + 2)
                        eng.reduce_max(c(5), rb[:, 192 + tt * 8:192 + (tt + 1) * 8],
                                       axis=AX.X).then_inc(dve.h, 1)
                        sems["dve"].wait(eng, C["dve"] + 3)
                        eng.tensor_scalar(rb[:, 64 + tt * 8:64 + (tt + 1) * 8], rb[:, sl8],
                                          c(5), None, ALU.is_ge).then_inc(dve.h, 1)
                        eng.tensor_scalar(rb[:, 192 + tt * 8:192 + (tt + 1) * 8], rb[:, sl8],
                                          c(3), None, ALU.mult).then_inc(dve.h, 1)
                        sems["dve"].wait(eng, C["dve"] + 5)
                        eng.tensor_mul(cwsb[:, sl8],
                                       rb[:, 192 + tt * 8:192 + (tt + 1) * 8],
                                       rb[:, 64 + tt * 8:64 + (tt + 1) * 8]).then_inc(dve.h, 1)
                    INC("dve", 6)
                    MK(("m_rcw", hf, tt), "dve")

                for e in range(MOE_EXPERTS):
                    shared = (e == 0)
                    sls = SH if shared else RT
                    nch = 22 if shared else 11
                    if shared:
                        w1a, w3a, w2a = W["sw1"], W["sw3"], W["sw2"]
                    else:
                        w1a, w3a, w2a = W["rw1"][e - 1], W["rw3"][e - 1], W["rw2"][e - 1]

                    for si, (f0, fw) in enumerate(sls):
                        nsub = fw // 128
                        for ph, wa in ((0, w1a), (1, w3a)):
                            wav = wa.rearrange("(c p) f -> p c f", p=128)
                            for g in range(4):
                                wti = C["wti"] = C.get("wti", 0) + 1
                                slot = (wti - 1) % 4
                                sn = f"wt{slot}"
                                if R == "gpsimd":
                                    if (("m_wfree", wti - 4) in marks) if R != "sim" else (wti > 4):
                                        WT("pe", ("m_wfree", wti - 4))
                                    wdst = bass.AP(wt, slot * 2048,
                                                   [[4 * 2048, 128], [512, 4], [1, fw]])
                                    eng.dma_start(
                                        out=wdst,
                                        in_=wav[:, g * 4:(g + 1) * 4, f0:f0 + fw]
                                    ).then_inc(sems[sn].h, 16)
                                INC(sn, 16)
                                MK(("m_wld", wti), sn)
                                if R == "tensor":
                                    WT(sn, ("m_wld", wti))
                                    if ph == 0 and g == 0:
                                        if e == 0 and si == 0:
                                            WT("act", ("m_rev", hf, 7))
                                            WT("dve", ("m_rcw", hf, 7))
                                        elif si == 0:
                                            WT("dve", ("m_yev", hf, e - 1, 3))
                                        else:
                                            WT("act", ("m_tev", hf, e, si - 1, 7))
                                    if ph == 1 and g == 0:
                                        WT("act", ("m_aev", hf, e, si, 7))
                                    for j in range(4):
                                        kc = g * 4 + j
                                        for tt in range(8):
                                            ins = eng.matmul(
                                                ps[tt][:, 0:fw],
                                                xnh[:, kc * 1024 + tt * 128:kc * 1024 + (tt + 1) * 128],
                                                wt[:, slot * 2048 + j * 512:slot * 2048 + j * 512 + fw],
                                                start=(kc == 0), stop=(kc == 15))
                                            if tt == 7 and j == 3:
                                                ins.then_inc(pe.h, 1)
                                INC("pe", 1)
                                MK(("m_wfree", wti), "pe")
                                if g == 3:
                                    MK(("m_gstop", hf, e, si, ph), "pe")
                            if ph == 0:
                                for tt in range(8):
                                    if R == "scalar":
                                        WT("pe", ("m_gstop", hf, e, si, 0))
                                        if si > 0 or e > 0:
                                            WT("dve", ("m_hev", hf, e, si - 1, 7) if si > 0
                                               else ("m_hev", hf, e - 1,
                                                     (len(SH) if e == 1 else len(RT)) - 1, 7))
                                        elif hf > 0:
                                            lsi = (len(SH) if MOE_EXPERTS == 1 else len(RT)) - 1
                                            WT("dve", ("m_hev", hf - 1, MOE_EXPERTS - 1, lsi, 7))
                                        eng.activation(s_sb[:, tt * 512:tt * 512 + fw],
                                                       ps[tt][:, 0:fw],
                                                       AF.Silu).then_inc(act.h, 1)
                                    INC("act", 1)
                                    MK(("m_aev", hf, e, si, tt), "act")
                        c0 = f0 // 128
                        for tt in range(8):
                            if R == "vector":
                                WT("pe", ("m_gstop", hf, e, si, 1))
                                if (hf, e, si, tt) != (0, 0, 0, tt):
                                    pass
                                if tt >= 2:
                                    WT("act", ("m_tev", hf, e, si, tt - 2))
                                if e == 0:
                                    eng.tensor_tensor(htmp[:, (tt % 2) * 512:(tt % 2) * 512 + fw],
                                                      ps[tt][:, 0:fw],
                                                      s_sb[:, tt * 512:tt * 512 + fw],
                                                      ALU.mult).then_inc(dve.h, 1)
                                else:
                                    eng.tensor_tensor(htA[:, (tt % 2) * 512:(tt % 2) * 512 + fw],
                                                      ps[tt][:, 0:fw],
                                                      s_sb[:, tt * 512:tt * 512 + fw],
                                                      ALU.mult)
                                    eng.tensor_scalar(
                                        htmp[:, (tt % 2) * 512:(tt % 2) * 512 + fw],
                                        htA[:, (tt % 2) * 512:(tt % 2) * 512 + fw],
                                        cwsb[:, tt * 8 + e - 1:tt * 8 + e], None,
                                        ALU.mult).then_inc(dve.h, 1)
                            INC("dve", 1)
                            MK(("m_hev", hf, e, si, tt), "dve")
                            if R == "tensor":
                                WT("dve", ("m_hev", hf, e, si, tt))
                                for j in range(nsub):
                                    ins = eng.transpose(
                                        ps[tt][:, j * 128:(j + 1) * 128],
                                        htmp[:, (tt % 2) * 512 + j * 128:(tt % 2) * 512 + (j + 1) * 128],
                                        ident32[:, :])
                                    if j == nsub - 1:
                                        ins.then_inc(pe.h, 1)
                            INC("pe", 1)
                            MK(("m_tpe", hf, e, si, tt), "pe")
                            if R == "scalar":
                                WT("pe", ("m_tpe", hf, e, si, tt))
                                dst_ap = bass.AP(h_fm, c0 * 1024 + tt * 128,
                                                 [[22 * 1024, 128], [1024, nsub], [1, 128]])
                                eng.activation(
                                    dst_ap,
                                    ps[tt][:, 0:nsub * 128].rearrange("p (s t) -> p s t", t=128),
                                    AF.Copy).then_inc(act.h, 1)
                            INC("act", 1)
                            MK(("m_tev", hf, e, si, tt), "act")

                    w2v = w2a.rearrange("(c p) d -> p c d", p=128)
                    ngr = (nch + 3) // 4
                    for ds in range(4):
                        for g in range(ngr):
                            gs = min(4, nch - 4 * g)
                            wti = C["wti"] = C.get("wti", 0) + 1
                            slot = (wti - 1) % 4
                            sn = f"wt{slot}"
                            if R == "gpsimd":
                                WT("pe", ("m_wfree", wti - 4))
                                wdst = bass.AP(wt, slot * 2048,
                                               [[4 * 2048, 128], [512, gs], [1, 512]])
                                eng.dma_start(out=wdst,
                                              in_=w2v[:, g * 4:g * 4 + gs,
                                                      ds * 512:(ds + 1) * 512]).then_inc(sems[sn].h, 16)
                            INC(sn, 16)
                            MK(("m_wld", wti), sn)
                            if R == "tensor":
                                WT(sn, ("m_wld", wti))
                                if g == 0:
                                    WT("act", ("m_tev", hf, e, len(sls) - 1, 7))
                                    if ds > 0:
                                        WT("dve", ("m_yev", hf, e, ds - 1, 7))
                                for j in range(gs):
                                    ic = g * 4 + j
                                    for tt in range(8):
                                        ins = eng.matmul(
                                            ps[tt][:, 0:512],
                                            h_fm[:, ic * 1024 + tt * 128:ic * 1024 + (tt + 1) * 128],
                                            wt[:, slot * 2048 + j * 512:slot * 2048 + (j + 1) * 512],
                                            start=(ic == 0), stop=(ic == nch - 1))
                                        if tt == 7 and j == gs - 1:
                                            ins.then_inc(pe.h, 1)
                            INC("pe", 1)
                            MK(("m_wfree", wti), "pe")
                            if g == ngr - 1:
                                MK(("m_ygstop", hf, e, ds), "pe")
                        for tt in range(8):
                            if R == "vector":
                                WT("pe", ("m_ygstop", hf, e, ds))
                                if e == 0 and ds == 0 and tt == 0:
                                    WT("qs", ("m_ld", hf))
                                eng.tensor_tensor(
                                    oac[:, tt * 2048 + ds * 512:tt * 2048 + (ds + 1) * 512],
                                    ps[tt][:, 0:512],
                                    oac[:, tt * 2048 + ds * 512:tt * 2048 + (ds + 1) * 512],
                                    ALU.add).then_inc(dve.h, 1)
                            INC("dve", 1)
                            if tt in (3, 7):
                                MK(("m_yev", hf, e, ds, tt), "dve")
                        MK(("m_yev", hf, e, ds), "dve")
                MK(("m_peend", hf), "pe")

                for tt in range(8):
                    if R == "scalar":
                        WT("dve", ("m_yev", hf, MOE_EXPERTS - 1, 3))
                        WT("qs", ("m_ld", hf))
                        eng.dma_start(out=dst[hf * 1024 + tt * 128:hf * 1024 + (tt + 1) * 128, :],
                                      in_=oac[:, tt * 2048:(tt + 1) * 2048]).then_inc(qc.h, 16)
                    INC("qc", 16)
                if MOE_DBG:
                    if R == "gpsimd":
                        WT("dve", ("m_rcw", hf, 7))
                        eng.dma_start(out=pubT[0:128, 1900 + hf * 64:1900 + (hf + 1) * 64],
                                      in_=cwsb[:, :]).then_inc(qg.h, 16)
                        eng.dma_start(out=pubT[128:256, 1900 + hf * 64:1900 + (hf + 1) * 64],
                                      in_=lg[:, :]).then_inc(qg.h, 16)
                        eng.dma_start(out=pubT[256:384, 1900 + hf * 64:1900 + (hf + 1) * 64],
                                      in_=rb[:, 0:64]).then_inc(qg.h, 16)
                    INC("qg", 32)
                MK(("m_st", hf), "qc")

            if MOE_DBG2:
                if R == "gpsimd":
                    WT("dve", ("m_yev", MOE_HALVES - 1, MOE_EXPERTS - 1, 3))
                    for i in range(11):
                        eng.dma_start(out=pubT[i * 128:(i + 1) * 128, :],
                                      in_=h_fm[:, i * 2048:(i + 1) * 2048]).then_inc(qg.h, 16)
                    for i in range(2):
                        eng.dma_start(out=pubT[(11 + i) * 128:(12 + i) * 128, :],
                                      in_=s_sb[:, i * 2048:(i + 1) * 2048]).then_inc(qg.h, 16)
                    for i in range(2):
                        eng.dma_start(out=pubT[(13 + i) * 128:(14 + i) * 128, :],
                                      in_=xnh[:, i * 2048:(i + 1) * 2048]).then_inc(qg.h, 16)
                INC("qg", 16 * 15)
            if R == "sim":
                for k in sems:
                    marks[("m_end", k)] = C[k]

        body("sim")
        with nc.Block() as blk:
            @blk.sync
            def _(sy):
                body("sync", sy)

            @blk.gpsimd
            def _(g):
                body("gpsimd", g)

            @blk.tensor
            def _(te):
                body("tensor", te)

            @blk.scalar
            def _(sc):
                body("scalar", sc)

            @blk.vector
            def _(v):
                body("vector", v)

        for k in sems:
            sems[k].n = marks[("m_end", k)]


def _tern_prepass(nc, W, sems, ones_c, ones_r, pcol, prow):
    import contextlib
    qc = sems["qc"]
    pe, act, dve = sems["pe"], sems["act"], sems["dve"]

    pieces = []
    for kc in range(16):
        for half in range(2):
            pieces.append(("wfcT", kc * 128, half * 2560, 2560))
    for kc in range(40):
        pieces.append(("wpjT", kc * 128, 0, 2048))
    NP_ = len(pieces)
    NEL = float(D * FT)

    with contextlib.ExitStack() as st:
        tb = st.enter_context(nc.sbuf_tensor("T_tb", [128, 2 * 2560], F32))
        stf = st.enter_context(nc.sbuf_tensor("T_stf", [128, 80], F32))
        p1 = st.enter_context(nc.psum_tensor("T_p1", [1, 1], F32))
        pb = st.enter_context(nc.psum_tensor("T_pb", [128, 1], F32))
        qc0, dve0, act0, pe0 = qc.n, dve.n, act.n, pe.n
        tw0, tw1 = sems["wt0"].n, sems["wt1"].n

        with nc.Block() as blk:
            @blk.scalar
            def _(sc):
                qc.wait(sc, qc0)
                for i, (nm, r0, c0, cw) in enumerate(pieces):
                    if i >= 2:
                        dve.wait(sc, dve0 + i - 1)
                    ws = sems["wt" + str(i % 2)]
                    sc.dma_start(out=tb[:, (i % 2) * 2560:(i % 2) * 2560 + cw],
                                 in_=W[nm][r0:r0 + 128, c0:c0 + cw]).then_inc(ws.h, 16)
                    ws.n += 16
                pe.wait(sc, pe0 + 1)
                sc.activation(prow[0:1, 130:131], p1[:, :], AF.Copy,
                              scale=1.0 / NEL).then_inc(act.h, 1)
                dve.wait(sc, dve0 + NP_ + 3)
                sc.mul(prow[0:1, 132:133], prow[0:1, 131:132], 0.7).then_inc(act.h, 1)
                pe.wait(sc, pe0 + 2)
                sc.activation(prow[0:1, 140:141], p1[:, :], AF.Copy,
                              scale=1.0 / NEL).then_inc(act.h, 1)
                dve.wait(sc, dve0 + NP_ + 4)
                sc.mul(prow[0:1, 142:143], prow[0:1, 141:142], 0.7).then_inc(act.h, 1)
                sc.activation(prow[0:1, 143:144], prow[0:1, 131:132],
                              AF.Square).then_inc(act.h, 1)
                act.n += 5
                for j, (col, dst0) in enumerate([(132, 0), (142, 2), (144, 4)]):
                    pe.wait(sc, pe0 + 3 + j)
                    sc.activation(pcol[:, dst0:dst0 + 1], pb[:, :], AF.Copy).then_inc(act.h, 1)
                    act.n += 1
                    if dst0 < 4:
                        sc.activation(pcol[:, dst0 + 1:dst0 + 2], pb[:, :], AF.Copy,
                                      scale=-1.0).then_inc(act.h, 1)
                        act.n += 1

            @blk.vector
            def _(v):
                qc.wait(v, qc0)
                for i in range(NP_):
                    cw = pieces[i][3]
                    ws = sems["wt" + str(i % 2)]
                    ws.wait(v, (tw0 if i % 2 == 0 else tw1) + 16 * (i // 2 + 1))
                    v.tensor_reduce(stf[:, i:i + 1],
                                    tb[:, (i % 2) * 2560:(i % 2) * 2560 + cw],
                                    AX.X, ALU.add,
                                    apply_absolute_value=True).then_inc(dve.h, 1)
                    dve.n += 1
                v.tensor_reduce(stf[:, 72:73], stf[:, 0:32], AX.X, ALU.add).then_inc(dve.h, 1)
                v.tensor_reduce(stf[:, 73:74], stf[:, 32:72], AX.X, ALU.add).then_inc(dve.h, 1)
                dve.n += 2
                act.wait(v, act0 + 1)
                v.tensor_scalar(prow[0:1, 131:132], prow[0:1, 130:131], 1e-5, None,
                                ALU.max).then_inc(dve.h, 1)
                act.wait(v, act0 + 3)
                v.tensor_scalar(prow[0:1, 141:142], prow[0:1, 140:141], 1e-5, None,
                                ALU.max).then_inc(dve.h, 1)
                act.wait(v, act0 + 5)
                v.tensor_mul(prow[0:1, 144:145], prow[0:1, 143:144],
                             prow[0:1, 141:142]).then_inc(dve.h, 1)
                dve.n += 3

            @blk.tensor
            def _(te):
                act.wait(te, act0)
                dve.wait(te, dve0)
                dve.wait(te, dve0 + NP_ + 2)
                te.matmul(p1[:, :], ones_c[:, :], stf[:, 72:73], start=True,
                          stop=True).then_inc(pe.h, 1)
                act.wait(te, act0 + 1)
                te.matmul(p1[:, :], ones_c[:, :], stf[:, 73:74], start=True,
                          stop=True).then_inc(pe.h, 1)
                pe.n += 2
                waits = [(act0 + 2, dve0 + NP_ + 3), (act0 + 5, dve0 + NP_ + 4),
                         (act0 + 5, dve0 + NP_ + 5)]
                for j, (col, (aw, dw)) in enumerate(zip((132, 142, 144), waits)):
                    act.wait(te, aw)
                    dve.wait(te, dw)
                    if j > 0:
                        act.wait(te, act0 + 5 + 2 * j)
                    te.matmul(pb[:, :], ones_r[:, :], prow[0:1, col:col + 1],
                              start=True, stop=True).then_inc(pe.h, 1)
                    pe.n += 1


def _tern_quant(nc, W, sems, pcol, qfcT, qpjT):
    import contextlib
    qc = sems["qc"]
    pool, dve = sems["pool"], sems["dve"]

    pieces = []
    for kc in range(16):
        for half in range(2):
            pieces.append((W["wfcT"], qfcT, kc * 128, half * 2560, 2560, 0))
    for kc in range(40):
        pieces.append((W["wpjT"], qpjT, kc * 128, 0, 2048, 2))
    NP_ = len(pieces)

    with contextlib.ExitStack() as st:
        tb = st.enter_context(nc.sbuf_tensor("Q_tb", [128, 2 * 2560], F32))
        gl = st.enter_context(nc.sbuf_tensor("Q_gl", [128, 4 * 2560], BF16))
        qt_ = st.enter_context(nc.sbuf_tensor("Q_qt", [128, 2 * 2560], BF16))
        qc0, pool0, dve0 = qc.n, pool.n, dve.n
        qw2, qw3 = sems["wt2"].n, sems["wt3"].n

        with nc.Block() as blk:
            @blk.scalar
            def _(sc):
                qc.wait(sc, qc0)
                for i, (srcT, dstT, r0, c0, cw, tc) in enumerate(pieces):
                    if i >= 2:
                        dve.wait(sc, dve0 + 2 * (i - 1) - 1)
                    ws = sems["wt" + str(2 + i % 2)]
                    sc.dma_start(out=tb[:, (i % 2) * 2560:(i % 2) * 2560 + cw],
                                 in_=srcT[r0:r0 + 128, c0:c0 + cw]).then_inc(ws.h, 16)
                    ws.n += 16
                    dve.wait(sc, dve0 + 2 * (i + 1))
                    sc.dma_start(out=dstT[r0:r0 + 128, c0:c0 + cw],
                                 in_=qt_[:, (i % 2) * 2560:(i % 2) * 2560 + cw]
                                 ).then_inc(qc.h, 16)
                    qc.n += 16

            @blk.vector
            def _(v):
                qc.wait(v, qc0)
                for i, (srcT, dstT, r0, c0, cw, tc) in enumerate(pieces):
                    ws = sems["wt" + str(2 + i % 2)]
                    ws.wait(v, (qw2 if i % 2 == 0 else qw3) + 16 * (i // 2 + 1))
                    if i >= 1:
                        qc.wait(v, qc0 + 16 * i)
                    sl = slice((i % 2) * 2560, (i % 2) * 2560 + cw)
                    v.tensor_scalar(gl[:, (i % 2) * 2560:(i % 2) * 2560 + cw],
                                    tb[:, sl], pcol[:, tc:tc + 1], None, ALU.is_gt)
                    v.tensor_scalar(gl[:, 5120 + (i % 2) * 2560:5120 + (i % 2) * 2560 + cw],
                                    tb[:, sl], pcol[:, tc + 1:tc + 2], None,
                                    ALU.is_lt).then_inc(dve.h, 1)
                    dve.n += 1
                    v.tensor_sub(qt_[:, (i % 2) * 2560:(i % 2) * 2560 + cw],
                                 gl[:, (i % 2) * 2560:(i % 2) * 2560 + cw],
                                 gl[:, 5120 + (i % 2) * 2560:5120 + (i % 2) * 2560 + cw]
                                 ).then_inc(dve.h, 1)
                    dve.n += 1


def _phase_attn(nc, lay, gatT, pubaT, gataT, W, sems, ident16, ones_c16,
                ones_r, pcol, prope_s, cc_idx, dve_guard=None):
    import contextlib
    qs, qc, qg, cc = sems["qs"], sems["qc"], sems["qg"], sems["cc"]
    pe, act, dve = sems["pe"], sems["act"], sems["dve"]
    base = {k: s.n for k, s in sems.items()}
    marks = {}
    L1 = (lay == 1)
    PR = PUB1 if L1 else PUB0
    S0 = 1.0 / np.sqrt(128.0)

    with contextlib.ExitStack() as st:
        wsl = st.enter_context(nc.sbuf_tensor(f"P{lay}_wsl", [128, 48 * 128], BF16))
        gwt = st.enter_context(nc.sbuf_tensor(f"P{lay}_gwt", [128, 128], BF16))
        xnb = st.enter_context(nc.sbuf_tensor(f"P{lay}_xnb", [128, 2 * 16 * 512], BF16))
        kh_a = st.enter_context(nc.sbuf_tensor(f"P{lay}_kha", [128, S], BF16))
        qh_a = st.enter_context(nc.sbuf_tensor(f"P{lay}_qha", [128, S], BF16))
        vT_h = st.enter_context(nc.sbuf_tensor(f"P{lay}_vth", [128, S], BF16))
        vtmp = st.enter_context(nc.sbuf_tensor(f"P{lay}_vtmp", [128, 512], BF16))
        probsT = st.enter_context(nc.sbuf_tensor(f"P{lay}_pT", [128, 2 * 512], BF16))
        ptmp = st.enter_context(nc.sbuf_tensor(f"P{lay}_ptmp", [128, 512], BF16))
        rs_sb = st.enter_context(nc.sbuf_tensor(f"P{lay}_rs", [1, 512], F32))
        rb_sb = st.enter_context(nc.sbuf_tensor(f"P{lay}_rb", [128, 512], F32))
        fac = st.enter_context(nc.sbuf_tensor(f"P{lay}_fac", [128, 512], F32))
        av_sb = st.enter_context(nc.sbuf_tensor(f"P{lay}_av", [128, 2 * 512], BF16))
        if L1:
            kh_b = st.enter_context(nc.sbuf_tensor("P1_khb", [128, S], BF16))
            qh_b = st.enter_context(nc.sbuf_tensor("P1_qhb", [128, S], BF16))
            ckvb = st.enter_context(nc.sbuf_tensor("P1_ckvb", [128, 2 * 4 * 512], BF16))
            cqb = st.enter_context(nc.sbuf_tensor("P1_cqb", [128, 2 * 8 * 512], BF16))
            tabb = st.enter_context(nc.sbuf_tensor("P1_tab", [128, 2 * 4 * 512], BF16))
            kraw = st.enter_context(nc.sbuf_tensor("P1_kraw", [128, 512], BF16))
            tmp1 = st.enter_context(nc.sbuf_tensor("P1_t1", [128, 512], F32))
            tmp2 = st.enter_context(nc.sbuf_tensor("P1_t2", [128, 512], F32))
        else:
            gate_sb = st.enter_context(nc.sbuf_tensor("P0_gate", [128, 512], F32))
        P = [st.enter_context(nc.psum_tensor(f"P{lay}_P{i}", [128, 512], F32))
             for i in range(5)]
        PT = st.enter_context(nc.psum_tensor(f"P{lay}_PT", [128, 512], BF16))
        PS = st.enter_context(nc.psum_tensor(f"P{lay}_PS", [1, 512], F32))
        MB = st.enter_context(nc.sbuf_tensor(f"P{lay}_mb", [128, 4 * 512], BF16))

        if L1:
            WTAB = [("wukT", 4, 0, "ckvb"), ("wuvT", 4, 4, "ckvb"),
                    ("wuqT", 8, 8, "cqb"), ("wkrT", 16, 16, "xnb")]
        else:
            WTAB = [("wqT", 16, 0, "xnb"), ("wkT", 16, 16, "xnb"),
                    ("wvT", 16, 32, "xnb")]

        def body(R, eng=None):
            C = dict(base)

            def INC(sname, k, ins=None):
                C[sname] += k
                if ins is not None:
                    ins.then_inc(sems[sname].h, k)
                return C[sname]

            def MK(key, sname):
                if R == "sim":
                    marks[key] = C[sname]

            def WT(sname, key_or_val):
                if R == "sim":
                    return
                v = marks[key_or_val] if isinstance(key_or_val, tuple) else key_or_val
                sems[sname].wait(eng, v)

            if R != "sim":
                sems["qc"].wait(eng, base["qc"])
            if R == "sync":
                eng.dma_start(out=MB[:, :], in_=W["masks"]).then_inc(qs.h, 16)
            INC("qs", 16)
            MK(("p_mb",), "qs")

            for h in range(HPC):
                if h > 0:
                    WT("pe", ("p_hd_pe", h - 1))
                for mi, (nm, nkc, tb0, _r) in enumerate(WTAB):
                    for kc in range(nkc):
                        if R == "gpsimd":
                            eng.dma_start(
                                out=wsl[:, (tb0 + kc) * 128:(tb0 + kc + 1) * 128],
                                in_=W[nm][kc * 128:(kc + 1) * 128, h * 128:(h + 1) * 128]
                            ).then_inc(qg.h, 16)
                        INC("qg", 16)
                if lay == 0 and h == 0:
                    if R == "gpsimd":
                        eng.dma_start(out=gwt[:, :], in_=W["gwT"]).then_inc(qg.h, 16)
                    INC("qg", 16)
                MK(("p_wsl", h), "qg")

                for ts in range(16):
                    r, lc = ts // 4, (ts % 4) * 512
                    sp = ts % 2
                    so = sp * (16 * 512)
                    sol = sp * (4 * 512)
                    som = sp * (8 * 512)
                    sot = sp * (4 * 512)
                    ssn = f"wt{sp}"
                    if R == "sync":
                        if h == 0 and ts == 0:
                            WT("cc", cc_idx)
                        gts = h * 16 + ts
                        if gts >= 2:
                            pts = (h, ts - 2) if ts >= 2 else (h - 1, 14 + ts)
                            WT("pe", ("p_pj",) + pts)
                            if L1:
                                WT("dve", ("p_rpq",) + pts)
                        for kc in range(16):
                            eng.dma_start(out=xnb[:, so + kc * 512:so + (kc + 1) * 512],
                                          in_=gatT[r * PR + kc * 128:r * PR + (kc + 1) * 128,
                                                   lc:lc + 512]).then_inc(sems[ssn].h, 16)
                        if L1:
                            for i in range(4):
                                eng.dma_start(out=ckvb[:, sol + i * 512:sol + (i + 1) * 512],
                                              in_=gatT[r * PR + D + i * 128:r * PR + D + (i + 1) * 128,
                                                       lc:lc + 512]).then_inc(sems[ssn].h, 16)
                            for i in range(8):
                                eng.dma_start(out=cqb[:, som + i * 512:som + (i + 1) * 512],
                                              in_=gatT[r * PR + D + DC + i * 128:
                                                       r * PR + D + DC + (i + 1) * 128,
                                                       lc:lc + 512]).then_inc(sems[ssn].h, 16)
                            for ti, tn in enumerate(("cosq", "sinq", "cosk", "sink")):
                                eng.dma_start(out=tabb[:, sot + ti * 512:sot + (ti + 1) * 512],
                                              in_=W[tn][:, ts * 512:(ts + 1) * 512]
                                              ).then_inc(sems[ssn].h, 16)
                    INC(ssn, 16 * (16 + (16 if L1 else 0)))
                    MK(("p_ld", h, ts), ssn)

                    if R == "tensor":
                        WT(ssn, ("p_ld", h, ts))
                        if h == 0 and ts == 0:
                            WT("qg", ("p_wsl", 0))
                            WT("act", base["act"])
                            WT("dve", base["dve"] if dve_guard is None else dve_guard)
                        elif ts == 0:
                            WT("qg", ("p_wsl", h))
                            WT("act", ("p_hd_act", h - 1))
                            WT("dve", ("p_hd_dve", h - 1))
                        if not (h == 0 and ts == 0):
                            WT("act", ("p_pjev", h, ts - 1) if ts > 0 else ("p_pjev", h - 1, 15))
                        if L1:
                            plan = [(0, 0, 4, "ckvb"), (2, 4, 4, "ckvb"),
                                    (3, 8, 8, "cqb"), (1, 16, 16, "xnb")]
                        else:
                            plan = [(0, 0, 16, "xnb"), (1, 16, 16, "xnb"), (2, 32, 16, "xnb")]
                        rhs_map = {"xnb": (xnb, so), "ckvb": ((ckvb if L1 else None), sol),
                                   "cqb": ((cqb if L1 else None), som)}
                        for pi, tb0, nkc, rn in plan:
                            rbuf, roff = rhs_map[rn]
                            for kc in range(nkc):
                                ins = eng.matmul(P[pi][:, :],
                                                 wsl[:, (tb0 + kc) * 128:(tb0 + kc + 1) * 128],
                                                 rbuf[:, roff + kc * 512:roff + (kc + 1) * 512],
                                                 start=(kc == 0), stop=(kc == nkc - 1))
                            ins.then_inc(pe.h, 1)
                        INC("pe", len(plan))
                    else:
                        INC("pe", 4 if L1 else 3)
                    MK(("p_pj", h, ts), "pe")

                    if R == "scalar":
                        WT("pe", ("p_pj", h, ts))
                        if L1 and not (h == 0 and ts == 0):
                            WT("dve", ("p_rpk", h, ts - 1) if ts > 0 else ("p_rpk", h - 1, 15))
                        if L1:
                            eng.activation(kh_a[:, ts * 512:(ts + 1) * 512], P[0][:, :],
                                           AF.Identity, bias=pcol[:, 18 + h:19 + h],
                                           scale=1.0 / 16.0).then_inc(act.h, 1)
                            eng.activation(vtmp[:, :], P[2][:, :], AF.Identity,
                                           bias=pcol[:, 22 + h:23 + h]).then_inc(act.h, 1)
                            eng.activation(qh_a[:, ts * 512:(ts + 1) * 512], P[3][:, :],
                                           AF.Identity, bias=pcol[:, 26 + h:27 + h]
                                           ).then_inc(act.h, 1)
                            eng.activation(kraw[:, :], P[1][:, :], AF.Identity,
                                           bias=pcol[:, 30 + h:31 + h]).then_inc(act.h, 1)
                        else:
                            eng.activation(qh_a[:, ts * 512:(ts + 1) * 512], P[0][:, :],
                                           AF.Copy).then_inc(act.h, 1)
                            eng.activation(kh_a[:, ts * 512:(ts + 1) * 512], P[1][:, :],
                                           AF.Copy, scale=S0).then_inc(act.h, 1)
                            eng.activation(vtmp[:, :], P[2][:, :], AF.Copy).then_inc(act.h, 1)
                    INC("act", 4 if L1 else 3)
                    MK(("p_pjev", h, ts), "act")

                    if R == "tensor":
                        WT("act", ("p_pjev", h, ts))
                        if not (h == 0 and ts == 0):
                            WT("act", ("p_vtev", h, ts - 1) if ts > 0 else ("p_vtev", h - 1, 15))
                        for j in range(4):
                            ins = eng.transpose(PT[:, j * 128:(j + 1) * 128],
                                                vtmp[:, j * 128:(j + 1) * 128],
                                                ident16[:, :])
                        ins.then_inc(pe.h, 1)
                    INC("pe", 1)
                    MK(("p_vt", h, ts), "pe")
                    if R == "scalar":
                        WT("pe", ("p_vt", h, ts))
                        eng.activation(vT_h[:, ts * 512:(ts + 1) * 512], PT[:, :],
                                       AF.Copy).then_inc(act.h, 1)
                    INC("act", 1)
                    MK(("p_vtev", h, ts), "act")

                    if L1:
                        if R == "tensor":
                            WT("act", ("p_pjev", h, ts))
                            if not (h == 0 and ts == 0):
                                WT("dve", ("p_rpq", h, ts - 1) if ts > 0 else ("p_rpq", h - 1, 15))
                            eng.matmul(P[4][:, :], prope_s[:, :], kraw[:, :],
                                       start=True, stop=True).then_inc(pe.h, 1)
                        INC("pe", 1)
                        MK(("p_rotk", h, ts), "pe")
                        if R == "vector":
                            WT("act", ("p_pjev", h, ts))
                            WT(ssn, ("p_ld", h, ts))
                            eng.tensor_mul(tmp1[:, :], kraw[:, :], tabb[:, sot + 2 * 512:sot + 3 * 512])
                            WT("pe", ("p_rotk", h, ts))
                            eng.tensor_mul(tmp2[:, :], P[4][:, :], tabb[:, sot + 3 * 512:sot + 4 * 512])
                            eng.tensor_add(kh_b[:, ts * 512:(ts + 1) * 512], tmp1[:, :],
                                           tmp2[:, :]).then_inc(dve.h, 1)
                        INC("dve", 1)
                        MK(("p_rpk", h, ts), "dve")
                        if R == "tensor":
                            WT("dve", ("p_rpk", h, ts))
                            eng.matmul(P[4][:, :], prope_s[:, :],
                                       qh_a[:, ts * 512:(ts + 1) * 512],
                                       start=True, stop=True).then_inc(pe.h, 1)
                        INC("pe", 1)
                        MK(("p_rotq", h, ts), "pe")
                        if R == "vector":
                            eng.tensor_mul(tmp1[:, :], qh_a[:, ts * 512:(ts + 1) * 512],
                                           tabb[:, sot:sot + 512])
                            WT("pe", ("p_rotq", h, ts))
                            eng.tensor_mul(tmp2[:, :], P[4][:, :], tabb[:, sot + 512:sot + 1024])
                            eng.tensor_add(qh_b[:, ts * 512:(ts + 1) * 512], tmp1[:, :],
                                           tmp2[:, :]).then_inc(dve.h, 1)
                        INC("dve", 1)
                        MK(("p_rpq", h, ts), "dve")

                kidx = 0
                for qt in range(16):
                    nkt = 4 * (qt + 1)

                    def emit_scores(kt):
                        ki = kidx + kt
                        if R == "tensor":
                            if ki >= 2:
                                WT("act", ("a_ex", h, ki - 2))
                            if qt == 0 and kt == 0:
                                WT("act", ("p_vtev", h, 15))
                                if L1:
                                    WT("dve", ("p_rpq", h, 15))
                            if L1:
                                eng.matmul(P[ki % 2][:, :],
                                           kh_a[:, kt * 128:(kt + 1) * 128],
                                           qh_a[:, qt * 512:(qt + 1) * 512],
                                           start=True, stop=False)
                                eng.matmul(P[ki % 2][:, :],
                                           kh_b[:, kt * 128:(kt + 1) * 128],
                                           qh_b[:, qt * 512:(qt + 1) * 512],
                                           start=False, stop=True).then_inc(pe.h, 1)
                            else:
                                eng.matmul(P[ki % 2][:, :],
                                           kh_a[:, kt * 128:(kt + 1) * 128],
                                           qh_a[:, qt * 512:(qt + 1) * 512],
                                           start=True, stop=True).then_inc(pe.h, 1)
                        INC("pe", 1)
                        MK(("a_sc", h, ki), "pe")

                    def emit_exp(kt):
                        ki = kidx + kt
                        diag = kt >= 4 * qt
                        j = kt - 4 * qt
                        if R == "scalar":
                            WT("pe", ("a_sc", h, ki))
                            if ki >= 2:
                                WT("pe", ("a_pv", h, ki - 2))
                            if diag:
                                prev = marks.get(("a_dprev", h, ki))
                                if prev is not None:
                                    WT("dve", prev)
                                eng.activation(ptmp[:, :], P[ki % 2][:, :],
                                               AF.Exp).then_inc(act.h, 1)
                            else:
                                eng.activation(probsT[:, (ki % 2) * 512:(ki % 2 + 1) * 512],
                                               P[ki % 2][:, :], AF.Exp).then_inc(act.h, 1)
                        INC("act", 1)
                        MK(("a_ex", h, ki), "act")
                        if diag:
                            if R == "vector":
                                WT("act", ("a_ex", h, ki))
                                eng.tensor_mul(probsT[:, (ki % 2) * 512:(ki % 2 + 1) * 512],
                                               ptmp[:, :],
                                               MB[:, j * 512:(j + 1) * 512]).then_inc(dve.h, 1)
                            INC("dve", 1)
                            MK(("a_mk", h, ki), "dve")
                            if R == "sim":
                                marks[("a_dprev", h, ki + 1 if kt < nkt - 1 else ki + 4 * qt + 5)] = C["dve"]

                    def emit_av(kt):
                        ki = kidx + kt
                        diag = kt >= 4 * qt
                        if R == "tensor":
                            if diag:
                                WT("dve", ("a_mk", h, ki))
                            else:
                                WT("act", ("a_ex", h, ki))
                            if kt == 0:
                                if (h, qt) != (0, 0):
                                    WT("dve", ("a_avm", h, qt - 1) if qt > 0 else ("a_avm", h - 1, 15))
                                    WT("dve", ("a_rc", h, qt - 1) if qt > 0 else ("a_rc", h - 1, 15))
                            eng.matmul(P[2][:, :], vT_h[:, kt * 128:(kt + 1) * 128],
                                       probsT[:, (ki % 2) * 512:(ki % 2 + 1) * 512],
                                       start=(kt == 0), stop=(kt == nkt - 1))
                            ins = eng.matmul(PS[:, :], ones_c16[:, :],
                                             probsT[:, (ki % 2) * 512:(ki % 2 + 1) * 512],
                                             start=(kt == 0), stop=(kt == nkt - 1))
                            ins.then_inc(pe.h, 1)
                        INC("pe", 1)
                        MK(("a_pv", h, ki), "pe")

                    for kt in range(nkt):
                        emit_scores(kt)
                        emit_exp(kt)
                        if kt > 0:
                            emit_av(kt - 1)
                    emit_av(nkt - 1)
                    kidx += nkt
                    if R == "vector":
                        WT("pe", ("a_pv", h, kidx - 1))
                        eng.reciprocal(rs_sb[:, :], PS[:, :]).then_inc(dve.h, 1)
                    INC("dve", 1)
                    MK(("a_rc", h, qt), "dve")
                    if R == "tensor":
                        WT("dve", ("a_rc", h, qt))
                        if (h, qt) != (0, 0):
                            WT("act", ("a_rbev", h, qt - 1) if qt > 0 else ("a_rbev", h - 1, 15))
                        eng.matmul(P[3][:, :], ones_r[:, :], rs_sb[:, :],
                                   start=True, stop=True).then_inc(pe.h, 1)
                        if lay == 0:
                            if (h, qt) != (0, 0):
                                WT("act", ("a_ggev", h, qt - 1) if qt > 0 else ("a_ggev", h - 1, 15))
                            eng.matmul(P[4][:, :], gwt[:, :],
                                       qh_a[:, qt * 512:(qt + 1) * 512],
                                       start=True, stop=True).then_inc(pe.h, 1)
                    INC("pe", 2 if lay == 0 else 1)
                    MK(("a_rb", h, qt), "pe")
                    if R == "scalar":
                        WT("pe", ("a_rb", h, qt))
                        eng.activation(rb_sb[:, :], P[3][:, :], AF.Copy).then_inc(act.h, 1)
                    INC("act", 1)
                    MK(("a_rbev", h, qt), "act")
                    if lay == 0:
                        if R == "scalar":
                            eng.activation(gate_sb[:, :], P[4][:, :], AF.Sigmoid,
                                           bias=pcol[:, 5:6]).then_inc(act.h, 1)
                        INC("act", 1)
                        MK(("a_ggev", h, qt), "act")
                    if R == "vector":
                        WT("act", ("a_rbev", h, qt))
                        if lay == 0:
                            WT("act", ("a_ggev", h, qt))
                            eng.tensor_mul(fac[:, :], gate_sb[:, :], rb_sb[:, :])
                            srcf = fac
                        else:
                            srcf = rb_sb
                        if not (h == 0 and qt < 2):
                            WT("qc", ("a_st", h, qt - 2) if qt >= 2 else ("a_st", h - 1, 14 + qt))
                        eng.tensor_mul(av_sb[:, (qt % 2) * 512:(qt % 2 + 1) * 512],
                                       P[2][:, :], srcf[:, :]).then_inc(dve.h, 1)
                    INC("dve", 1)
                    MK(("a_avm", h, qt), "dve")
                    if R == "scalar":
                        WT("dve", ("a_avm", h, qt))
                        eng.dma_start(out=pubaT[h * 128:(h + 1) * 128,
                                                qt * 512:(qt + 1) * 512],
                                      in_=av_sb[:, (qt % 2) * 512:(qt % 2 + 1) * 512]
                                      ).then_inc(qc.h, 16)
                    INC("qc", 16)
                    MK(("a_st", h, qt), "qc")
                MK(("p_hd_pe", h), "pe")
                MK(("p_hd_act", h), "act")
                MK(("p_hd_dve", h), "dve")

            if R == "gpsimd":
                WT("qc", ("a_st", HPC - 1, 15))
                if ATTN_CC:
                    eng.collective_compute(
                        "AllGather", ALU.bypass, replica_groups=RGROUPS,
                        ins=[pubaT.opt()], outs=[gataT.opt()]).then_inc(cc.h, 1)
                else:
                    eng.engine_nop().then_inc(cc.h, 1)
            INC("cc", 1)
            if R == "sim":
                for k in sems:
                    marks[("p_end", k)] = C[k]

        body("sim")
        with nc.Block() as blk:
            @blk.sync
            def _(sy):
                body("sync", sy)

            @blk.gpsimd
            def _(g):
                body("gpsimd", g)

            @blk.tensor
            def _(te):
                body("tensor", te)

            @blk.scalar
            def _(sc):
                body("scalar", sc)

            @blk.vector
            def _(v):
                body("vector", v)

        for k in sems:
            sems[k].n = marks[("p_end", k)]


def _phase_oproj(nc, gataT, srcT, dstT, woTap, sems, cc_idx, tag=0):
    import contextlib
    qs, qc, qg = sems["qs"], sems["qc"], sems["qg"]
    pe, act, dve = sems["pe"], sems["act"], sems["dve"]
    base = {k: s.n for k, s in sems.items()}
    marks = {}

    with contextlib.ExitStack() as st:
        woT_sb = st.enter_context(nc.sbuf_tensor(f"O_wo{tag}", [128, 16 * 2048], BF16))
        avb = st.enter_context(nc.sbuf_tensor(f"O_avb{tag}", [128, 16 * 1024], BF16))
        x1b = st.enter_context(nc.sbuf_tensor(f"O_x1b{tag}", [128, 2 * 2048], F32))
        outb = st.enter_context(nc.sbuf_tensor(f"O_outb{tag}", [128, 2 * 2048], F32))
        po = [st.enter_context(nc.psum_tensor(f"O_po{tag}_{i}", [128, 512], F32))
              for i in range(4)]

        def body(R, eng=None):
            C = dict(base)

            def INC(sname, k, ins=None):
                C[sname] += k
                if ins is not None:
                    ins.then_inc(sems[sname].h, k)
                return C[sname]

            def MK(key, sname):
                if R == "sim":
                    marks[key] = C[sname]

            def WT(sname, key_or_val):
                if R == "sim":
                    return
                v = marks[key_or_val] if isinstance(key_or_val, tuple) else key_or_val
                sems[sname].wait(eng, v)

            if R != "sim":
                sems["qc"].wait(eng, base["qc"])
            if R == "gpsimd":
                for kc in range(16):
                    eng.dma_start(out=woT_sb[:, kc * 2048:(kc + 1) * 2048],
                                  in_=woTap[kc * 128:(kc + 1) * 128, :]).then_inc(qg.h, 16)
            INC("qg", 16 * 16)
            MK(("o_wo",), "qg")

            if R == "gpsimd":
                own0 = (nc.partition_id() % 4) * 2048
                offs = [own0, own0 + 1024]
                gv = gataT.rearrange("(c p) t -> p c t", p=128)
                av = avb[:, :].rearrange("p (c t) -> p c t", t=1024)

            for tg in range(2):
                if R == "gpsimd":
                    if tg == 0:
                        WT("cc", cc_idx)
                    else:
                        WT("pe", ("o_pj", 7, 3))
                    for grp in range(2):
                        eng.dma_start(out=av[:, grp * 8:(grp + 1) * 8, :],
                                      in_=gv[:, grp * 8:(grp + 1) * 8,
                                             bass.ds(offs[tg], 1024)]
                                      ).then_inc(qg.h, 16)
                INC("qg", 16 * 2)
                MK(("o_avb", tg), "qg")
                for tt8 in range(8):
                    tt = tg * 8 + tt8
                    sxn = "wt" + str(tt % 2)
                    if R == "sync":
                        if tt == 0:
                            WT("qc", base["qc"])
                        if tt >= 2:
                            WT("dve", ("o_ev", tt - 2, 3))
                        eng.dma_start(out=x1b[:, (tt % 2) * 2048:(tt % 2 + 1) * 2048],
                                      in_=srcT[tt * 128:(tt + 1) * 128, :]).then_inc(sems[sxn].h, 16)
                    INC(sxn, 16)
                    MK(("o_x1", tt), sxn)
                    for ds in range(4):
                        if R == "tensor":
                            if tt == 0 and ds == 0 and tg == 0:
                                WT("qg", ("o_wo",))
                                WT("act", base["act"])
                                WT("dve", base["dve"])
                            WT("qg", ("o_avb", tg))
                            if tt8 > 0 or tg > 0:
                                pt = tt - 1
                                WT("dve", ("o_ev", pt, ds))
                            for kc in range(16):
                                ins = eng.matmul(po[ds][:, :],
                                                 avb[:, kc * 1024 + tt8 * 128:kc * 1024 + (tt8 + 1) * 128],
                                                 woT_sb[:, kc * 2048 + ds * 512:kc * 2048 + (ds + 1) * 512],
                                                 start=(kc == 0), stop=(kc == 15))
                            ins.then_inc(pe.h, 1)
                        INC("pe", 1)
                        MK(("o_pj", tt, ds), "pe")
                        if R == "vector":
                            WT("pe", ("o_pj", tt, ds))
                            if ds == 0:
                                WT("wt" + str(tt % 2), ("o_x1", tt))
                                if tt >= 1:
                                    WT("qc", ("o_st", tt - 1))
                            eng.tensor_add(outb[:, (tt % 2) * 2048 + ds * 512:
                                                (tt % 2) * 2048 + (ds + 1) * 512],
                                           po[ds][:, :],
                                           x1b[:, (tt % 2) * 2048 + ds * 512:
                                               (tt % 2) * 2048 + (ds + 1) * 512]
                                           ).then_inc(dve.h, 1)
                        INC("dve", 1)
                        MK(("o_ev", tt, ds), "dve")
                    if R == "scalar":
                        WT("dve", ("o_ev", tt, 3))
                        eng.dma_start(out=dstT[tt * 128:(tt + 1) * 128, :],
                                      in_=outb[:, (tt % 2) * 2048:(tt % 2 + 1) * 2048]
                                      ).then_inc(qc.h, 16)
                    INC("qc", 16)
                    MK(("o_st", tt), "qc")

            if R == "sim":
                for k in sems:
                    marks[("o_end", k)] = C[k]

        body("sim")
        with nc.Block() as blk:
            @blk.sync
            def _(sy):
                body("sync", sy)

            @blk.gpsimd
            def _(g):
                body("gpsimd", g)

            @blk.tensor
            def _(te):
                body("tensor", te)

            @blk.scalar
            def _(sc):
                body("scalar", sc)

            @blk.vector
            def _(v):
                body("vector", v)

        for k in sems:
            sems[k].n = marks[("o_end", k)]


def _phase_ffn1(nc, src, dst, pubT, qfcTap, qpjTap, W, sems, pcol, bo_bc):
    import contextlib
    qs, qc, qg = sems["qs"], sems["qc"], sems["qg"]
    pe, act, dve = sems["pe"], sems["act"], sems["dve"]
    base = {k: s.n for k, s in sems.items()}
    marks = {}

    with contextlib.ExitStack() as st:
        xnh = st.enter_context(nc.sbuf_tensor("F_xnh", [128, 16 * 1024], BF16))
        oac = st.enter_context(nc.sbuf_tensor("F_oac", [128, 8 * 2048], F32))
        rtmp = st.enter_context(nc.sbuf_tensor("F_rt", [128, 2 * 512], F32))
        htmp = st.enter_context(nc.sbuf_tensor("F_ht", [128, 2 * 512], F32))
        ytmp = st.enter_context(nc.sbuf_tensor("F_yt", [128, 2 * 512], F32))
        wt = st.enter_context(nc.sbuf_tensor("F_wt", [128, 4 * 512], BF16))
        h_fm = st.enter_context(nc.sbuf_tensor("F_hfm", [128, 20 * 1024], BF16))
        ident32 = st.enter_context(nc.sbuf_tensor("F_id32", [128, 128], F32))
        ps = [st.enter_context(nc.psum_tensor(f"F_ps{i}", [128, 512], F32))
              for i in range(8)]

        def body(R, eng=None):
            C = dict(base)

            def INC(sname, k, ins=None):
                C[sname] += k
                if ins is not None:
                    ins.then_inc(sems[sname].h, k)
                return C[sname]

            def MK(key, sname):
                if R == "sim":
                    marks[key] = C[sname]

            def WT(sname, key_or_val):
                if R == "sim":
                    return
                v = marks[key_or_val] if isinstance(key_or_val, tuple) else key_or_val
                sems[sname].wait(eng, v)

            if R != "sim":
                sems["qc"].wait(eng, base["qc"])
            if R == "sync":
                eng.dma_start(out=ident32[:, :], in_=W["ident32"]).then_inc(qs.h, 16)
            INC("qs", 16)

            for hf in range(2):
                if hf == 0:
                    if R == "sync":
                        WT("qc", base["qc"])
                else:
                    if R != "sim":
                        WT("qc", ("f_st", 0))
                    if R == "sync":
                        WT("pe", ("f_pe_end", 0))
                for tt in range(8):
                    if R == "sync":
                        eng.dma_start(out=oac[:, tt * 2048:(tt + 1) * 2048],
                                      in_=src[hf * 1024 + tt * 128:hf * 1024 + (tt + 1) * 128, :]
                                      ).then_inc(qs.h, 16)
                    INC("qs", 16)
                MK(("f_oacld", hf), "qs")
                for tt in range(8):
                    if R == "vector":
                        WT("qs", ("f_ld", hf))
                        eng.tensor_add(oac[:, tt * 2048:(tt + 1) * 2048],
                                       oac[:, tt * 2048:(tt + 1) * 2048],
                                       bo_bc[:, :]).then_inc(dve.h, 1)
                    INC("dve", 1)
                MK(("f_bo", hf), "dve")
                for kc in range(16):
                    if R == "sync":
                        eng.dma_start(out=xnh[:, kc * 1024:(kc + 1) * 1024],
                                      in_=pubT[kc * 128:(kc + 1) * 128,
                                               hf * 1024:(hf + 1) * 1024]).then_inc(qs.h, 16)
                    INC("qs", 16)
                MK(("f_ld", hf), "qs")

                for fh in range(2):
                    for si in range(5):
                        for kc in range(16):
                            wti = C["wti"] = C.get("wti", 0) + 1
                            slot = (wti - 1) % 4
                            sn = f"wt{slot}"
                            if R == "sync":
                                if wti > 4:
                                    WT("pe", ("f_wfree", wti - 4))
                                eng.dma_start(out=wt[:, slot * 512:(slot + 1) * 512],
                                              in_=qfcTap[kc * 128:(kc + 1) * 128,
                                                         fh * 2560 + si * 512:
                                                         fh * 2560 + (si + 1) * 512]
                                              ).then_inc(sems[sn].h, 16)
                            INC(sn, 16)
                            MK(("f_wld", wti), sn)
                            if R == "tensor":
                                WT(sn, ("f_wld", wti))
                                if kc == 0:
                                    WT("qs", ("f_ld", hf))
                                    if hf == 0 and fh == 0 and si == 0:
                                        WT("act", base["act"])
                                        WT("dve", base["dve"])
                                    elif si == 0 and fh == 0:
                                        WT("dve", ("f_yev", hf - 1, 1, 3))
                                    elif si == 0:
                                        WT("dve", ("f_yev", hf, fh - 1, 3))
                                    else:
                                        WT("act", ("f_hev", hf, fh, si - 1, 7))
                                for tt in range(8):
                                    ins = eng.matmul(
                                        ps[tt][:, :],
                                        xnh[:, kc * 1024 + tt * 128:kc * 1024 + (tt + 1) * 128],
                                        wt[:, slot * 512:(slot + 1) * 512],
                                        start=(kc == 0), stop=(kc == 15))
                                    if tt == 7:
                                        ins.then_inc(pe.h, 1)
                            INC("pe", 1)
                            MK(("f_wfree", wti), "pe")
                            if kc == 15:
                                MK(("f_gstop", hf, fh, si), "pe")
                        for tt in range(8):
                            if R == "scalar":
                                WT("pe", ("f_gstop", hf, fh, si))
                                if tt >= 2 or si > 0 or fh > 0 or hf > 0:
                                    pk = ("f_tp", hf, fh, si, tt - 2)
                                    if tt < 2:
                                        pk = None
                                    if pk is not None and pk in marks:
                                        WT("pe", pk)
                                eng.activation(rtmp[:, (tt % 2) * 512:(tt % 2 + 1) * 512],
                                               ps[tt][:, :], AF.Relu).then_inc(act.h, 1)
                                eng.activation(htmp[:, (tt % 2) * 512:(tt % 2 + 1) * 512],
                                               rtmp[:, (tt % 2) * 512:(tt % 2 + 1) * 512],
                                               AF.Square).then_inc(act.h, 1)
                            INC("act", 2)
                            MK(("f_sq", hf, fh, si, tt), "act")
                            if R == "tensor":
                                WT("act", ("f_sq", hf, fh, si, tt))
                                for j in range(4):
                                    ins = eng.transpose(
                                        ps[tt][:, j * 128:(j + 1) * 128],
                                        htmp[:, (tt % 2) * 512 + j * 128:(tt % 2) * 512 + (j + 1) * 128],
                                        ident32[:, :])
                                ins.then_inc(pe.h, 1)
                            INC("pe", 1)
                            MK(("f_tp", hf, fh, si, tt), "pe")
                            if R == "scalar":
                                WT("pe", ("f_tp", hf, fh, si, tt))
                                if hf > 0 or fh > 0:
                                    WT("pe", ("f_ygstop", hf if fh > 0 else hf - 1,
                                              0 if fh > 0 else 1, 3))
                                dst_ap = bass.AP(h_fm, si * 4 * 1024 + tt * 128,
                                                 [[20 * 1024, 128], [1024, 4], [1, 128]])
                                eng.activation(dst_ap,
                                               ps[tt][:, :].rearrange("p (s t) -> p s t", t=128),
                                               AF.Copy).then_inc(act.h, 1)
                            INC("act", 1)
                            MK(("f_hev", hf, fh, si, tt), "act")

                    for ds in range(4):
                        for ic in range(20):
                            wti = C["wti"] = C.get("wti", 0) + 1
                            slot = (wti - 1) % 4
                            sn = f"wt{slot}"
                            if R == "sync":
                                WT("pe", ("f_wfree", wti - 4))
                                eng.dma_start(out=wt[:, slot * 512:(slot + 1) * 512],
                                              in_=qpjTap[fh * 2560 + ic * 128:
                                                         fh * 2560 + (ic + 1) * 128,
                                                         ds * 512:(ds + 1) * 512]
                                              ).then_inc(sems[sn].h, 16)
                            INC(sn, 16)
                            MK(("f_wld", wti), sn)
                            if R == "tensor":
                                WT(sn, ("f_wld", wti))
                                if ic == 0:
                                    WT("act", ("f_hev", hf, fh, 4, 7))
                                    if ds > 0:
                                        WT("dve", ("f_yev", hf, fh, ds - 1))
                                for tt in range(8):
                                    ins = eng.matmul(
                                        ps[tt][:, :],
                                        h_fm[:, ic * 1024 + tt * 128:ic * 1024 + (tt + 1) * 128],
                                        wt[:, slot * 512:(slot + 1) * 512],
                                        start=(ic == 0), stop=(ic == 19))
                                    if tt == 7:
                                        ins.then_inc(pe.h, 1)
                            INC("pe", 1)
                            MK(("f_wfree", wti), "pe")
                            if ic == 19:
                                MK(("f_ygstop", hf, fh, ds), "pe")
                        for tt in range(8):
                            if R == "vector":
                                WT("pe", ("f_ygstop", hf, fh, ds))
                                eng.tensor_scalar(ytmp[:, (tt % 2) * 512:(tt % 2 + 1) * 512],
                                                  ps[tt][:, :], pcol[:, 4:5], None,
                                                  ALU.mult)
                                eng.tensor_add(
                                    oac[:, tt * 2048 + ds * 512:tt * 2048 + (ds + 1) * 512],
                                    ytmp[:, (tt % 2) * 512:(tt % 2 + 1) * 512],
                                    oac[:, tt * 2048 + ds * 512:tt * 2048 + (ds + 1) * 512]
                                    ).then_inc(dve.h, 1)
                            INC("dve", 1)
                            if tt in (3, 7):
                                MK(("f_yev", hf, fh, ds, tt), "dve")
                        MK(("f_yev", hf, fh, ds), "dve")
                MK(("f_pe_end", hf), "pe")
                if R == "vector":
                    eng.memset(ytmp[:, 0:4], 0.0).then_inc(dve.h, 1)
                    eng.memset(ytmp[:, 4:8], 0.0).then_inc(dve.h, 1)
                INC("dve", 2)
                MK(("f_flush", hf), "dve")

                for tt in range(8):
                    if R == "scalar":
                        WT("dve", ("f_flush", hf))
                        eng.dma_start(out=dst[hf * 1024 + tt * 128:hf * 1024 + (tt + 1) * 128, :],
                                      in_=oac[:, tt * 2048:(tt + 1) * 2048]).then_inc(qc.h, 16)
                    INC("qc", 16)
                MK(("f_st", hf), "qc")

            if R == "sim":
                for k in sems:
                    marks[("f_end", k)] = C[k]

        body("sim")
        with nc.Block() as blk:
            @blk.sync
            def _(sy):
                body("sync", sy)

            @blk.gpsimd
            def _(g):
                body("gpsimd", g)

            @blk.tensor
            def _(te):
                body("tensor", te)

            @blk.scalar
            def _(sc):
                body("scalar", sc)

            @blk.vector
            def _(v):
                body("vector", v)

        for k in sems:
            sems[k].n = marks[("f_end", k)]


_NC_CACHE = {}


def _get_program(layers, taps=()):
    key = (tuple(layers), tuple(taps))
    if key not in _NC_CACHE:
        _NC_CACHE[key] = build(layers, taps)
    return _NC_CACHE[key]


def _assemble(outs):
    x = np.empty((B, S, D), np.float32)
    for c in range(NCORES):
        b, p = c // 4, c % 4
        x[b, p * TOK:(p + 1) * TOK] = outs[c]
    return x


FUSED = True


def kernel(**inputs):
    ins = {k: np.asarray(v) for k, v in inputs.items()}
    if FUSED:
        maps = host_prep(ins, layers=(0, 1))
        nc = _get_program((0, 1))
        res = run_bass_kernel_spmd(nc, maps, core_ids=list(range(NCORES)))
        return _assemble([res.results[c]["out"] for c in range(NCORES)])
    maps0 = host_prep(ins, layers=(0,))
    res0 = run_bass_kernel_spmd(_get_program((0,)), maps0,
                                core_ids=list(range(NCORES)))
    x1 = _assemble([res0.results[c]["out"] for c in range(NCORES)])
    ins2 = dict(ins)
    ins2["x1"] = x1
    maps1 = host_prep(ins2, layers=(1,))
    res1 = run_bass_kernel_spmd(_get_program((1,)), maps1,
                                core_ids=list(range(NCORES)))
    return _assemble([res1.results[c]["out"] for c in range(NCORES)])
